# version 17
# baseline (speedup 1.0000x reference)
import numpy as np
import ml_dtypes
import concourse.bass as bass
import concourse.mybir as mybir
from concourse.bass_utils import run_bass_kernel_spmd

F32 = mybir.dt.float32
BF16 = mybir.dt.bfloat16
U32 = mybir.dt.uint32
AF = mybir.ActivationFunctionType
ALU = mybir.AluOpType
AX = mybir.AxisListType

N_CORES = 8
NTOK = 4096
D = 1024
P = 128
NT = NTOK // P
HEADS = 8
DH = 64
HID = 512
SCALE = DH ** -0.5
NE = 16
NSH = 2
NMAT = NE + NSH
NSTEP = 1 + NMAT
NQ = 4
QT = NT // NQ
EPS = 1e-5
DEBUG = False
PHASES = 'ALL'


class _Dummy:
    def __getattr__(self, k):
        return lambda *a, **kw: self

    def __call__(self, *a, **kw):
        return self


class Em:

    def __init__(self):
        self.vals = {}
        self.cnt = {}
        self.mode = "count"
        self.sem = {}

    def _bump(self, chan, label, k):
        c = self.cnt.get(chan, 0) + k
        self.cnt[chan] = c
        d = self.vals.setdefault(chan, {})
        if self.mode == "count":
            assert label not in d, f"dup label {chan}/{label}"
            d[label] = c
        else:
            assert d.get(label) == c, f"label drift {chan}/{label}: {d.get(label)} != {c}"
        return c

    def ms(self, eng, chan, label):
        self._bump(chan, label, 1)
        if self.mode == "emit":
            eng.drain().then_inc(self.sem[chan], 1)

    def pe_ms(self, instr, chan, label):
        self._bump(chan, label, 1)
        if self.mode == "emit":
            instr.then_inc(self.sem[chan], 1)

    def op_ms(self, instr, chan, label):
        self._bump(chan, label, 1)
        if self.mode == "emit":
            instr.then_inc(self.sem[chan], 1)

    def dma(self, eng, chan, label, out, in_, **kw):
        self._bump(chan, label, 16)
        if self.mode == "emit":
            eng.dma_start(out=out, in_=in_, **kw).then_inc(self.sem[chan], 16)

    def wait(self, eng, chan, label):
        if self.mode == "emit":
            eng.wait_ge(self.sem[chan], self.vals[chan][label])


def build_nc():
    nc = bass.Bass("TRN2", target_bir_lowering=False, debug=False)

    xin = nc.dram_tensor("xin", [NTOK, D], F32, kind="ExternalInput").ap()
    wq = nc.dram_tensor("wq", [D, HID], F32, kind="ExternalInput").ap()
    wk = nc.dram_tensor("wk", [D, HID], F32, kind="ExternalInput").ap()
    wv = nc.dram_tensor("wv", [D, HID], F32, kind="ExternalInput").ap()
    wo = nc.dram_tensor("wo", [HID, D], F32, kind="ExternalInput").ap()
    bo = nc.dram_tensor("bo", [1, D], F32, kind="ExternalInput").ap()
    cen = nc.dram_tensor("cen", [NE, D], F32, kind="ExternalInput").ap()
    ebi = nc.dram_tensor("ebi", [1, NE], F32, kind="ExternalInput").ap()
    wr = nc.dram_tensor("wr", [NE, D, D], F32, kind="ExternalInput").ap()
    brr = nc.dram_tensor("brr", [NE, D], F32, kind="ExternalInput").ap()
    ws = nc.dram_tensor("ws", [NSH, D, D], F32, kind="ExternalInput").ap()
    bss = nc.dram_tensor("bss", [NSH, D], F32, kind="ExternalInput").ap()
    eye_in = nc.dram_tensor("eye_in", [P, P], BF16, kind="ExternalInput").ap()
    out = nc.dram_tensor("out", [NTOK, D], F32, kind="ExternalOutput").ap()
    wbf = nc.dram_tensor("wbf", [NMAT, D, D], BF16, kind="Internal").ap()
    yscr = nc.dram_tensor("yscr", [NTOK, D], F32, kind="Internal").ap()
    if DEBUG:
        dbg_E = nc.dram_tensor("dbg_E", [P, 4 * 65], F32, kind="ExternalOutput").ap()
        dbg_ctx = nc.dram_tensor("dbg_ctx", [P, 4 * DH], F32, kind="ExternalOutput").ap()
        dbg_g = nc.dram_tensor("dbg_g", [P, NT * NE], F32, kind="ExternalOutput").ap()
        dbg_y = nc.dram_tensor("dbg_y", [NTOK, D], F32, kind="ExternalOutput").ap()
        dbg_uT = nc.dram_tensor("dbg_uT", [P, 8 * NTOK], F32, kind="ExternalOutput").ap()
        dbg_racc = nc.dram_tensor("dbg_racc", [P, QT * D], F32, kind="ExternalOutput").ap()
        dbg_gTx = nc.dram_tensor("dbg_gTx", [NMAT, NTOK], F32, kind="ExternalOutput").ap()

    qscr = nc.dram_tensor("qscr", [NTOK, HID], BF16, kind="Internal").ap()

    A = nc.alloc_sbuf_tensor
    eye = A("eye", [P, P], BF16)
    onesf = A("onesf", [4, P], F32)
    boRep = A("boRep", [P, D], F32)
    ebRep = A("ebRep", [P, NE], F32)
    ct_bf = A("ct_bf", [P, P], BF16)
    brx = A("brx", [NMAT, D], BF16)
    gTx = A("gTx", [NMAT, NTOK], BF16)
    uT = A("uT", [P, 8 * NTOK], BF16)
    gAll = A("gAll", [P, NT * NE], F32)
    halfc = A("halfc", [P, 1], F32)
    neg3e4 = A("neg3e4", [P, NE], F32)
    zero16 = A("zero16", [P, NE], F32)
    gb = A("gb", [P, NE], BF16)
    epsc = A("epsc", [P, 1], F32)
    eyeblk = A("eyeblk", [P, 4 * HID], BF16)
    E_sb = A("E_sb", [P, 4 * 65], F32)
    sInv = A("sInv", [P, 4], F32)
    ctx = A("ctx", [P, 4 * DH], BF16)

    def stat4(nm):
        return ([A(f"{nm}bn{i}", [P, 12], F32) for i in range(2)],
                [A(f"{nm}ag{i}", [P, 2], F32) for i in range(2)],
                [A(f"{nm}lv{i}", [P, 1], F32) for i in range(2)],
                [A(f"{nm}rs{i}", [P, 1], F32) for i in range(2)])

    bnA, agA, lvA, rsA = stat4("A")
    bn0, ag0, lv0, rs0 = stat4("B")
    ag4, lv4, rs4 = agA + ag0, lvA + lv0, rsA + rs0
    bn1, ag1, lv1, rs1 = stat4("C")
    bn2, ag2, lv2, rs2 = stat4("D")
    bnF, agF, lvF, rsF = stat4("E")

    qs = [A(f"qs{i}", [P, HEADS], F32) for i in range(2)]
    rq = [A(f"rq{i}", [P, HEADS], F32) for i in range(2)]
    scb = [A(f"scb{i}", [P, NE], F32) for i in range(2)]
    esb = [A(f"esb{i}", [P, NE], F32) for i in range(2)]
    ngm = [A(f"ngm{i}", [P, 1], F32) for i in range(2)]
    rsm = [A(f"rsm{i}", [P, 1], F32) for i in range(2)]
    mx1 = [A(f"mx1{i}", [P, 1], F32) for i in range(2)]
    m2v = [A(f"m2v{i}", [P, 1], F32) for i in range(2)]
    mk1 = [A(f"mk1{i}", [P, NE], U32) for i in range(2)]
    mk2 = [A(f"mk2{i}", [P, NE], U32) for i in range(2)]
    sm2 = [A(f"sm2{i}", [P, NE], F32) for i in range(2)]
    gsl = [A(f"gsl{i}", [P, NE], F32) for i in range(2)]
    ssm = [A(f"ssm{i}", [P, 1], F32) for i in range(2)]

    wqb = A("wqb", [P, 8 * HID], BF16)
    wob = A("wob", [P, 4 * D], BF16)
    xA = [A(f"xA{i}", [P, D], F32) for i in range(2)]
    hb = [A(f"hb{i}", [P, D], BF16) for i in range(2)]
    hT = [A(f"hT{i}", [P, D], BF16) for i in range(2)]
    qe = [A(f"qe{i}", [P, HID], F32) for i in range(2)]
    qn = [A(f"qn{i}", [P, HID], BF16) for i in range(2)]
    qTt = [A(f"qTt{i}", [P, HID], BF16) for i in range(2)]
    o2b = [A(f"o2b{i}", [P, HID], BF16) for i in range(2)]
    att = [A(f"att{i}", [P, D], F32) for i in range(2)]
    ub = [A(f"ub{i}", [P, D], BF16) for i in range(2)]
    wrt = [A(f"wrt{i}", [P, 8 * HID], BF16, is_transpose_dest=True) for i in range(2)]
    _rs, _re = nc.bump_sbuf(max(QT * D * 4, 32768), "left", align_bytes=128)
    racc = nc.alloc_sbuf_tensor_at("racc", [P, QT * D], F32, offset=_rs)
    wkb = nc.alloc_sbuf_tensor_at("wkb", [P, 8 * HID], BF16, offset=_rs)
    wvb = nc.alloc_sbuf_tensor_at("wvb", [P, 8 * HID], BF16, offset=_rs + 8 * HID * 2)
    xA4 = xA + [nc.alloc_sbuf_tensor_at(f"xA{i}", [P, D], F32,
                offset=_rs + 16384 + (i - 2) * 4096) for i in (2, 3)]
    hT4 = hT + [nc.alloc_sbuf_tensor_at(f"hT{i}", [P, D], BF16,
                offset=_rs + 24576 + (i - 2) * 2048) for i in (2, 3)]
    hb4 = hb + [nc.alloc_sbuf_tensor_at(f"hb{i}", [P, D], BF16,
                offset=_rs + 28672 + (i - 2) * 2048) for i in (2, 3)]
    _fs, _fe = nc.bump_sbuf(16384, "left", align_bytes=128)
    AT = nc.alloc_sbuf_tensor_at
    yld = [AT(f"yld{i}", [P, D], F32, offset=_fs + i * 4096) for i in range(2)]
    ftmp = [AT(f"ftmp{i}", [P, D], F32, offset=_fs + 8192 + i * 4096) for i in range(2)]
    bo_row = AT("bo_row", [1, D], F32, offset=_fs)
    cen_st = AT("cen_st", [NE, D], BF16, offset=_fs + 4096)
    eb_row = AT("eb_row", [1, NE], F32, offset=_fs + 6144)
    ek = [AT(f"ek{i}", [P, HID], BF16, offset=_fs + 6272 + i * 1024) for i in range(2)]
    vbq = [AT(f"vb{i}", [P, 8 * 65], BF16, offset=_fs + 8352 + i * 1056) for i in range(2)]

    PA = nc.alloc_psum_tensor
    pbT = PA("pbT", [P, 1024], BF16)
    pbT2 = PA("pbT2", [P, 1024], BF16)
    pb = [PA(f"pb{i}", [P, 512], F32) for i in range(6)]

    def a1_items(t):
        return [("a1_v_st", t), ("a1_a_rs", t), ("a1_v_hb", t),
                ("a1_p_trhb", t), ("a1_v_hT", t), ("a1_p_k", t),
                ("a1_a_ek", t), ("a1_p_v", t), ("a1_v_vb", t),
                ("a1_p_E", t), ("a1_v_Eacc", t),
                ("a1_p_q2", t), ("a1_a_eq", t), ("a1_v_qn2", t)]

    def zip_merge(la, lb):
        outl = []
        na, nb = len(la), len(lb)
        ia = ib = 0
        while ia < na or ib < nb:
            if ia < na:
                outl.append(la[ia]); ia += 1
            if ib < nb:
                outl.append(lb[ib]); ib += 1
        return outl

    flow_a1 = []
    HEADI = 7
    if NT % 4 == 0:
        for t in range(0, NT, 4):
            lists = [a1_items(t + d) for d in range(4)]
            n = len(lists[0])
            for r in range(HEADI):
                for lst in lists:
                    flow_a1.append(lst[r])
            for pp in range(2):
                for r in range(HEADI, n):
                    for lst in lists[2 * pp:2 * pp + 2]:
                        flow_a1.append(lst[r])
    else:
        for t in range(0, NT, 2):
            flow_a1 += zip_merge(a1_items(t), a1_items(t + 1))

    def a2_items(t):
        return [("a2_p_o2", t), ("a2_v_o2ev", t),
                ("a2_p_wo", t), ("a2_v_att", t), ("a2_a_rs1", t),
                ("a2_v_y", t), ("a2_a_rs2", t), ("a2_v_ub", t),
                ("a2_p_trub", t), ("a2_v_uT", t), ("a2_p_sc", t),
                ("a2_v_scev", t), ("a2_a_es", t), ("a2_v_g", t),
                ("a2_p_trg", t), ("a2_v_gT", t)]

    units = [(q, s, oh) for q in range(NQ) for s in range(1, NSTEP)
             for oh in range(2)]

    def unit_idx(q, s, oh):
        return (q * (NSTEP - 1) + (s - 1)) * 2 + oh

    def ffn_quarter(q):
        its = []
        for oh in range(2):
            for j in range(QT):
                its.append(("ffnp", q, 0, oh, j))
                its.append(("ffnv", q, 0, oh, j))
        for s in range(1, NSTEP):
            for oh in range(2):
                k = unit_idx(q, s, oh)
                if k + 2 < len(units):
                    its.append(("ffnl",) + units[k + 2])
                for j in range(QT):
                    its.append(("ffnp", q, s, oh, j))
                    its.append(("ffnv", q, s, oh, j))
        return its

    def fin_quarter(q):
        its = []
        for j in range(QT):
            its.append(("fina", q, j))
            its.append(("finact", q, j))
            if j >= 1:
                its.append(("finb", q, j - 1))
        its.append(("finb", q, QT - 1))
        return its

    def prop_merge(la, lb):
        outl = []
        na, nb = len(la), len(lb)
        if na == 0:
            return list(lb)
        ratio = nb / na
        ib = 0
        for ia in range(na):
            outl.append(la[ia])
            want = int(round((ia + 1) * ratio * 0.65))
            while ib < want:
                outl.append(lb[ib]); ib += 1
        outl.extend(lb[ib:])
        return outl

    flow_tail = []
    flow_tail.append(("ffnl",) + units[0])
    flow_tail.append(("ffnl",) + units[1])
    for w in range(NQ + 2):
        la = []
        if w < NQ:
            for t in range(QT * w, QT * w + QT, 2):
                la += zip_merge(a2_items(t), a2_items(t + 1))
        lb = []
        if 2 <= w:
            lb += fin_quarter(w - 2)
        if 1 <= w <= NQ:
            lb += ffn_quarter(w - 1)
        flow_tail += prop_merge(la, lb)

    pool = [
        [0, ("aA1", (NT - 2, "expk")), -1],
        [1, ("vA1", (NT - 1, "vbev")), -1],
        [2, ("vA1", (NT - 2, "Eacc")), -1],
        [3, ("vA1", (NT - 1, "Eacc")), -1],
        [4, ("aA1", (NT - 1, "expk")), -1],
        [5, ("aA1", (NT - 1, "expq1")), -1],
    ]
    _needs = {"a2_p_o2": ("vA2", "o2ev", 1),
              "a2_p_wo": ("vA2", "att", 2), "a2_p_sc": ("vA2", "scev", 1)}
    _evac_tag = {"a2_p_o2": "a2_v_o2ev",
                 "a2_p_wo": "a2_v_att", "a2_p_sc": "a2_v_scev"}
    evac_pos = {}
    for pos, it in enumerate(flow_tail):
        if it[0] in ("a2_v_o2ev", "a2_v_att", "a2_v_scev"):
            evac_pos[(it[0], it[1])] = pos
        elif it[0] == "ffnv":
            evac_pos[("ffnv", it[1], it[2], it[3], it[4])] = pos

    def pool_take(pos, n):
        got = []
        for _ in range(n):
            cands = [e for e in pool if e[2] < pos]
            assert cands, f"no reusable psum bank at flow pos {pos}"
            e = min(cands, key=lambda x: x[2])
            pool.remove(e)
            got.append(e)
        return got

    pend = {}
    flow2 = []
    for pos, it in enumerate(flow_tail):
        tag = it[0]
        if tag in _needs:
            t = it[1]
            chan, lbl, nbk = _needs[tag]
            got = pool_take(pos, nbk)
            ep = evac_pos[(_evac_tag[tag], t)]
            res = []
            for bk, pv, _e in got:
                res.append((bk, pv))
                pool.append([bk, (chan, (t, lbl)), ep])
            pend[(tag, t)] = res
            flow2.append(it + tuple(res))
        elif tag == "a2_v_o2ev":
            flow2.append(it + (pend[("a2_p_o2", it[1])][0][0],))
        elif tag == "a2_v_att":
            g = pend[("a2_p_wo", it[1])]
            flow2.append(it + (g[0][0], g[1][0]))
        elif tag == "a2_v_scev":
            flow2.append(it + (pend[("a2_p_sc", it[1])][0][0],))
        elif tag == "ffnp":
            q, s, oh, j = it[1], it[2], it[3], it[4]
            got = pool_take(pos, 1)[0]
            ep = evac_pos[("ffnv", q, s, oh, j)]
            pool.append([got[0], ("vF", (q, s, oh, j)), ep])
            pend[("f", q, s, oh, j)] = (got[0], got[1])
            flow2.append(it + ((got[0], got[1]),))
        elif tag == "ffnv":
            q, s, oh, j = it[1], it[2], it[3], it[4]
            flow2.append(it + (pend[("f", q, s, oh, j)][0],))
        else:
            flow2.append(it)
    flow_tail = flow2

    def h_sync(em, sync, it):
        tag = it[0]
        if tag == "a2_v_y":
            t = it[1]; i = t % 2
            em.wait(sync, "vA2", (t, "y"))
            em.dma(sync, f"dY{i}", t, yscr[t * P:(t + 1) * P, :], att[i][:])
            if t + 2 < NT:
                em.dma(sync, f"dX2{i}", t + 2, xA[i][:, 0:HID],
                       xin[(t + 2) * P:(t + 3) * P, 0:HID])
                em.dma(sync, f"dX2h{i}", t + 2, xA[i][:, HID:D],
                       xin[(t + 2) * P:(t + 3) * P, HID:D])

        elif tag == "ffnl":
            q, s, oh = it[1], it[2], it[3]
            k = unit_idx(q, s, oh)
            par = k % 2
            if k >= 2:
                pq, ps, po = units[k - 2]
                em.wait(sync, "pF", (pq, ps, po, QT - 1))
            em.wait(sync, "dWC", NMAT - 1)
            for kk in range(8):
                em.dma(sync, f"dWT{par}", (q, s, oh, kk),
                       wrt[par][:, kk * HID:(kk + 1) * HID],
                       wbf[s - 1, oh * HID:(oh + 1) * HID, kk * P:(kk + 1) * P],
                       transpose=True)
        elif tag == "fina":
            q, j = it[1], it[2]
            t = q * QT + j
            em.wait(sync, f"dY{t % 2}", t)
            if (q, j) not in ((0, 0), (0, 1)):
                pj = (q, j - 2) if j >= 2 else (q - 1, j + QT - 2)
                em.wait(sync, "vFF", (pj[0], pj[1], "fout"))
            em.dma(sync, f"dYL{j % 2}", (q, j), yld[j % 2][:],
                   yscr[t * P:(t + 1) * P, :])
        elif tag == "finb":
            q, j = it[1], it[2]
            t = q * QT + j
            em.wait(sync, "vFF", (q, j, "fout"))
            em.dma(sync, f"dOUT{j % 2}", (q, j), out[t * P:(t + 1) * P, :],
                   ftmp[j % 2][:])

    def ln_stats(v, bn, ag, src):
        v.bn_stats(bn[:, 0:6], src[:, 0:512])
        v.bn_stats(bn[:, 6:12], src[:, 512:1024])
        v.drain()
        v.bn_aggr(ag[:], bn[:, 0:12])

    def h_vector(em, v, it):
        tag = it[0]
        t = it[1] if len(it) > 1 else None
        i = t % 2 if isinstance(t, int) else None
        if tag == "a1_v_st":
            i4 = t % 4 if NT % 4 == 0 else t % 2
            em.wait(v, f"dX{i4}", t)
            em.wait(v, f"dXh{i4}", t)
            if t >= 4:
                em.wait(v, "aA1", (t - 4, "rs"))
            ln_stats(v, bnA[i], ag4[i4], xA4[i4])
            em.ms(v, "vA1", (t, "st"))
        elif tag == "a1_v_hb":
            i4 = t % 4 if NT % 4 == 0 else t % 2
            em.wait(v, "aA1", (t, "rs"))
            if t >= 4:
                em.wait(v, "pA1", (t - 4, "trhb"))
            v.tensor_scalar(hb4[i4][:], xA4[i4][:], ag4[i4][:, 0:1], rs4[i4][:],
                            ALU.subtract, ALU.mult)
            em.ms(v, "vA1", (t, "hb"))
        elif tag == "a1_v_hT":
            i4 = t % 4 if NT % 4 == 0 else t % 2
            em.wait(v, "pA1", (t, "trhb"))
            if t >= 4:
                em.wait(v, "pA1", (t - 4, "mm_q1"))
            pT = pbT if t % 2 == 0 else pbT2
            v.tensor_copy(hT4[i4][:].rearrange("p (k c) -> p k c", k=8),
                          pT[:].rearrange("p (k c) -> p k c", k=8))
            em.ms(v, "vA1", (t, "hTev"))
        elif tag == "a1_v_vb":
            em.wait(v, "pA1", (t, "mm_v"))
            if t >= 2:
                em.wait(v, "pA1", (t - 2, "mm_E"))
            v.tensor_copy(vbq[i][:].rearrange("p (h c) -> p h c", c=65)[:, :, 0:64],
                          pb[1][:].rearrange("p (h c) -> p h c", c=64))
            em.ms(v, "vA1", (t, "vbev"))
        elif tag == "a1_v_Eacc":
            em.wait(v, "pA1", (t, "mm_E"))
            v.drain()
            ii = v.tensor_tensor(E_sb[:], E_sb[:], pb[2 + i][:, 0:260], ALU.add)
            em.op_ms(ii, "vA1", (t, "Eacc"))
        elif tag == "a1_v_qn2":
            em.wait(v, "aA1", (t, "expq1"))
            if t >= 2:
                em.wait(v, f"dQS{t % 2}", t - 2)
            v.reduce_sum(qs[i][:].rearrange("p (h o) -> p h o", o=1),
                         qe[i][:].rearrange("p (h c) -> p h c", h=HEADS),
                         axis=AX.X)
            v.drain()
            v.reciprocal(rq[i][:], qs[i][:])
            v.drain()
            for h in range(HEADS):
                v.tensor_scalar(qn[i][:, h * DH:(h + 1) * DH],
                                qe[i][:, h * DH:(h + 1) * DH],
                                rq[i][:, h:h + 1], SCALE, ALU.mult, ALU.mult)
            em.ms(v, "vA1", (t, "qn2"))
        elif tag == "mid":
            v.drain()
            for h in range(HEADS):
                pbase, fo = (h % 2) * 64, (h // 2) * 65
                v.reciprocal(sInv[pbase:pbase + 64, h // 2:h // 2 + 1],
                             E_sb[pbase:pbase + 64, fo + 64:fo + 65])
            v.drain()
            for h in range(HEADS):
                pbase, fo = (h % 2) * 64, (h // 2) * 65
                v.tensor_scalar(ctx[pbase:pbase + 64, (h // 2) * DH:(h // 2 + 1) * DH],
                                E_sb[pbase:pbase + 64, fo:fo + 64],
                                sInv[pbase:pbase + 64, h // 2:h // 2 + 1],
                                None, ALU.mult)
            em.ms(v, "vMID", "ctx")
        elif tag == "a2_v_o2ev":
            flow2.append(it + (pend[("a2_p_o2", it[1])][0][0],))
        elif tag == "a2_v_att":
            g = pend[("a2_p_wo", it[1])]
            flow2.append(it + (g[0][0], g[1][0]))
        elif tag == "a2_v_scev":
            flow2.append(it + (pend[("a2_p_sc", it[1])][0][0],))
        elif tag == "ffnp":
            q, s, oh, j = it[1], it[2], it[3], it[4]
            got = pool_take(pos, 1)[0]
            ep = evac_pos[("ffnv", q, s, oh, j)]
            pool.append([got[0], ("vF", (q, s, oh, j)), ep])
            pend[("f", q, s, oh, j)] = (got[0], got[1])
            flow2.append(it + ((got[0], got[1]),))
        elif tag == "ffnv":
            q, s, oh, j = it[1], it[2], it[3], it[4]
            flow2.append(it + (pend[("f", q, s, oh, j)][0],))
        else:
            flow2.append(it)
    flow_tail = flow2

    def h_sync(em, sync, it):
        tag = it[0]
        if tag == "a2_v_y":
            t = it[1]; i = t % 2
            em.wait(sync, "vA2", (t, "y"))
            em.dma(sync, f"dY{i}", t, yscr[t * P:(t + 1) * P, :], att[i][:])
            if t + 2 < NT:
                em.dma(sync, f"dX2{i}", t + 2, xA[i][:, 0:HID],
                       xin[(t + 2) * P:(t + 3) * P, 0:HID])
                em.dma(sync, f"dX2h{i}", t + 2, xA[i][:, HID:D],
                       xin[(t + 2) * P:(t + 3) * P, HID:D])

        elif tag == "ffnl":
            q, s, oh = it[1], it[2], it[3]
            k = unit_idx(q, s, oh)
            par = k % 2
            if k >= 2:
                pq, ps, po = units[k - 2]
                em.wait(sync, "pF", (pq, ps, po, QT - 1))
            em.wait(sync, "dWC", NMAT - 1)
            for kk in range(8):
                em.dma(sync, f"dWT{par}", (q, s, oh, kk),
                       wrt[par][:, kk * HID:(kk + 1) * HID],
                       wbf[s - 1, oh * HID:(oh + 1) * HID, kk * P:(kk + 1) * P],
                       transpose=True)
        elif tag == "fina":
            q, j = it[1], it[2]
            t = q * QT + j
            em.wait(sync, f"dY{t % 2}", t)
            if (q, j) not in ((0, 0), (0, 1)):
                pj = (q, j - 2) if j >= 2 else (q - 1, j + QT - 2)
                em.wait(sync, "vFF", (pj[0], pj[1], "fout"))
            em.dma(sync, f"dYL{j % 2}", (q, j), yld[j % 2][:],
                   yscr[t * P:(t + 1) * P, :])
        elif tag == "finb":
            q, j = it[1], it[2]
            t = q * QT + j
            em.wait(sync, "vFF", (q, j, "fout"))
            em.dma(sync, f"dOUT{j % 2}", (q, j), out[t * P:(t + 1) * P, :],
                   ftmp[j % 2][:])

    def ln_stats(v, bn, ag, src):
        v.bn_stats(bn[:, 0:6], src[:, 0:512])
        v.bn_stats(bn[:, 6:12], src[:, 512:1024])
        v.drain()
        v.bn_aggr(ag[:], bn[:, 0:12])

    def h_vector(em, v, it):
        tag = it[0]
        t = it[1] if len(it) > 1 else None
        i = t % 2 if isinstance(t, int) else None
        if tag == "a1_v_st":
            i4 = t % 4 if NT % 4 == 0 else t % 2
            em.wait(v, f"dX{i4}", t)
            em.wait(v, f"dXh{i4}", t)
            if t >= 4:
                em.wait(v, "aA1", (t - 4, "rs"))
            ln_stats(v, bnA[i], ag4[i4], xA4[i4])
            em.ms(v, "vA1", (t, "st"))
        elif tag == "a1_v_hb":
            i4 = t % 4 if NT % 4 == 0 else t % 2
            em.wait(v, "aA1", (t, "rs"))
            if t >= 4:
                em.wait(v, "pA1", (t - 4, "trhb"))
            v.tensor_scalar(hb4[i4][:], xA4[i4][:], ag4[i4][:, 0:1], rs4[i4][:],
                            ALU.subtract, ALU.mult)
            em.ms(v, "vA1", (t, "hb"))
        elif tag == "a1_v_hT":
            i4 = t % 4 if NT % 4 == 0 else t % 2
            em.wait(v, "pA1", (t, "trhb"))
            if t >= 4:
                em.wait(v, "pA1", (t - 4, "mm_q1"))
            pT = pbT if t % 2 == 0 else pbT2
            v.tensor_copy(hT4[i4][:].rearrange("p (k c) -> p k c", k=8),
                          pT[:].rearrange("p (k c) -> p k c", k=8))
            em.ms(v, "vA1", (t, "hTev"))
        elif tag == "a1_v_vb":
            em.wait(v, "pA1", (t, "mm_v"))
            if t >= 2:
                em.wait(v, "pA1", (t - 2, "mm_E"))
            v.tensor_copy(vbq[i][:].rearrange("p (h c) -> p h c", c=65)[:, :, 0:64],
                          pb[1][:].rearrange("p (h c) -> p h c", c=64))
            em.ms(v, "vA1", (t, "vbev"))
        elif tag == "a1_v_Eacc":
            em.wait(v, "pA1", (t, "mm_E"))
            v.drain()
            ii = v.tensor_tensor(E_sb[:], E_sb[:], pb[2 + i][:, 0:260], ALU.add)
            em.op_ms(ii, "vA1", (t, "Eacc"))
        elif tag == "a1_v_qn2":
            em.wait(v, "aA1", (t, "expq1"))
            if t >= 2:
                em.wait(v, f"dQS{t % 2}", t - 2)
            v.reduce_sum(qs[i][:].rearrange("p (h o) -> p h o", o=1),
                         qe[i][:].rearrange("p (h c) -> p h c", h=HEADS),
                         axis=AX.X)
            v.drain()
            v.reciprocal(rq[i][:], qs[i][:])
            v.drain()
            for h in range(HEADS):
                v.tensor_scalar(qn[i][:, h * DH:(h + 1) * DH],
                                qe[i][:, h * DH:(h + 1) * DH],
                                rq[i][:, h:h + 1], SCALE, ALU.mult, ALU.mult)
            em.ms(v, "vA1", (t, "qn2"))
        elif tag == "mid":
            v.drain()
            for h in range(HEADS):
                pbase, fo = (h % 2) * 64, (h // 2) * 65
                v.reciprocal(sInv[pbase:pbase + 64, h // 2:h // 2 + 1],
                             E_sb[pbase:pbase + 64, fo + 64:fo + 65])
            v.drain()
            for h in range(HEADS):
                pbase, fo = (h % 2) * 64, (h // 2) * 65
                v.tensor_scalar(ctx[pbase:pbase + 64, (h // 2) * DH:(h // 2 + 1) * DH],
                                E_sb[pbase:pbase + 64, fo:fo + 64],
                                sInv[pbase:pbase + 64, h // 2:h // 2 + 1],
                                None, ALU.mult)
            em.ms(v, "vMID", "ctx")
        elif tag == "a2_v_o2ev":
            flow2.append(it + (pend[("a2_p_o2", it[1])][0][0],))
        elif tag == "a2_v_att":
            g = pend[("a2_p_wo", it[1])]
            flow2.append(it + (g[0][0], g[1][0]))
        elif tag == "a2_v_scev":
            flow2.append(it + (pend[("a2_p_sc", it[1])][0][0],))
        elif tag == "ffnp":
            q, s, oh, j = it[1], it[2], it[3], it[4]
            got = pool_take(pos, 1)[0]
            ep = evac_pos[("ffnv", q, s, oh, j)]
            pool.append([got[0], ("vF", (q, s, oh, j)), ep])
            pend[("f", q, s, oh, j)] = (got[0], got[1])
            flow2.append(it + ((got[0], got[1]),))
        elif tag == "ffnv":
            q, s, oh, j = it[1], it[2], it[3], it[4]
            flow2.append(it + (pend[("f", q, s, oh, j)][0],))
        else:
            flow2.append(it)
    flow_tail = flow2

    def h_sync(em, sync, it):
        tag = it[0]
        if tag == "a2_v_y":
            t = it[1]; i = t % 2
            em.wait(sync, "vA2", (t, "y"))
            em.dma(sync, f"dY{i}", t, yscr[t * P:(t + 1) * P, :], att[i][:])
            if t + 2 < NT:
                em.dma(sync, f"dX2{i}", t + 2, xA[i][:, 0:HID],
                       xin[(t + 2) * P:(t + 3) * P, 0:HID])
                em.dma(sync, f"dX2h{i}", t + 2, xA[i][:, HID:D],
                       xin[(t + 2) * P:(t + 3) * P, HID:D])

        elif tag == "ffnl":
            q, s, oh = it[1], it[2], it[3]
            k = unit_idx(q, s, oh)
            par = k % 2
            if k >= 2:
                pq, ps, po = units[k - 2]
                em.wait(sync, "pF", (pq, ps, po, QT - 1))
            em.wait(sync, "dWC", NMAT - 1)
            for kk in range(8):
                em.dma(sync, f"dWT{par}", (q, s, oh, kk),
                       wrt[par][:, kk * HID:(kk + 1) * HID],
                       wbf[s - 1, oh * HID:(oh + 1) * HID, kk * P:(kk + 1) * P],
                       transpose=True)
        elif tag == "fina":
            q, j = it[1], it[2]
            t = q * QT + j
            em.wait(sync, f"dY{t % 2}", t)
            if (q, j) not in ((0, 0), (0, 1)):
                pj = (q, j - 2) if j >= 2 else (q - 1, j + QT - 2)
                em.wait(sync, "vFF", (pj[0], pj[1], "fout"))
            em.dma(sync, f"dYL{j % 2}", (q, j), yld[j % 2][:],
                   yscr[t * P:(t + 1) * P, :])
        elif tag == "finb":
            q, j = it[1], it[2]
            t = q * QT + j
            em.wait(sync, "vFF", (q, j, "fout"))
            em.dma(sync, f"dOUT{j % 2}", (q, j), out[t * P:(t + 1) * P, :],
                   ftmp[j % 2][:])

    def ln_stats(v, bn, ag, src):
        v.bn_stats(bn[:, 0:6], src[:, 0:512])
        v.bn_stats(bn[:, 6:12], src[:, 512:1024])
        v.drain()
        v.bn_aggr(ag[:], bn[:, 0:12])

    def h_vector(em, v, it):
        tag = it[0]
        t = it[1] if len(it) > 1 else None
        i = t % 2 if isinstance(t, int) else None
        if tag == "a1_v_st":
            i4 = t % 4 if NT % 4 == 0 else t % 2
            em.wait(v, f"dX{i4}", t)
            em.wait(v, f"dXh{i4}", t)
            if t >= 4:
                em.wait(v, "aA1", (t - 4, "rs"))
            ln_stats(v, bnA[i], ag4[i4], xA4[i4])
            em.ms(v, "vA1", (t, "st"))
        elif tag == "a1_v_hb":
            i4 = t % 4 if NT % 4 == 0 else t % 2
            em.wait(v, "aA1", (t, "rs"))
            if t >= 4:
                em.wait(v, "pA1", (t - 4, "trhb"))
            v.tensor_scalar(hb4[i4][:], xA4[i4][:], ag4[i4][:, 0:1], rs4[i4][:],
                            ALU.subtract, ALU.mult)
            em.ms(v, "vA1", (t, "hb"))
        elif tag == "a1_v_hT":
            i4 = t % 4 if NT % 4 == 0 else t % 2
            em.wait(v, "pA1", (t, "trhb"))
            if t >= 4:
                em.wait(v, "pA1", (t - 4, "mm_q1"))
            pT = pbT if t % 2 == 0 else pbT2
            v.tensor_copy(hT4[i4][:].rearrange("p (k c) -> p k c", k=8),
                          pT[:].rearrange("p (k c) -> p k c", k=8))
            em.ms(v, "vA1", (t, "hTev"))
        elif tag == "a1_v_vb":
            em.wait(v, "pA1", (t, "mm_v"))
            if t >= 2:
                em.wait(v, "pA1", (t - 2, "mm_E"))
            v.tensor_copy(vbq[i][:].rearrange("p (h c) -> p h c", c=65)[:, :, 0:64],
                          pb[1][:].rearrange("p (h c) -> p h c", c=64))
            em.ms(v, "vA1", (t, "vbev"))
        elif tag == "a1_v_Eacc":
            em.wait(v, "pA1", (t, "mm_E"))
            v.drain()
            ii = v.tensor_tensor(E_sb[:], E_sb[:], pb[2 + i][:, 0:260], ALU.add)
            em.op_ms(ii, "vA1", (t, "Eacc"))
        elif tag == "a1_v_qn2":
            em.wait(v, "aA1", (t, "expq1"))
            if t >= 2:
                em.wait(v, f"dQS{t % 2}", t - 2)
            v.reduce_sum(qs[i][:].rearrange("p (h o) -> p h o", o=1),
                         qe[i][:].rearrange("p (h c) -> p h c", h=HEADS),
                         axis=AX.X)
            v.drain()
            v.reciprocal(rq[i][:], qs[i][:])
            v.drain()
            for h in range(HEADS):
                v.tensor_scalar(qn[i][:, h * DH:(h + 1) * DH],
                                qe[i][:, h * DH:(h + 1) * DH],
                                rq[i][:, h:h + 1], SCALE, ALU.mult, ALU.mult)
            em.ms(v, "vA1", (t, "qn2"))
        elif tag == "mid":
            v.drain()
            for h in range(HEADS):
                pbase, fo = (h % 2) * 64, (h // 2) * 65
                v.reciprocal(sInv[pbase:pbase + 64, h // 2:h // 2 + 1],
                             E_sb[pbase:pbase + 64, fo + 64:fo + 65])
            v.drain()
            for h in range(HEADS):
                pbase, fo = (h % 2) * 64, (h // 2) * 65
                v.tensor_scalar(ctx[pbase:pbase + 64, (h // 2) * DH:(h // 2 + 1) * DH],
                                E_sb[pbase:pbase + 64, fo:fo + 64],
                                sInv[pbase:pbase + 64, h // 2:h // 2 + 1],
                                None, ALU.mult)
            em.ms(v, "vMID", "ctx")
        elif tag == "a2_v_o2ev":
            flow2.append(it + (pend[("a2_p_o2", it[1])][0][0],))
        elif tag == "a2_v_att":
            g = pend[("a2_p_wo", it[1])]
            flow2.append(it + (g[0][0], g[1][0]))
        elif tag == "a2_v_scev":
            flow2.append(it + (pend[("a2_p_sc", it[1])][0][0],))
        elif tag == "ffnp":
            q, s, oh, j = it[1], it[2], it[3], it[4]
            got = pool_take(pos, 1)[0]
            ep = evac_pos[("ffnv", q, s, oh, j)]
            pool.append([got[0], ("vF", (q, s, oh, j)), ep])
            pend[("f", q, s, oh, j)] = (got[0], got[1])
            flow2.append(it + ((got[0], got[1]),))
        elif tag == "ffnv":
            q, s, oh, j = it[1], it[2], it[3], it[4]
            flow2.append(it + (pend[("f", q, s, oh, j)][0],))
        else:
            flow2.append(it)
    flow_tail = flow2

    def h_sync(em, sync, it):
        tag = it[0]
        if tag == "a2_v_y":
            t = it[1]; i = t % 2
            em.wait(sync, "vA2", (t, "y"))
            em.dma(sync, f"dY{i}", t, yscr[t * P:(t + 1) * P, :], att[i][:])
            if t + 2 < NT:
                em.dma(sync, f"dX2{i}", t + 2, xA[i][:, 0:HID],
                       xin[(t + 2) * P:(t + 3) * P, 0:HID])
                em.dma(sync, f"dX2h{i}", t + 2, xA[i][:, HID:D],
                       xin[(t + 2) * P:(t + 3) * P, HID:D])

        elif tag == "ffnl":
            q, s, oh = it[1], it[2], it[3]
            k = unit_idx(q, s, oh)
            par = k % 2
            if k >= 2:
                pq, ps, po = units[k - 2]
                em.wait(sync, "pF", (pq, ps, po, QT - 1))
            em.wait(sync, "dWC", NMAT - 1)
            for kk in range(8):
                em.dma(sync, f"dWT{par}", (q, s, oh, kk),
                       wrt[par][:, kk * HID:(kk + 1) * HID],
                       wbf[s - 1, oh * HID:(oh + 1) * HID, kk * P:(kk + 1) * P],
                       transpose=True)
        elif tag == "fina":
            q, j = it[1], it[2]
            t = q * QT + j
            em.wait(sync, f"dY{t % 2}", t)
            if (q, j) not in ((0, 0), (0, 1)):
                pj = (q, j - 2) if j >= 2 else (q - 1, j + QT - 2)
                em.wait(sync, "vFF", (pj[0], pj[1], "fout"))
            em.dma(sync, f"dYL{j % 2}", (q, j), yld[j % 2][:],
                   yscr[t * P:(t + 1) * P, :])
        elif tag == "finb":
            q, j = it[1], it[2]
            t = q * QT + j
            em.wait(sync, "vFF", (q, j, "fout"))
            em.dma(sync, f"dOUT{j % 2}", (q, j), out[t * P:(t + 1) * P, :],
                   ftmp[j % 2][:])

    def ln_stats(v, bn, ag, src):
        v.bn_stats(bn[:, 0:6], src[:, 0:512])
        v.bn_stats(bn[:, 6:12], src[:, 512:1024])
        v.drain()
        v.bn_aggr(ag[:], bn[:, 0:12])

    def h_vector(em, v, it):
        tag = it[0]
        t = it[1] if len(it) > 1 else None
        i = t % 2 if isinstance(t, int) else None
        if tag == "a1_v_st":
            i4 = t % 4 if NT % 4 == 0 else t % 2
            em.wait(v, f"dX{i4}", t)
            em.wait(v, f"dXh{i4}", t)
            if t >= 4:
                em.wait(v, "aA1", (t - 4, "rs"))
            ln_stats(v, bnA[i], ag4[i4], xA4[i4])
            em.ms(v, "vA1", (t, "st"))
        elif tag == "a1_v_hb":
            i4 = t % 4 if NT % 4 == 0 else t % 2
            em.wait(v, "aA1", (t, "rs"))
            if t >= 4:
                em.wait(v, "pA1", (t - 4, "trhb"))
            v.tensor_scalar(hb4[i4][:], xA4[i4][:], ag4[i4][:, 0:1], rs4[i4][:],
                            ALU.subtract, ALU.mult)
            em.ms(v, "vA1", (t, "hb"))
        elif tag == "a1_v_hT":
            i4 = t % 4 if NT % 4 == 0 else t % 2
            em.wait(v, "pA1", (t, "trhb"))
            if t >= 4:
                em.wait(v, "pA1", (t - 4, "mm_q1"))
            pT = pbT if t % 2 == 0 else pbT2
            v.tensor_copy(hT4[i4][:].rearrange("p (k c) -> p k c", k=8),
                          pT[:].rearrange("p (k c) -> p k c", k=8))
            em.ms(v, "vA1", (t, "hTev"))
        elif tag == "a1_v_vb":
            em.wait(v, "pA1", (t, "mm_v"))
            if t >= 2:
                em.wait(v, "pA1", (t - 2, "mm_E"))
            v.tensor_copy(vbq[i][:].rearrange("p (h c) -> p h c", c=65)[:, :, 0:64],
                          pb[1][:].rearrange("p (h c) -> p h c", c=64))
            em.ms(v, "vA1", (t, "vbev"))
        elif tag == "a1_v_Eacc":
            em.wait(v, "pA1", (t, "mm_E"))
            v.drain()
            ii = v.tensor_tensor(E_sb[:], E_sb[:], pb[2 + i][:, 0:260], ALU.add)
            em.op_ms(ii, "vA1", (t, "Eacc"))
        elif tag == "a1_v_qn2":
            em.wait(v, "aA1", (t, "expq1"))
            if t >= 2:
                em.wait(v, f"dQS{t % 2}", t - 2)
            v.reduce_sum(qs[i][:].rearrange("p (h o) -> p h o", o=1),
                         qe[i][:].rearrange("p (h c) -> p h c", h=HEADS),
                         axis=AX.X)
            v.drain()
            v.reciprocal(rq[i][:], qs[i][:])
            v.drain()
            for h in range(HEADS):
                v.tensor_scalar(qn[i][:, h * DH:(h + 1) * DH],
                                qe[i][:, h * DH:(h + 1) * DH],
                                rq[i][:, h:h + 1], SCALE, ALU.mult, ALU.mult)
            em.ms(v, "vA1", (t, "qn2"))
        elif tag == "mid":
            v.drain()
            for h in range(HEADS):
                pbase, fo = (h % 2) * 64, (h // 2) * 65
                v.reciprocal(sInv[pbase:pbase + 64, h // 2:h // 2 + 1],
                             E_sb[pbase:pbase + 64, fo + 64:fo + 65])
            v.drain()
            for h in range(HEADS):
                pbase, fo = (h % 2) * 64, (h // 2) * 65
                v.tensor_scalar(ctx[pbase:pbase + 64, (h // 2) * DH:(h // 2 + 1) * DH],
                                E_sb[pbase:pbase + 64, fo:fo + 64],
                                sInv[pbase:pbase + 64, h // 2:h // 2 + 1],
                                None, ALU.mult)
            em.ms(v, "vMID", "ctx")
        elif tag == "a2_v_qn":
            em.wait(v, "aA2", (t, "expq"))
            v.reduce_sum(qs[i][:].rearrange("p (h o) -> p h o", o=1),
                         qe[i][:].rearrange("p (h c) -> p h c", h=HEADS),
                         axis=AX.X)
            v.drain()
            v.reciprocal(rq[i][:], qs[i][:])
            v.drain()
            if t >= 2:
                em.wait(v, "pA2", (t - 2, "trqn"))
            for h in range(HEADS):
                v.tensor_scalar(qn[i][:, h * DH:(h + 1) * DH],
                                qe[i][:, h * DH:(h + 1) * DH],
                                rq[i][:, h:h + 1], SCALE, ALU.mult, ALU.mult)
            em.ms(v, "vA2", (t, "qn"))
        elif tag == "a2_v_qT":
            em.wait(v, "pA2", (t, "trqn"))
            if t >= 2:
                em.wait(v, "pA2", (t - 2, "o2"))
            pT = pbT if t % 2 == 0 else pbT2
            v.tensor_copy(qTt[i][:], pT[:, 0:512])
            em.ms(v, "vA2", (t, "qTev"))
        elif tag == "a2_v_o2ev":
            em.wait(v, "pA2", (t, "o2"))
            if t >= 2:
                em.wait(v, "pA2", (t - 2, "wo"))
            v.tensor_copy(o2b[i][:], pb[it[2]][:])
            em.ms(v, "vA2", (t, "o2ev"))
        elif tag == "a2_v_att":
            em.wait(v, "pA2", (t, "wo0"))
            if t == 0:
                em.wait(v, "vRP", "boeb")
            if t >= 2:
                em.wait(v, "vA2", (t - 2, "ub"))
                em.wait(v, f"dY{i}", t - 2)
            v.tensor_tensor(att[i][:, 0:512], pb[it[2]][:], boRep[:, 0:512], ALU.add)
            em.wait(v, "pA2", (t, "wo"))
            ii = v.tensor_tensor(att[i][:, 512:1024], pb[it[3]][:],
                                 boRep[:, 512:1024], ALU.add)
            em.op_ms(ii, "vA2", (t, "att"))
            v.drain()
            if t >= 2:
                em.wait(v, "aA2", (t - 2, "rs1"))
            ln_stats(v, bn1[i], ag1[i], att[i])
            em.ms(v, "vA2", (t, "st1"))
        elif tag == "a2_v_y":
            em.wait(v, "aA2", (t, "rs1"))
            v.tensor_scalar(att[i][:], att[i][:], ag1[i][:, 0:1], rs1[i][:],
                            ALU.subtract, ALU.mult)
            v.drain()
            em.wait(v, f"dX2{i}", t)
            em.wait(v, f"dX2h{i}", t)
            v.tensor_tensor(att[i][:], att[i][:], xA[i][:], ALU.add)
            em.ms(v, "vA2", (t, "y"))
            if t >= 2:
                em.wait(v, "aA2", (t - 2, "rs2"))
            ln_stats(v, bn2[i], ag2[i], att[i])
            em.ms(v, "vA2", (t, "st2"))
        elif tag == "a2_v_ub":
            em.wait(v, "aA2", (t, "rs2"))
            if t >= 2:
                em.wait(v, "pA2", (t - 2, "trub"))
            v.tensor_scalar(ub[i][:], att[i][:], ag2[i][:, 0:1], rs2[i][:],
                            ALU.subtract, ALU.mult)
            em.ms(v, "vA2", (t, "ub"))
        elif tag == "a2_v_uT":
            em.wait(v, "pA2", (t, "trub"))
            pT = pbT if t % 2 == 0 else pbT2
            v.tensor_copy(uT[:].rearrange("p (k n) -> p k n", k=8)[:, :, t * P:(t + 1) * P],
                          pT[:].rearrange("p (k n) -> p k n", k=8))
            em.ms(v, "vA2", (t, "uTev"))
        elif tag == "a2_v_scev":
            em.wait(v, "pA2", (t, "sc"))
            if t >= 2:
                em.wait(v, "aA2", (t - 2, "es"))
            v.tensor_tensor(scb[i][:], pb[it[2]][:, 0:NE], ebRep[:], ALU.add)
            v.drain()
            v.reduce_max(ngm[i][:], scb[i][:], axis=AX.X, negate=True)
            em.ms(v, "vA2", (t, "scev"))
        elif tag == "a2_v_g":
            em.wait(v, "aA2", (t, "es"))
            v.reduce_sum(ssm[i][:], esb[i][:], axis=AX.X)
            v.drain()
            v.reciprocal(rsm[i][:], ssm[i][:])
            v.reduce_max(mx1[i][:], esb[i][:], axis=AX.X)
            v.drain()
            v.tensor_scalar(mk1[i][:], esb[i][:], mx1[i][:], None, ALU.is_ge)
            v.drain()
            v.select(sm2[i][:], mk1[i][:], neg3e4[:], esb[i][:], add_drain=True)
            v.drain()
            v.reduce_max(m2v[i][:], sm2[i][:], axis=AX.X)
            v.drain()
            v.tensor_scalar(mk2[i][:], esb[i][:], m2v[i][:], None, ALU.is_ge)
            v.drain()
            v.select(gsl[i][:], mk2[i][:], esb[i][:], zero16[:], add_drain=True)
            v.drain()
            v.tensor_scalar(gAll[:, t * NE:(t + 1) * NE], gsl[i][:],
                            rsm[i][:], None, ALU.mult)
            v.drain()
            if t >= 1:
                em.wait(v, "pA2", (t - 1, "trg"))
            v.tensor_copy(gb[:], gAll[:, t * NE:(t + 1) * NE])
            em.ms(v, "vA2", (t, "g"))
        elif tag == "a2_v_gT":
            em.wait(v, "pA2", (t, "trg"))
            pT = pbT if t % 2 == 0 else pbT2
            v.tensor_copy(gTx[0:NE, t * P:(t + 1) * P], pT[0:NE, 0:P])
            em.ms(v, "vA2", (t, "gTev"))
        elif tag == "ffnv":
            q, s, oh, j, bk = it[1], it[2], it[3], it[4], it[5]
            tt = q * QT + j
            em.wait(v, "pF", (q, s, oh, j))
            if q == 0 and s == 0:
                em.wait(v, "pA1", (NT - 1, "mm_E"))
            if s >= 1:
                em.wait(v, "vF", (q, s - 1, oh, j))
            elif q >= 1:
                em.wait(v, "vFF", (q - 1, j, "fout"))
            rsl = racc[:, j * D + oh * HID:j * D + (oh + 1) * HID]
            if s == 0:
                ii = v.tensor_copy(rsl, pb[bk][:])
            else:
                e = s - 1
                gcol = (gAll[:, tt * NE + e:tt * NE + e + 1]
                        if e < NE else halfc[:])
                ii = v.scalar_tensor_tensor(rsl, pb[bk][:], gcol, rsl,
                                            ALU.mult, ALU.add)
            em.op_ms(ii, "vF", (q, s, oh, j))
        elif tag == "fina":
            q, j = it[1], it[2]
            jp = j % 2
            em.wait(v, "pF", (q, NSTEP - 1, 1, j))
            if q == 0 and j < 2:
                em.wait(v, "pA1", (NT - 1, "mm_E"))
            if (q, j) not in ((0, 0), (0, 1)):
                pj = (q, j - 2) if j >= 2 else (q - 1, j + QT - 2)
                em.wait(v, "aFF", (pj[0], pj[1], "rsF"))
            v.drain()
            ln_stats(v, bnF[jp], agF[jp], racc[:, j * D:(j + 1) * D])
            em.ms(v, "vFF", (q, j, "stF"))
        elif tag == "finb":
            q, j = it[1], it[2]
            jp = j % 2
            em.wait(v, "aFF", (q, j, "rsF"))
            em.wait(v, f"dYL{jp}", (q, j))
            if (q, j) not in ((0, 0), (0, 1)):
                pj = (q, j - 2) if j >= 2 else (q - 1, j + QT - 2)
                em.wait(v, f"dOUT{jp}", pj)
            v.tensor_scalar(ftmp[jp][:], racc[:, j * D:(j + 1) * D],
                            agF[jp][:, 0:1], rsF[jp][:], ALU.subtract, ALU.mult)
            v.drain()
            v.tensor_tensor(ftmp[jp][:], ftmp[jp][:], yld[jp][:], ALU.add)
            em.ms(v, "vFF", (q, j, "fout"))
        elif tag == "setup_v":
            v.memset(onesf[:], 1.0)
            v.memset(neg3e4[:], -30000.0)
            v.memset(zero16[:], 0.0)
            v.memset(halfc[:], 0.5)
            v.memset(gTx[:], 0.5)
            v.memset(vbq[0][:], 1.0)
            v.memset(vbq[1][:], 1.0)
            v.memset(E_sb[:], 0.0)
            v.memset(eyeblk[:], 0.0)
            v.memset(epsc[:], EPS)
            em.wait(v, "dSU", "eb")
            v.drain()
            for kx in range(4):
                v.tensor_copy(eyeblk[:, kx * HID + kx * P:kx * HID + (kx + 1) * P],
                              eye[:])
            em.ms(v, "vST", "memset")
            em.wait(v, "pRP", "bo0")
            v.tensor_copy(boRep[:, 0:512], pb[5][:])
            em.ms(v, "vRP", "bo0ev")
            em.wait(v, "pRP", "bo1")
            v.tensor_copy(boRep[:, 512:1024], pb[4][:])
            em.wait(v, "pRP", "eb")
            v.tensor_copy(ebRep[:], pb[5][:, 0:NE])
            em.ms(v, "vRP", "boeb")
            em.wait(v, "pRP", "cent")
            v.tensor_copy(ct_bf[:], pbT[:, 0:P])
            em.ms(v, "vRP", "ctev")

    def h_scalar(em, sc, it):
        tag = it[0]
        t = it[1] if len(it) > 1 else None
        i = t % 2 if isinstance(t, int) else None
        if tag == "a1_a_rs":
            i4 = t % 4 if NT % 4 == 0 else t % 2
            em.wait(sc, "vA1", (t, "st"))
            if t >= 4:
                em.wait(sc, "vA1", (t - 4, "hb"))
            sc.activation(lv4[i4][:], ag4[i4][:, 1:2], AF.Ln, bias=epsc[:])
            sc.drain()
            sc.activation(rs4[i4][:], lv4[i4][:], AF.Exp, scale=-0.5)
            em.ms(sc, "aA1", (t, "rs"))
        elif tag == "a1_a_ek":
            em.wait(sc, "pA1", (t, "mm_k"))
            if t >= 2:
                em.wait(sc, "pA1", (t - 2, "mm_E"))
            sc.activation(ek[i][:], pb[0 if i == 0 else 4][:], AF.Exp)
            em.ms(sc, "aA1", (t, "expk"))
            if t + 4 < NT:
                em.dma(sc, f"dXh{(t + 4) % 4}", t + 4,
                       xA4[(t + 4) % 4][:, HID:D],
                       xin[(t + 4) * P:(t + 5) * P, HID:D])
            if t == 8 and NT >= 10:
                for c in range(min(2, NT)):
                    em.wait(sc, f"dQS{c % 2}", c)
                    for jx in range(4):
                        em.dma(sc, f"dQT{c % 2}", (c, jx),
                               qTt[c % 2][:, jx * P:(jx + 1) * P],
                               qscr[c * P:(c + 1) * P, jx * P:(jx + 1) * P],
                               transpose=True)
        elif tag == "a1_a_eq":
            em.wait(sc, "pA1", (t, "mm_q1"))
            if t >= 2:
                em.wait(sc, "vA1", (t - 2, "qn2"))
            sc.activation(qe[i][:], pb[5][:], AF.Exp)
            em.ms(sc, "aA1", (t, "expq1"))
            if t >= 1:
                em.wait(sc, "vA1", (t - 1, "qn2"))
                em.dma(sc, f"dQS{(t - 1) % 2}", t - 1,
                       qscr[(t - 1) * P:t * P, :], qn[(t - 1) % 2][:])
        elif tag == "a2_a_rs1":
            em.wait(sc, "vA2", (t, "st1"))
            if t + 2 < NT:
                em.wait(sc, f"dQS{(t + 2) % 2}", t + 2)
                for jx in range(4):
                    em.dma(sc, f"dQT{(t + 2) % 2}", (t + 2, jx),
                           qTt[(t + 2) % 2][:, jx * P:(jx + 1) * P],
                           qscr[(t + 2) * P:(t + 3) * P, jx * P:(jx + 1) * P],
                           transpose=True)
            if t >= 2:
                em.wait(sc, "vA2", (t - 2, "y"))
            sc.activation(lv1[i][:], ag1[i][:, 1:2], AF.Ln, bias=epsc[:])
            sc.drain()
            sc.activation(rs1[i][:], lv1[i][:], AF.Exp, scale=-0.5)
            em.ms(sc, "aA2", (t, "rs1"))
        elif tag == "a2_a_rs2":
            em.wait(sc, "vA2", (t, "st2"))
            if t >= 2:
                em.wait(sc, "vA2", (t - 2, "ub"))
            sc.activation(lv2[i][:], ag2[i][:, 1:2], AF.Ln, bias=epsc[:])
            sc.drain()
            sc.activation(rs2[i][:], lv2[i][:], AF.Exp, scale=-0.5)
            em.ms(sc, "aA2", (t, "rs2"))
        elif tag == "a2_a_es":
            em.wait(sc, "vA2", (t, "scev"))
            if t >= 2:
                em.wait(sc, "vA2", (t - 2, "g"))
            sc.activation(esb[i][:], scb[i][:], AF.Exp, bias=ngm[i][:])
            em.ms(sc, "aA2", (t, "es"))
        elif tag == "finact":
            q, j = it[1], it[2]
            jp = j % 2
            em.wait(sc, "vFF", (q, j, "stF"))
            if (q, j) not in ((0, 0), (0, 1)):
                pj = (q, j - 2) if j >= 2 else (q - 1, j + QT - 2)
                em.wait(sc, "vFF", (pj[0], pj[1], "fout"))
            sc.activation(lvF[jp][:], agF[jp][:, 1:2], AF.Ln, bias=epsc[:])
            sc.drain()
            sc.activation(rsF[jp][:], lvF[jp][:], AF.Exp, scale=-0.5)
            em.ms(sc, "aFF", (q, j, "rsF"))

    def h_tensor(em, pe, it):
        tag = it[0]
        t = it[1] if len(it) > 1 else None
        i = t % 2 if isinstance(t, int) else None
        if tag == "a1_p_trhb":
            em.wait(pe, "vA1", (t, "hb"))
            if t == 0:
                em.wait(pe, "vRP", "ctev")
            if t >= 2:
                em.wait(pe, "vA1", (t - 2, "hTev"))
            i4 = t % 4 if NT % 4 == 0 else t % 2
            pT = pbT if t % 2 == 0 else pbT2
            last = None
            for kk in range(8):
                last = pe.transpose(pT[:, kk * P:(kk + 1) * P],
                                    hb4[i4][:, kk * P:(kk + 1) * P], eye[:])
            em.pe_ms(last, "pA1", (t, "trhb"))
        elif tag == "a1_p_k":
            i4 = t % 4 if NT % 4 == 0 else t % 2
            em.wait(pe, "vA1", (t, "hTev"))
            if t == 0:
                em.wait(pe, "dSW", "cen")
            if t == 1:
                em.wait(pe, "vRP", "boeb")
            if t >= 2:
                em.wait(pe, "aA1", (t - 2, "expk"))
            last = None
            for kk in range(8):
                last = pe.matmul(pb[0 if i == 0 else 4][:],
                                 hT4[i4][:, kk * P:(kk + 1) * P],
                                 wkb[:, kk * HID:(kk + 1) * HID],
                                 start=(kk == 0), stop=(kk == 7))
            em.pe_ms(last, "pA1", (t, "mm_k"))
        elif tag == "a1_p_v":
            i4 = t % 4 if NT % 4 == 0 else t % 2
            if t >= 1:
                em.wait(pe, "vA1", (t - 1, "vbev"))
            last = None
            for kk in range(8):
                last = pe.matmul(pb[1][:],
                                 hT4[i4][:, kk * P:(kk + 1) * P],
                                 wvb[:, kk * HID:(kk + 1) * HID],
                                 start=(kk == 0), stop=(kk == 7))
            em.pe_ms(last, "pA1", (t, "mm_v"))
        elif tag == "a1_p_q2":
            i4 = t % 4 if NT % 4 == 0 else t % 2
            if t == 0:
                em.wait(pe, "vRP", "boeb")
            if t >= 1:
                em.wait(pe, "aA1", (t - 1, "expq1"))
            last = None
            for kk in range(8):
                last = pe.matmul(pb[5][:],
                                 hT4[i4][:, kk * P:(kk + 1) * P],
                                 wqb[:, kk * HID:(kk + 1) * HID],
                                 start=(kk == 0), stop=(kk == 7))
            em.pe_ms(last, "pA1", (t, "mm_q1"))
        elif tag == "a1_p_E":
            em.wait(pe, "aA1", (t, "expk"))
            em.wait(pe, "vA1", (t, "vbev"))
            if t >= 2:
                em.wait(pe, "vA1", (t - 2, "Eacc"))
            last = None
            for h in range(HEADS):
                pbase = (h % 2) * 64
                fo = (h // 2) * 65
                last = pe.matmul(pb[2 + i][pbase:pbase + 64, fo:fo + 65],
                                 ek[i][:, h * DH:(h + 1) * DH],
                                 vbq[i][:, h * 65:(h + 1) * 65],
                                 start=True, stop=True,
                                 tile_position=(0, pbase))
            em.pe_ms(last, "pA1", (t, "mm_E"))
        elif tag == "a2_p_o2":
            bk, pv = it[2]
            em.wait(pe, f"dQT{t % 2}", (t, 3))
            if t == 0:
                em.wait(pe, "vMID", "ctx")
            em.wait(pe, pv[0], pv[1])
            last = None
            for h in range(HEADS):
                pbase = (h % 2) * 64
                last = pe.matmul(pb[bk][pbase:pbase + 64, (h // 2) * P:(h // 2 + 1) * P],
                                 ctx[pbase:pbase + 64, (h // 2) * DH:(h // 2 + 1) * DH],
                                 qTt[i][pbase:pbase + 64, (h // 2) * P:(h // 2 + 1) * P],
                                 start=True, stop=True,
                                 tile_position=(pbase, pbase))
            em.pe_ms(last, "pA2", (t, "o2"))
        elif tag == "a2_p_wo":
            b0, pv0 = it[2]
            b1, pv1 = it[3]
            if t == 0:
                em.wait(pe, "dSW", "bs")
            em.wait(pe, "vA2", (t, "o2ev"))
            em.wait(pe, pv0[0], pv0[1])
            em.wait(pe, pv1[0], pv1[1])
            last = None
            for oh, bk in ((0, b0), (1, b1)):
                for jx in range(4):
                    last = pe.matmul(pb[bk][:], o2b[i][:, jx * P:(jx + 1) * P],
                                     wob[:, jx * D + oh * HID:jx * D + (oh + 1) * HID],
                                     start=(jx == 0), stop=(jx == 3))
                if oh == 0:
                    em.pe_ms(last, "pA2", (t, "wo0"))
            em.pe_ms(last, "pA2", (t, "wo"))
        elif tag == "a2_p_trub":
            em.wait(pe, "vA2", (t, "ub"))
            if t < 2:
                em.wait(pe, "vA1", (NT - 2 + t, "hTev"))
            else:
                em.wait(pe, "vA2", (t - 2, "gTev"))
            pT = pbT if t % 2 == 0 else pbT2
            last = None
            for kk in range(8):
                last = pe.transpose(pT[:, kk * P:(kk + 1) * P],
                                    ub[i][:, kk * P:(kk + 1) * P], eye[:])
            em.pe_ms(last, "pA2", (t, "trub"))
        elif tag == "a2_p_sc":
            bk, pv = it[2]
            em.wait(pe, "vA2", (t, "uTev"))
            em.wait(pe, pv[0], pv[1])
            last = None
            for kk in range(8):
                last = pe.matmul(pb[bk][:, 0:NE],
                                 uT[:, kk * NTOK + t * P:kk * NTOK + (t + 1) * P],
                                 ct_bf[:, kk * NE:(kk + 1) * NE],
                                 start=(kk == 0), stop=(kk == 7))
            em.pe_ms(last, "pA2", (t, "sc"))
        elif tag == "a2_p_trg":
            em.wait(pe, "vA2", (t, "g"))
            em.wait(pe, "vA2", (t, "uTev"))
            pT = pbT if t % 2 == 0 else pbT2
            last = pe.transpose(pT[0:NE, 0:P], gb[:], eye[:])
            em.pe_ms(last, "pA2", (t, "trg"))
        elif tag == "ffnp":
            q, s, oh, j = it[1], it[2], it[3], it[4]
            bk, pv = it[5]
            tt = q * QT + j
            em.wait(pe, "vA2", (tt, "gTev"))
            em.wait(pe, pv[0], pv[1])
            if s == 0:
                last = pe.matmul(pb[bk][:], gTx[:, tt * P:(tt + 1) * P],
                                 brx[:, oh * HID:(oh + 1) * HID],
                                 start=True, stop=False)
                for kx in range(4):
                    kk = 4 * oh + kx
                    last = pe.matmul(
                        pb[bk][:],
                        uT[:, kk * NTOK + tt * P:kk * NTOK + (tt + 1) * P],
                        eyeblk[:, kx * HID:(kx + 1) * HID],
                        start=False, stop=(kx == 3))
            else:
                par = unit_idx(q, s, oh) % 2
                if j == 0:
                    em.wait(pe, f"dWT{par}", (q, s, oh, 7))
                last = None
                for kk in range(8):
                    last = pe.matmul(
                        pb[bk][:],
                        uT[:, kk * NTOK + tt * P:kk * NTOK + (tt + 1) * P],
                        wrt[par][:, kk * HID:(kk + 1) * HID],
                        start=(kk == 0), stop=(kk == 7))
            em.pe_ms(last, "pF", (q, s, oh, j))
        elif tag == "setup_p":
            em.wait(pe, "dSU", "eb")
            em.wait(pe, "vST", "memset")
            i1 = pe.matmul(pb[5][:], onesf[0:1, :], bo_row[:, 0:512],
                           start=True, stop=True)
            em.pe_ms(i1, "pRP", "bo0")
            i2 = pe.matmul(pb[4][:], onesf[0:1, :], bo_row[:, 512:1024],
                           start=True, stop=True)
            em.pe_ms(i2, "pRP", "bo1")
            em.wait(pe, "vRP", "bo0ev")
            i3 = pe.matmul(pb[5][:, 0:NE], onesf[0:1, :], eb_row[:],
                           start=True, stop=True)
            em.pe_ms(i3, "pRP", "eb")
            em.wait(pe, "dSW", "cen")
            last = None
            for kk in range(8):
                last = pe.transpose(pbT[:, kk * NE:(kk + 1) * NE],
                                    cen_st[0:NE, kk * P:(kk + 1) * P],
                                    eye[0:NE, 0:NE])
            em.pe_ms(last, "pRP", "cent")

    def emit_sync(em, sync):
        em.dma(sync, "dSU", "eye", eye[:], eye_in[:])
        em.dma(sync, "dSU", "bo", bo_row[:], bo[:])
        em.dma(sync, "dSU", "eb", eb_row[:], ebi[:])
        xdepth = 4 if NT % 4 == 0 else 2
        for t in range(NT):
            if t >= xdepth:
                em.wait(sync, "vA1", (t - xdepth, "hb"))
            em.dma(sync, f"dX{t % xdepth}", t, xA4[t % xdepth][:, 0:HID],
                   xin[t * P:(t + 1) * P, 0:HID])
        em.wait(sync, "vA1", (NT - 1, "hb"))
        em.dma(sync, "dX20", 0, xA[0][:, 0:HID], xin[0:P, 0:HID])
        em.dma(sync, "dX2h0", 0, xA[0][:, HID:D], xin[0:P, HID:D])
        em.dma(sync, "dX21", 1, xA[1][:, 0:HID], xin[P:2 * P, 0:HID])
        em.dma(sync, "dX2h1", 1, xA[1][:, HID:D], xin[P:2 * P, HID:D])
        for it in flow_tail:
            h_sync(em, sync, it)
        em.wait(sync, "dOUT0", (NQ - 1, QT - 2))
        em.wait(sync, "dOUT1", (NQ - 1, QT - 1))

    def emit_gpsimd(em, gp):
        em.dma(gp, "dSW", "wq", wqb[:].rearrange("p (k c) -> p k c", k=8),
               wq.rearrange("(k p) c -> p k c", p=P))
        em.dma(gp, "dSW", "wk", wkb[:].rearrange("p (k c) -> p k c", k=8),
               wk.rearrange("(k p) c -> p k c", p=P))
        em.dma(gp, "dSW", "wv", wvb[:].rearrange("p (k c) -> p k c", k=8),
               wv.rearrange("(k p) c -> p k c", p=P))
        em.dma(gp, "dSW", "cen", cen_st[:], cen[:])
        em.wait(gp, "vA1", (min(4, NT - 1), "hb"))
        em.dma(gp, "dSW", "wo", wob[:].rearrange("p (k c) -> p k c", k=4),
               wo.rearrange("(k p) c -> p k c", p=P))
        em.dma(gp, "dSW", "br", brx[0:NE, :], brr[:])
        em.dma(gp, "dSW", "bs", brx[NE:NMAT, :], bss[:])
        for m in range(NMAT):
            src = wr[m] if m < NE else ws[m - NE]
            if m >= 1:
                em.wait(gp, "vA1", (min(2 * m + 3, NT - 1), "hb"))
            em.dma(gp, "dWC", m, wbf[m], src)
        if DEBUG:
            em.wait(gp, "dOUT0", (NQ - 1, QT - 2))
            em.wait(gp, "dOUT1", (NQ - 1, QT - 1))
            em.dma(gp, "dDBG", 0, dbg_E[:], E_sb[:])
            em.dma(gp, "dDBG", 1, dbg_ctx[:], ctx[:])
            em.dma(gp, "dDBG", 2, dbg_g[:], gAll[:])
            em.dma(gp, "dDBG", 3, dbg_y[:], yscr[:])
            em.dma(gp, "dDBG", 4, dbg_uT[:], uT[:])
            em.dma(gp, "dDBG", 6, dbg_racc[:], racc[:])
            em.dma(gp, "dDBG", 7, dbg_gTx[:], gTx[:])
            em.wait(gp, "dDBG", 7)

    def emit_vector(em, v):
        h_vector(em, v, ("setup_v",))
        for it in flow_a1:
            h_vector(em, v, it)
        h_vector(em, v, ("mid",))
        for it in flow_tail:
            h_vector(em, v, it)

    def emit_scalar(em, sc):
        for t0 in range(min(4, NT)):
            em.dma(sc, f"dXh{t0 % 4}", t0, xA4[t0 % 4][:, HID:D],
                   xin[t0 * P:(t0 + 1) * P, HID:D])
        for it in flow_a1:
            h_scalar(em, sc, it)
        em.wait(sc, "vA1", (NT - 1, "qn2"))
        em.dma(sc, f"dQS{(NT - 1) % 2}", NT - 1,
               qscr[(NT - 1) * P:NT * P, :], qn[(NT - 1) % 2][:])
        if NT < 10:
            for c in range(min(2, NT)):
                em.wait(sc, f"dQS{c % 2}", c)
                for jx in range(4):
                    em.dma(sc, f"dQT{c % 2}", (c, jx),
                           qTt[c % 2][:, jx * P:(jx + 1) * P],
                           qscr[c * P:(c + 1) * P, jx * P:(jx + 1) * P],
                           transpose=True)
        for it in flow_tail:
            h_scalar(em, sc, it)

    def emit_tensor(em, pe):
        h_tensor(em, pe, ("setup_p",))
        for it in flow_a1:
            h_tensor(em, pe, it)
        for it in flow_tail:
            h_tensor(em, pe, it)

    em = Em()
    dummy = _Dummy()
    for f in (emit_sync, emit_gpsimd, emit_tensor, emit_vector, emit_scalar):
        f(em, dummy)
    em.mode = "emit"
    em.cnt = {}
    for name in em.vals:
        em.sem[name] = nc.alloc_semaphore(name)
    with nc.Block() as block:
        @block.sync
        def _(sync):
            emit_sync(em, sync)

        @block.gpsimd
        def _(gp):
            emit_gpsimd(em, gp)

        @block.tensor
        def _(pe):
            emit_tensor(em, pe)

        @block.vector
        def _(v):
            emit_vector(em, v)

        @block.scalar
        def _(sc):
            emit_scalar(em, sc)

    return nc


def kernel(L0_attn_x, L0_attn_Wq, L0_attn_Wk, L0_attn_Wv, L0_attn_Wo,
           L0_attn_bo, L0_ffn_centroids, L0_ffn_expert_biases, L0_ffn_Wr,
           L0_ffn_br, L0_ffn_Ws, L0_ffn_bs):
    x = np.ascontiguousarray(np.asarray(L0_attn_x, dtype=np.float32))
    com = {
        "wq": np.ascontiguousarray(np.asarray(L0_attn_Wq, np.float32)),
        "wk": np.ascontiguousarray(np.asarray(L0_attn_Wk, np.float32)),
        "wv": np.ascontiguousarray(np.asarray(L0_attn_Wv, np.float32)),
        "wo": np.ascontiguousarray(np.asarray(L0_attn_Wo, np.float32)),
        "bo": np.ascontiguousarray(np.asarray(L0_attn_bo, np.float32).reshape(1, D)),
        "cen": np.ascontiguousarray(np.asarray(L0_ffn_centroids, np.float32)),
        "ebi": np.ascontiguousarray(np.asarray(L0_ffn_expert_biases, np.float32).reshape(1, NE)),
        "wr": np.ascontiguousarray(np.asarray(L0_ffn_Wr, np.float32)),
        "brr": np.ascontiguousarray(np.asarray(L0_ffn_br, np.float32)),
        "ws": np.ascontiguousarray(np.asarray(L0_ffn_Ws, np.float32)),
        "bss": np.ascontiguousarray(np.asarray(L0_ffn_bs, np.float32)),
        "eye_in": np.eye(P, dtype=ml_dtypes.bfloat16),
    }
    nc = build_nc()
    in_maps = [dict(com, xin=np.ascontiguousarray(x[b])) for b in range(N_CORES)]
    res = run_bass_kernel_spmd(nc, in_maps, core_ids=list(range(N_CORES)))
    return np.stack([r["out"] for r in res.results], axis=0)


# version 18
# speedup vs baseline: 1.1182x; 1.1182x over previous
import numpy as np
import ml_dtypes
import concourse.bass as bass
import concourse.mybir as mybir
from concourse.bass_utils import run_bass_kernel_spmd

F32 = mybir.dt.float32
BF16 = mybir.dt.bfloat16
U32 = mybir.dt.uint32
AF = mybir.ActivationFunctionType
ALU = mybir.AluOpType
AX = mybir.AxisListType

N_CORES = 8
NTOK = 4096
D = 1024
P = 128
NT = NTOK // P
HEADS = 8
DH = 64
HID = 512
SCALE = DH ** -0.5
NE = 16
NSH = 2
NMAT = NE + NSH
NSTEP = 1 + NMAT
NQ = 4
QT = NT // NQ
EPS = 1e-5
DEBUG = False
PHASES = 'ALL'


class _Dummy:
    def __getattr__(self, k):
        return lambda *a, **kw: self

    def __call__(self, *a, **kw):
        return self


class Em:

    def __init__(self):
        self.vals = {}
        self.cnt = {}
        self.mode = "count"
        self.sem = {}

    def _bump(self, chan, label, k):
        c = self.cnt.get(chan, 0) + k
        self.cnt[chan] = c
        d = self.vals.setdefault(chan, {})
        if self.mode == "count":
            assert label not in d, f"dup label {chan}/{label}"
            d[label] = c
        else:
            assert d.get(label) == c, f"label drift {chan}/{label}: {d.get(label)} != {c}"
        return c

    def ms(self, eng, chan, label):
        self._bump(chan, label, 1)
        if self.mode == "emit":
            eng.drain().then_inc(self.sem[chan], 1)

    def pe_ms(self, instr, chan, label):
        self._bump(chan, label, 1)
        if self.mode == "emit":
            instr.then_inc(self.sem[chan], 1)

    def op_ms(self, instr, chan, label):
        self._bump(chan, label, 1)
        if self.mode == "emit":
            instr.then_inc(self.sem[chan], 1)

    def dma(self, eng, chan, label, out, in_, **kw):
        self._bump(chan, label, 16)
        if self.mode == "emit":
            eng.dma_start(out=out, in_=in_, **kw).then_inc(self.sem[chan], 16)

    def wait(self, eng, chan, label):
        if self.mode == "emit":
            eng.wait_ge(self.sem[chan], self.vals[chan][label])


def build_nc():
    nc = bass.Bass("TRN2", target_bir_lowering=False, debug=False)

    xin = nc.dram_tensor("xin", [NTOK, D], F32, kind="ExternalInput").ap()
    wq = nc.dram_tensor("wq", [D, HID], F32, kind="ExternalInput").ap()
    wk = nc.dram_tensor("wk", [D, HID], F32, kind="ExternalInput").ap()
    wv = nc.dram_tensor("wv", [D, HID], F32, kind="ExternalInput").ap()
    wo = nc.dram_tensor("wo", [HID, D], F32, kind="ExternalInput").ap()
    bo = nc.dram_tensor("bo", [1, D], F32, kind="ExternalInput").ap()
    cen = nc.dram_tensor("cen", [NE, D], F32, kind="ExternalInput").ap()
    ebi = nc.dram_tensor("ebi", [1, NE], F32, kind="ExternalInput").ap()
    wr = nc.dram_tensor("wr", [NE, D, D], F32, kind="ExternalInput").ap()
    brr = nc.dram_tensor("brr", [NE, D], F32, kind="ExternalInput").ap()
    ws = nc.dram_tensor("ws", [NSH, D, D], F32, kind="ExternalInput").ap()
    bss = nc.dram_tensor("bss", [NSH, D], F32, kind="ExternalInput").ap()
    eye_in = nc.dram_tensor("eye_in", [P, P], BF16, kind="ExternalInput").ap()
    out = nc.dram_tensor("out", [NTOK, D], F32, kind="ExternalOutput").ap()
    wbf = nc.dram_tensor("wbf", [NMAT, D, D], BF16, kind="Internal").ap()
    yscr = nc.dram_tensor("yscr", [NTOK, D], F32, kind="Internal").ap()
    if DEBUG:
        dbg_E = nc.dram_tensor("dbg_E", [P, 4 * 65], F32, kind="ExternalOutput").ap()
        dbg_ctx = nc.dram_tensor("dbg_ctx", [P, 4 * DH], F32, kind="ExternalOutput").ap()
        dbg_g = nc.dram_tensor("dbg_g", [P, NT * NE], F32, kind="ExternalOutput").ap()
        dbg_y = nc.dram_tensor("dbg_y", [NTOK, D], F32, kind="ExternalOutput").ap()
        dbg_uT = nc.dram_tensor("dbg_uT", [P, 8 * NTOK], F32, kind="ExternalOutput").ap()
        dbg_racc = nc.dram_tensor("dbg_racc", [P, QT * D], F32, kind="ExternalOutput").ap()
        dbg_gTx = nc.dram_tensor("dbg_gTx", [NMAT, NTOK], F32, kind="ExternalOutput").ap()

    qscr = nc.dram_tensor("qscr", [NTOK, HID], BF16, kind="Internal").ap()

    A = nc.alloc_sbuf_tensor
    eye = A("eye", [P, P], BF16)
    onesf = A("onesf", [4, P], F32)
    boRep = A("boRep", [P, D], F32)
    ebRep = A("ebRep", [P, NE], F32)
    ct_bf = A("ct_bf", [P, P], BF16)
    brx = A("brx", [NMAT, D], BF16)
    gTx = A("gTx", [NMAT, NTOK], BF16)
    uT = A("uT", [P, 8 * NTOK], BF16)
    gAll = A("gAll", [P, NT * NE], F32)
    halfc = A("halfc", [P, 1], F32)
    neg3e4 = A("neg3e4", [P, NE], F32)
    zero16 = A("zero16", [P, NE], F32)
    gb = A("gb", [P, NE], BF16)
    epsc = A("epsc", [P, 1], F32)
    eyeblk = A("eyeblk", [P, 4 * HID], BF16)
    E_sb = A("E_sb", [P, 4 * 65], F32)
    sInv = A("sInv", [P, 4], F32)
    ctx = A("ctx", [P, 4 * DH], BF16)

    def stat4(nm):
        return ([A(f"{nm}bn{i}", [P, 12], F32) for i in range(2)],
                [A(f"{nm}ag{i}", [P, 2], F32) for i in range(2)],
                [A(f"{nm}lv{i}", [P, 1], F32) for i in range(2)],
                [A(f"{nm}rs{i}", [P, 1], F32) for i in range(2)])

    bnA, agA, lvA, rsA = stat4("A")
    bn0, ag0, lv0, rs0 = stat4("B")
    ag4, lv4, rs4 = agA + ag0, lvA + lv0, rsA + rs0
    bn1, ag1, lv1, rs1 = stat4("C")
    bn2, ag2, lv2, rs2 = stat4("D")
    bnF, agF, lvF, rsF = stat4("E")

    qs = [A(f"qs{i}", [P, HEADS], F32) for i in range(2)]
    rq = [A(f"rq{i}", [P, HEADS], F32) for i in range(2)]
    scb = [A(f"scb{i}", [P, NE], F32) for i in range(2)]
    esb = [A(f"esb{i}", [P, NE], F32) for i in range(2)]
    ngm = [A(f"ngm{i}", [P, 1], F32) for i in range(2)]
    rsm = [A(f"rsm{i}", [P, 1], F32) for i in range(2)]
    mx1 = [A(f"mx1{i}", [P, 1], F32) for i in range(2)]
    m2v = [A(f"m2v{i}", [P, 1], F32) for i in range(2)]
    mk1 = [A(f"mk1{i}", [P, NE], U32) for i in range(2)]
    mk2 = [A(f"mk2{i}", [P, NE], U32) for i in range(2)]
    sm2 = [A(f"sm2{i}", [P, NE], F32) for i in range(2)]
    gsl = [A(f"gsl{i}", [P, NE], F32) for i in range(2)]
    ssm = [A(f"ssm{i}", [P, 1], F32) for i in range(2)]

    wqb = A("wqb", [P, 8 * HID], BF16)
    wob = A("wob", [P, 4 * D], BF16)
    xA = [A(f"xA{i}", [P, D], F32) for i in range(2)]
    hb = [A(f"hb{i}", [P, D], BF16) for i in range(2)]
    hT = [A(f"hT{i}", [P, D], BF16) for i in range(2)]
    qe = [A(f"qe{i}", [P, HID], F32) for i in range(2)]
    qn = [A(f"qn{i}", [P, HID], BF16) for i in range(2)]
    qTt = [A(f"qTt{i}", [P, HID], BF16) for i in range(2)]
    o2b = [A(f"o2b{i}", [P, HID], BF16) for i in range(2)]
    att = [A(f"att{i}", [P, D], F32) for i in range(2)]
    ub = [A(f"ub{i}", [P, D], BF16) for i in range(2)]
    wrt = [A(f"wrt{i}", [P, 8 * HID], BF16, is_transpose_dest=True) for i in range(2)]
    _rs, _re = nc.bump_sbuf(max(QT * D * 4, 32768), "left", align_bytes=128)
    racc = nc.alloc_sbuf_tensor_at("racc", [P, QT * D], F32, offset=_rs)
    wkb = nc.alloc_sbuf_tensor_at("wkb", [P, 8 * HID], BF16, offset=_rs)
    wvb = nc.alloc_sbuf_tensor_at("wvb", [P, 8 * HID], BF16, offset=_rs + 8 * HID * 2)
    xA4 = xA + [nc.alloc_sbuf_tensor_at(f"xA{i}", [P, D], F32,
                offset=_rs + 16384 + (i - 2) * 4096) for i in (2, 3)]
    hT4 = hT + [nc.alloc_sbuf_tensor_at(f"hT{i}", [P, D], BF16,
                offset=_rs + 24576 + (i - 2) * 2048) for i in (2, 3)]
    hb4 = hb + [nc.alloc_sbuf_tensor_at(f"hb{i}", [P, D], BF16,
                offset=_rs + 28672 + (i - 2) * 2048) for i in (2, 3)]
    _fs, _fe = nc.bump_sbuf(16384, "left", align_bytes=128)
    AT = nc.alloc_sbuf_tensor_at
    yld = [AT(f"yld{i}", [P, D], F32, offset=_fs + i * 4096) for i in range(2)]
    ftmp = [AT(f"ftmp{i}", [P, D], F32, offset=_fs + 8192 + i * 4096) for i in range(2)]
    bo_row = AT("bo_row", [1, D], F32, offset=_fs)
    cen_st = AT("cen_st", [NE, D], BF16, offset=_fs + 4096)
    eb_row = AT("eb_row", [1, NE], F32, offset=_fs + 6144)
    ek = [AT(f"ek{i}", [P, HID], BF16, offset=_fs + 6272 + i * 1024) for i in range(2)]
    vbq = [AT(f"vb{i}", [P, 8 * 65], BF16, offset=_fs + 8352 + i * 1056) for i in range(2)]

    PA = nc.alloc_psum_tensor
    pbT = PA("pbT", [P, 1024], BF16)
    pbT2 = PA("pbT2", [P, 1024], BF16)
    pb = [PA(f"pb{i}", [P, 512], F32) for i in range(6)]

    def a1_items(t):
        return [("a1_v_st", t), ("a1_a_rs", t), ("a1_v_hb", t),
                ("a1_p_trhb", t), ("a1_v_hT", t), ("a1_p_k", t),
                ("a1_a_ek", t), ("a1_p_v", t), ("a1_v_vb", t),
                ("a1_p_E", t), ("a1_v_Eacc", t),
                ("a1_p_q2", t), ("a1_a_eq", t), ("a1_v_qn2", t)]

    def zip_merge(la, lb):
        outl = []
        na, nb = len(la), len(lb)
        ia = ib = 0
        while ia < na or ib < nb:
            if ia < na:
                outl.append(la[ia]); ia += 1
            if ib < nb:
                outl.append(lb[ib]); ib += 1
        return outl

    flow_a1 = []
    HEADI = 7
    if NT % 4 == 0:
        for t in range(0, NT, 4):
            lists = [a1_items(t + d) for d in range(4)]
            n = len(lists[0])
            for r in range(HEADI):
                for lst in lists:
                    flow_a1.append(lst[r])
            for pp in range(2):
                for r in range(HEADI, n):
                    for lst in lists[2 * pp:2 * pp + 2]:
                        flow_a1.append(lst[r])
    else:
        for t in range(0, NT, 2):
            flow_a1 += zip_merge(a1_items(t), a1_items(t + 1))

    def a2_items(t):
        return [("a2_p_o2", t), ("a2_v_o2ev", t),
                ("a2_p_wo", t), ("a2_v_att", t), ("a2_a_rs1", t),
                ("a2_v_y", t), ("a2_a_rs2", t), ("a2_v_ub", t),
                ("a2_p_trub", t), ("a2_v_uT", t), ("a2_p_sc", t),
                ("a2_v_scev", t), ("a2_a_es", t), ("a2_v_g", t),
                ("a2_p_trg", t), ("a2_v_gT", t)]

    units = [(q, s, oh) for q in range(NQ) for s in range(1, NSTEP)
             for oh in range(2)]

    def unit_idx(q, s, oh):
        return (q * (NSTEP - 1) + (s - 1)) * 2 + oh

    def ffn_quarter(q):
        its = []
        for oh in range(2):
            for j in range(QT):
                its.append(("ffnp", q, 0, oh, j))
                its.append(("ffnv", q, 0, oh, j))
        for s in range(1, NSTEP):
            for oh in range(2):
                k = unit_idx(q, s, oh)
                if k + 2 < len(units):
                    its.append(("ffnl",) + units[k + 2])
                for j in range(QT):
                    its.append(("ffnp", q, s, oh, j))
                    its.append(("ffnv", q, s, oh, j))
        return its

    def fin_quarter(q):
        its = []
        for j in range(QT):
            its.append(("fina", q, j))
            its.append(("finact", q, j))
            if j >= 1:
                its.append(("finb", q, j - 1))
        its.append(("finb", q, QT - 1))
        return its

    def prop_merge(la, lb):
        outl = []
        na, nb = len(la), len(lb)
        if na == 0:
            return list(lb)
        ratio = nb / na
        ib = 0
        for ia in range(na):
            outl.append(la[ia])
            want = int(round((ia + 1) * ratio * 0.65))
            while ib < want:
                outl.append(lb[ib]); ib += 1
        outl.extend(lb[ib:])
        return outl

    flow_tail = []
    flow_tail.append(("ffnl",) + units[0])
    flow_tail.append(("ffnl",) + units[1])
    for w in range(NQ + 2):
        la = []
        if w < NQ:
            for t in range(QT * w, QT * w + QT, 2):
                la += zip_merge(a2_items(t), a2_items(t + 1))
        lb = []
        if 2 <= w:
            lb += fin_quarter(w - 2)
        if 1 <= w <= NQ:
            lb += ffn_quarter(w - 1)
        flow_tail += prop_merge(la, lb)

    pool = [
        [0, ("aA1", (NT - 2, "expk")), -1],
        [1, ("vA1", (NT - 1, "vbev")), -1],
        [2, ("vA1", (NT - 2, "Eacc")), -1],
        [3, ("vA1", (NT - 1, "Eacc")), -1],
        [4, ("aA1", (NT - 1, "expk")), -1],
        [5, ("aA1", (NT - 1, "expq1")), -1],
    ]
    _needs = {"a2_p_o2": ("vA2", "o2ev", 1),
              "a2_p_wo": ("vA2", "att", 2), "a2_p_sc": ("vA2", "scev", 1)}
    _evac_tag = {"a2_p_o2": "a2_v_o2ev",
                 "a2_p_wo": "a2_v_att", "a2_p_sc": "a2_v_scev"}
    evac_pos = {}
    for pos, it in enumerate(flow_tail):
        if it[0] in ("a2_v_o2ev", "a2_v_att", "a2_v_scev"):
            evac_pos[(it[0], it[1])] = pos
        elif it[0] == "ffnv":
            evac_pos[("ffnv", it[1], it[2], it[3], it[4])] = pos

    def pool_take(pos, n):
        got = []
        for _ in range(n):
            cands = [e for e in pool if e[2] < pos]
            assert cands, f"no reusable psum bank at flow pos {pos}"
            e = min(cands, key=lambda x: x[2])
            pool.remove(e)
            got.append(e)
        return got

    pend = {}
    flow2 = []
    for pos, it in enumerate(flow_tail):
        tag = it[0]
        if tag in _needs:
            t = it[1]
            chan, lbl, nbk = _needs[tag]
            got = pool_take(pos, nbk)
            ep = evac_pos[(_evac_tag[tag], t)]
            res = []
            for bk, pv, _e in got:
                res.append((bk, pv))
                pool.append([bk, (chan, (t, lbl)), ep])
            pend[(tag, t)] = res
            flow2.append(it + tuple(res))
        elif tag == "a2_v_o2ev":
            flow2.append(it + (pend[("a2_p_o2", it[1])][0][0],))
        elif tag == "a2_v_att":
            g = pend[("a2_p_wo", it[1])]
            flow2.append(it + (g[0][0], g[1][0]))
        elif tag == "a2_v_scev":
            flow2.append(it + (pend[("a2_p_sc", it[1])][0][0],))
        elif tag == "ffnp":
            q, s, oh, j = it[1], it[2], it[3], it[4]
            got = pool_take(pos, 1)[0]
            ep = evac_pos[("ffnv", q, s, oh, j)]
            pool.append([got[0], ("vF", (q, s, oh, j)), ep])
            pend[("f", q, s, oh, j)] = (got[0], got[1])
            flow2.append(it + ((got[0], got[1]),))
        elif tag == "ffnv":
            q, s, oh, j = it[1], it[2], it[3], it[4]
            flow2.append(it + (pend[("f", q, s, oh, j)][0],))
        else:
            flow2.append(it)
    flow_tail = flow2

    def h_sync(em, sync, it):
        tag = it[0]
        if tag == "a2_v_y":
            t = it[1]; i = t % 2
            em.wait(sync, "vA2", (t, "y"))
            em.dma(sync, f"dY{i}", t, yscr[t * P:(t + 1) * P, :], att[i][:])
            if t + 2 < NT:
                em.wait(sync, f"dQS{(t + 2) % 2}", t + 2)
                for jx in range(4):
                    em.dma(sync, f"dQT{(t + 2) % 2}", (t + 2, jx),
                           qTt[(t + 2) % 2][:, jx * P:(jx + 1) * P],
                           qscr[(t + 2) * P:(t + 3) * P, jx * P:(jx + 1) * P],
                           transpose=True)
            if t + 2 < NT:
                em.dma(sync, f"dX2{i}", t + 2, xA[i][:, 0:HID],
                       xin[(t + 2) * P:(t + 3) * P, 0:HID])
                em.dma(sync, f"dX2h{i}", t + 2, xA[i][:, HID:D],
                       xin[(t + 2) * P:(t + 3) * P, HID:D])

        elif tag == "ffnl":
            q, s, oh = it[1], it[2], it[3]
            k = unit_idx(q, s, oh)
            par = k % 2
            if k >= 2:
                pq, ps, po = units[k - 2]
                em.wait(sync, "pF", (pq, ps, po, QT - 1))
            em.wait(sync, "dWC", NMAT - 1)
            for kk in range(8):
                em.dma(sync, f"dWT{par}", (q, s, oh, kk),
                       wrt[par][:, kk * HID:(kk + 1) * HID],
                       wbf[s - 1, oh * HID:(oh + 1) * HID, kk * P:(kk + 1) * P],
                       transpose=True)
        elif tag == "fina":
            q, j = it[1], it[2]
            t = q * QT + j
            em.wait(sync, f"dY{t % 2}", t)
            if (q, j) not in ((0, 0), (0, 1)):
                pj = (q, j - 2) if j >= 2 else (q - 1, j + QT - 2)
                em.wait(sync, "vFF", (pj[0], pj[1], "fout"))
            em.dma(sync, f"dYL{j % 2}", (q, j), yld[j % 2][:],
                   yscr[t * P:(t + 1) * P, :])
        elif tag == "finb":
            q, j = it[1], it[2]
            t = q * QT + j
            em.wait(sync, "vFF", (q, j, "fout"))
            em.dma(sync, f"dOUT{j % 2}", (q, j), out[t * P:(t + 1) * P, :],
                   ftmp[j % 2][:])

    def ln_stats(v, bn, ag, src):
        v.bn_stats(bn[:, 0:6], src[:, 0:512])
        v.bn_stats(bn[:, 6:12], src[:, 512:1024])
        v.drain()
        v.bn_aggr(ag[:], bn[:, 0:12])

    def h_vector(em, v, it):
        tag = it[0]
        t = it[1] if len(it) > 1 else None
        i = t % 2 if isinstance(t, int) else None
        if tag == "a1_v_st":
            i4 = t % 4 if NT % 4 == 0 else t % 2
            em.wait(v, f"dX{i4}", t)
            em.wait(v, f"dXh{i4}", t)
            if t >= 4:
                em.wait(v, "aA1", (t - 4, "rs"))
            ln_stats(v, bnA[i], ag4[i4], xA4[i4])
            em.ms(v, "vA1", (t, "st"))
        elif tag == "a1_v_hb":
            i4 = t % 4 if NT % 4 == 0 else t % 2
            em.wait(v, "aA1", (t, "rs"))
            if t >= 4:
                em.wait(v, "pA1", (t - 4, "trhb"))
            v.tensor_scalar(hb4[i4][:], xA4[i4][:], ag4[i4][:, 0:1], rs4[i4][:],
                            ALU.subtract, ALU.mult)
            em.ms(v, "vA1", (t, "hb"))
        elif tag == "a1_v_hT":
            i4 = t % 4 if NT % 4 == 0 else t % 2
            em.wait(v, "pA1", (t, "trhb"))
            if t >= 4:
                em.wait(v, "pA1", (t - 4, "mm_q1"))
            pT = pbT if t % 2 == 0 else pbT2
            v.tensor_copy(hT4[i4][:].rearrange("p (k c) -> p k c", k=8),
                          pT[:].rearrange("p (k c) -> p k c", k=8))
            em.ms(v, "vA1", (t, "hTev"))
        elif tag == "a1_v_vb":
            em.wait(v, "pA1", (t, "mm_v"))
            if t >= 2:
                em.wait(v, "pA1", (t - 2, "mm_E"))
            v.tensor_copy(vbq[i][:].rearrange("p (h c) -> p h c", c=65)[:, :, 0:64],
                          pb[1][:].rearrange("p (h c) -> p h c", c=64))
            em.ms(v, "vA1", (t, "vbev"))
        elif tag == "a1_v_Eacc":
            em.wait(v, "pA1", (t, "mm_E"))
            v.drain()
            ii = v.tensor_tensor(E_sb[:], E_sb[:], pb[2 + i][:, 0:260], ALU.add)
            em.op_ms(ii, "vA1", (t, "Eacc"))
        elif tag == "a1_v_qn2":
            em.wait(v, "aA1", (t, "expq1"))
            if t >= 2:
                em.wait(v, f"dQS{t % 2}", t - 2)
            v.reduce_sum(qs[i][:].rearrange("p (h o) -> p h o", o=1),
                         qe[i][:].rearrange("p (h c) -> p h c", h=HEADS),
                         axis=AX.X)
            v.drain()
            v.reciprocal(rq[i][:], qs[i][:])
            v.drain()
            for h in range(HEADS):
                v.tensor_scalar(qn[i][:, h * DH:(h + 1) * DH],
                                qe[i][:, h * DH:(h + 1) * DH],
                                rq[i][:, h:h + 1], SCALE, ALU.mult, ALU.mult)
            em.ms(v, "vA1", (t, "qn2"))
        elif tag == "mid":
            v.drain()
            for h in range(HEADS):
                pbase, fo = (h % 2) * 64, (h // 2) * 65
                v.reciprocal(sInv[pbase:pbase + 64, h // 2:h // 2 + 1],
                             E_sb[pbase:pbase + 64, fo + 64:fo + 65])
            v.drain()
            for h in range(HEADS):
                pbase, fo = (h % 2) * 64, (h // 2) * 65
                v.tensor_scalar(ctx[pbase:pbase + 64, (h // 2) * DH:(h // 2 + 1) * DH],
                                E_sb[pbase:pbase + 64, fo:fo + 64],
                                sInv[pbase:pbase + 64, h // 2:h // 2 + 1],
                                None, ALU.mult)
            em.ms(v, "vMID", "ctx")
        elif tag == "a2_v_o2ev":
            flow2.append(it + (pend[("a2_p_o2", it[1])][0][0],))
        elif tag == "a2_v_att":
            g = pend[("a2_p_wo", it[1])]
            flow2.append(it + (g[0][0], g[1][0]))
        elif tag == "a2_v_scev":
            flow2.append(it + (pend[("a2_p_sc", it[1])][0][0],))
        elif tag == "ffnp":
            q, s, oh, j = it[1], it[2], it[3], it[4]
            got = pool_take(pos, 1)[0]
            ep = evac_pos[("ffnv", q, s, oh, j)]
            pool.append([got[0], ("vF", (q, s, oh, j)), ep])
            pend[("f", q, s, oh, j)] = (got[0], got[1])
            flow2.append(it + ((got[0], got[1]),))
        elif tag == "ffnv":
            q, s, oh, j = it[1], it[2], it[3], it[4]
            flow2.append(it + (pend[("f", q, s, oh, j)][0],))
        else:
            flow2.append(it)
    flow_tail = flow2

    def h_sync(em, sync, it):
        tag = it[0]
        if tag == "a2_v_y":
            t = it[1]; i = t % 2
            em.wait(sync, "vA2", (t, "y"))
            em.dma(sync, f"dY{i}", t, yscr[t * P:(t + 1) * P, :], att[i][:])
            if t + 2 < NT:
                em.wait(sync, f"dQS{(t + 2) % 2}", t + 2)
                for jx in range(4):
                    em.dma(sync, f"dQT{(t + 2) % 2}", (t + 2, jx),
                           qTt[(t + 2) % 2][:, jx * P:(jx + 1) * P],
                           qscr[(t + 2) * P:(t + 3) * P, jx * P:(jx + 1) * P],
                           transpose=True)
            if t + 2 < NT:
                em.dma(sync, f"dX2{i}", t + 2, xA[i][:, 0:HID],
                       xin[(t + 2) * P:(t + 3) * P, 0:HID])
                em.dma(sync, f"dX2h{i}", t + 2, xA[i][:, HID:D],
                       xin[(t + 2) * P:(t + 3) * P, HID:D])

        elif tag == "ffnl":
            q, s, oh = it[1], it[2], it[3]
            k = unit_idx(q, s, oh)
            par = k % 2
            if k >= 2:
                pq, ps, po = units[k - 2]
                em.wait(sync, "pF", (pq, ps, po, QT - 1))
            em.wait(sync, "dWC", NMAT - 1)
            for kk in range(8):
                em.dma(sync, f"dWT{par}", (q, s, oh, kk),
                       wrt[par][:, kk * HID:(kk + 1) * HID],
                       wbf[s - 1, oh * HID:(oh + 1) * HID, kk * P:(kk + 1) * P],
                       transpose=True)
        elif tag == "fina":
            q, j = it[1], it[2]
            t = q * QT + j
            em.wait(sync, f"dY{t % 2}", t)
            if (q, j) not in ((0, 0), (0, 1)):
                pj = (q, j - 2) if j >= 2 else (q - 1, j + QT - 2)
                em.wait(sync, "vFF", (pj[0], pj[1], "fout"))
            em.dma(sync, f"dYL{j % 2}", (q, j), yld[j % 2][:],
                   yscr[t * P:(t + 1) * P, :])
        elif tag == "finb":
            q, j = it[1], it[2]
            t = q * QT + j
            em.wait(sync, "vFF", (q, j, "fout"))
            em.dma(sync, f"dOUT{j % 2}", (q, j), out[t * P:(t + 1) * P, :],
                   ftmp[j % 2][:])

    def ln_stats(v, bn, ag, src):
        v.bn_stats(bn[:, 0:6], src[:, 0:512])
        v.bn_stats(bn[:, 6:12], src[:, 512:1024])
        v.drain()
        v.bn_aggr(ag[:], bn[:, 0:12])

    def h_vector(em, v, it):
        tag = it[0]
        t = it[1] if len(it) > 1 else None
        i = t % 2 if isinstance(t, int) else None
        if tag == "a1_v_st":
            i4 = t % 4 if NT % 4 == 0 else t % 2
            em.wait(v, f"dX{i4}", t)
            em.wait(v, f"dXh{i4}", t)
            if t >= 4:
                em.wait(v, "aA1", (t - 4, "rs"))
            ln_stats(v, bnA[i], ag4[i4], xA4[i4])
            em.ms(v, "vA1", (t, "st"))
        elif tag == "a1_v_hb":
            i4 = t % 4 if NT % 4 == 0 else t % 2
            em.wait(v, "aA1", (t, "rs"))
            if t >= 4:
                em.wait(v, "pA1", (t - 4, "trhb"))
            v.tensor_scalar(hb4[i4][:], xA4[i4][:], ag4[i4][:, 0:1], rs4[i4][:],
                            ALU.subtract, ALU.mult)
            em.ms(v, "vA1", (t, "hb"))
        elif tag == "a1_v_hT":
            i4 = t % 4 if NT % 4 == 0 else t % 2
            em.wait(v, "pA1", (t, "trhb"))
            if t >= 4:
                em.wait(v, "pA1", (t - 4, "mm_q1"))
            pT = pbT if t % 2 == 0 else pbT2
            v.tensor_copy(hT4[i4][:].rearrange("p (k c) -> p k c", k=8),
                          pT[:].rearrange("p (k c) -> p k c", k=8))
            em.ms(v, "vA1", (t, "hTev"))
        elif tag == "a1_v_vb":
            em.wait(v, "pA1", (t, "mm_v"))
            if t >= 2:
                em.wait(v, "pA1", (t - 2, "mm_E"))
            v.tensor_copy(vbq[i][:].rearrange("p (h c) -> p h c", c=65)[:, :, 0:64],
                          pb[1][:].rearrange("p (h c) -> p h c", c=64))
            em.ms(v, "vA1", (t, "vbev"))
        elif tag == "a1_v_Eacc":
            em.wait(v, "pA1", (t, "mm_E"))
            v.drain()
            ii = v.tensor_tensor(E_sb[:], E_sb[:], pb[2 + i][:, 0:260], ALU.add)
            em.op_ms(ii, "vA1", (t, "Eacc"))
        elif tag == "a1_v_qn2":
            em.wait(v, "aA1", (t, "expq1"))
            if t >= 2:
                em.wait(v, f"dQS{t % 2}", t - 2)
            v.reduce_sum(qs[i][:].rearrange("p (h o) -> p h o", o=1),
                         qe[i][:].rearrange("p (h c) -> p h c", h=HEADS),
                         axis=AX.X)
            v.drain()
            v.reciprocal(rq[i][:], qs[i][:])
            v.drain()
            for h in range(HEADS):
                v.tensor_scalar(qn[i][:, h * DH:(h + 1) * DH],
                                qe[i][:, h * DH:(h + 1) * DH],
                                rq[i][:, h:h + 1], SCALE, ALU.mult, ALU.mult)
            em.ms(v, "vA1", (t, "qn2"))
        elif tag == "mid":
            v.drain()
            for h in range(HEADS):
                pbase, fo = (h % 2) * 64, (h // 2) * 65
                v.reciprocal(sInv[pbase:pbase + 64, h // 2:h // 2 + 1],
                             E_sb[pbase:pbase + 64, fo + 64:fo + 65])
            v.drain()
            for h in range(HEADS):
                pbase, fo = (h % 2) * 64, (h // 2) * 65
                v.tensor_scalar(ctx[pbase:pbase + 64, (h // 2) * DH:(h // 2 + 1) * DH],
                                E_sb[pbase:pbase + 64, fo:fo + 64],
                                sInv[pbase:pbase + 64, h // 2:h // 2 + 1],
                                None, ALU.mult)
            em.ms(v, "vMID", "ctx")
        elif tag == "a2_v_o2ev":
            flow2.append(it + (pend[("a2_p_o2", it[1])][0][0],))
        elif tag == "a2_v_att":
            g = pend[("a2_p_wo", it[1])]
            flow2.append(it + (g[0][0], g[1][0]))
        elif tag == "a2_v_scev":
            flow2.append(it + (pend[("a2_p_sc", it[1])][0][0],))
        elif tag == "ffnp":
            q, s, oh, j = it[1], it[2], it[3], it[4]
            got = pool_take(pos, 1)[0]
            ep = evac_pos[("ffnv", q, s, oh, j)]
            pool.append([got[0], ("vF", (q, s, oh, j)), ep])
            pend[("f", q, s, oh, j)] = (got[0], got[1])
            flow2.append(it + ((got[0], got[1]),))
        elif tag == "ffnv":
            q, s, oh, j = it[1], it[2], it[3], it[4]
            flow2.append(it + (pend[("f", q, s, oh, j)][0],))
        else:
            flow2.append(it)
    flow_tail = flow2

    def h_sync(em, sync, it):
        tag = it[0]
        if tag == "a2_v_y":
            t = it[1]; i = t % 2
            em.wait(sync, "vA2", (t, "y"))
            em.dma(sync, f"dY{i}", t, yscr[t * P:(t + 1) * P, :], att[i][:])
            if t + 2 < NT:
                em.wait(sync, f"dQS{(t + 2) % 2}", t + 2)
                for jx in range(4):
                    em.dma(sync, f"dQT{(t + 2) % 2}", (t + 2, jx),
                           qTt[(t + 2) % 2][:, jx * P:(jx + 1) * P],
                           qscr[(t + 2) * P:(t + 3) * P, jx * P:(jx + 1) * P],
                           transpose=True)
            if t + 2 < NT:
                em.dma(sync, f"dX2{i}", t + 2, xA[i][:, 0:HID],
                       xin[(t + 2) * P:(t + 3) * P, 0:HID])
                em.dma(sync, f"dX2h{i}", t + 2, xA[i][:, HID:D],
                       xin[(t + 2) * P:(t + 3) * P, HID:D])

        elif tag == "ffnl":
            q, s, oh = it[1], it[2], it[3]
            k = unit_idx(q, s, oh)
            par = k % 2
            if k >= 2:
                pq, ps, po = units[k - 2]
                em.wait(sync, "pF", (pq, ps, po, QT - 1))
            em.wait(sync, "dWC", NMAT - 1)
            for kk in range(8):
                em.dma(sync, f"dWT{par}", (q, s, oh, kk),
                       wrt[par][:, kk * HID:(kk + 1) * HID],
                       wbf[s - 1, oh * HID:(oh + 1) * HID, kk * P:(kk + 1) * P],
                       transpose=True)
        elif tag == "fina":
            q, j = it[1], it[2]
            t = q * QT + j
            em.wait(sync, f"dY{t % 2}", t)
            if (q, j) not in ((0, 0), (0, 1)):
                pj = (q, j - 2) if j >= 2 else (q - 1, j + QT - 2)
                em.wait(sync, "vFF", (pj[0], pj[1], "fout"))
            em.dma(sync, f"dYL{j % 2}", (q, j), yld[j % 2][:],
                   yscr[t * P:(t + 1) * P, :])
        elif tag == "finb":
            q, j = it[1], it[2]
            t = q * QT + j
            em.wait(sync, "vFF", (q, j, "fout"))
            em.dma(sync, f"dOUT{j % 2}", (q, j), out[t * P:(t + 1) * P, :],
                   ftmp[j % 2][:])

    def ln_stats(v, bn, ag, src):
        v.bn_stats(bn[:, 0:6], src[:, 0:512])
        v.bn_stats(bn[:, 6:12], src[:, 512:1024])
        v.drain()
        v.bn_aggr(ag[:], bn[:, 0:12])

    def h_vector(em, v, it):
        tag = it[0]
        t = it[1] if len(it) > 1 else None
        i = t % 2 if isinstance(t, int) else None
        if tag == "a1_v_st":
            i4 = t % 4 if NT % 4 == 0 else t % 2
            em.wait(v, f"dX{i4}", t)
            em.wait(v, f"dXh{i4}", t)
            if t >= 4:
                em.wait(v, "aA1", (t - 4, "rs"))
            ln_stats(v, bnA[i], ag4[i4], xA4[i4])
            em.ms(v, "vA1", (t, "st"))
        elif tag == "a1_v_hb":
            i4 = t % 4 if NT % 4 == 0 else t % 2
            em.wait(v, "aA1", (t, "rs"))
            if t >= 4:
                em.wait(v, "pA1", (t - 4, "trhb"))
            v.tensor_scalar(hb4[i4][:], xA4[i4][:], ag4[i4][:, 0:1], rs4[i4][:],
                            ALU.subtract, ALU.mult)
            em.ms(v, "vA1", (t, "hb"))
        elif tag == "a1_v_hT":
            i4 = t % 4 if NT % 4 == 0 else t % 2
            em.wait(v, "pA1", (t, "trhb"))
            if t >= 4:
                em.wait(v, "pA1", (t - 4, "mm_q1"))
            pT = pbT if t % 2 == 0 else pbT2
            v.tensor_copy(hT4[i4][:].rearrange("p (k c) -> p k c", k=8),
                          pT[:].rearrange("p (k c) -> p k c", k=8))
            em.ms(v, "vA1", (t, "hTev"))
        elif tag == "a1_v_vb":
            em.wait(v, "pA1", (t, "mm_v"))
            if t >= 2:
                em.wait(v, "pA1", (t - 2, "mm_E"))
            v.tensor_copy(vbq[i][:].rearrange("p (h c) -> p h c", c=65)[:, :, 0:64],
                          pb[1][:].rearrange("p (h c) -> p h c", c=64))
            em.ms(v, "vA1", (t, "vbev"))
        elif tag == "a1_v_Eacc":
            em.wait(v, "pA1", (t, "mm_E"))
            v.drain()
            ii = v.tensor_tensor(E_sb[:], E_sb[:], pb[2 + i][:, 0:260], ALU.add)
            em.op_ms(ii, "vA1", (t, "Eacc"))
        elif tag == "a1_v_qn2":
            em.wait(v, "aA1", (t, "expq1"))
            if t >= 2:
                em.wait(v, f"dQS{t % 2}", t - 2)
            v.reduce_sum(qs[i][:].rearrange("p (h o) -> p h o", o=1),
                         qe[i][:].rearrange("p (h c) -> p h c", h=HEADS),
                         axis=AX.X)
            v.drain()
            v.reciprocal(rq[i][:], qs[i][:])
            v.drain()
            for h in range(HEADS):
                v.tensor_scalar(qn[i][:, h * DH:(h + 1) * DH],
                                qe[i][:, h * DH:(h + 1) * DH],
                                rq[i][:, h:h + 1], SCALE, ALU.mult, ALU.mult)
            em.ms(v, "vA1", (t, "qn2"))
        elif tag == "mid":
            v.drain()
            for h in range(HEADS):
                pbase, fo = (h % 2) * 64, (h // 2) * 65
                v.reciprocal(sInv[pbase:pbase + 64, h // 2:h // 2 + 1],
                             E_sb[pbase:pbase + 64, fo + 64:fo + 65])
            v.drain()
            for h in range(HEADS):
                pbase, fo = (h % 2) * 64, (h // 2) * 65
                v.tensor_scalar(ctx[pbase:pbase + 64, (h // 2) * DH:(h // 2 + 1) * DH],
                                E_sb[pbase:pbase + 64, fo:fo + 64],
                                sInv[pbase:pbase + 64, h // 2:h // 2 + 1],
                                None, ALU.mult)
            em.ms(v, "vMID", "ctx")
        elif tag == "a2_v_o2ev":
            flow2.append(it + (pend[("a2_p_o2", it[1])][0][0],))
        elif tag == "a2_v_att":
            g = pend[("a2_p_wo", it[1])]
            flow2.append(it + (g[0][0], g[1][0]))
        elif tag == "a2_v_scev":
            flow2.append(it + (pend[("a2_p_sc", it[1])][0][0],))
        elif tag == "ffnp":
            q, s, oh, j = it[1], it[2], it[3], it[4]
            got = pool_take(pos, 1)[0]
            ep = evac_pos[("ffnv", q, s, oh, j)]
            pool.append([got[0], ("vF", (q, s, oh, j)), ep])
            pend[("f", q, s, oh, j)] = (got[0], got[1])
            flow2.append(it + ((got[0], got[1]),))
        elif tag == "ffnv":
            q, s, oh, j = it[1], it[2], it[3], it[4]
            flow2.append(it + (pend[("f", q, s, oh, j)][0],))
        else:
            flow2.append(it)
    flow_tail = flow2

    def h_sync(em, sync, it):
        tag = it[0]
        if tag == "a2_v_y":
            t = it[1]; i = t % 2
            em.wait(sync, "vA2", (t, "y"))
            em.dma(sync, f"dY{i}", t, yscr[t * P:(t + 1) * P, :], att[i][:])
            if t + 2 < NT:
                em.wait(sync, f"dQS{(t + 2) % 2}", t + 2)
                for jx in range(4):
                    em.dma(sync, f"dQT{(t + 2) % 2}", (t + 2, jx),
                           qTt[(t + 2) % 2][:, jx * P:(jx + 1) * P],
                           qscr[(t + 2) * P:(t + 3) * P, jx * P:(jx + 1) * P],
                           transpose=True)
            if t + 2 < NT:
                em.dma(sync, f"dX2{i}", t + 2, xA[i][:, 0:HID],
                       xin[(t + 2) * P:(t + 3) * P, 0:HID])
                em.dma(sync, f"dX2h{i}", t + 2, xA[i][:, HID:D],
                       xin[(t + 2) * P:(t + 3) * P, HID:D])

        elif tag == "ffnl":
            q, s, oh = it[1], it[2], it[3]
            k = unit_idx(q, s, oh)
            par = k % 2
            if k >= 2:
                pq, ps, po = units[k - 2]
                em.wait(sync, "pF", (pq, ps, po, QT - 1))
            em.wait(sync, "dWC", NMAT - 1)
            for kk in range(8):
                em.dma(sync, f"dWT{par}", (q, s, oh, kk),
                       wrt[par][:, kk * HID:(kk + 1) * HID],
                       wbf[s - 1, oh * HID:(oh + 1) * HID, kk * P:(kk + 1) * P],
                       transpose=True)
        elif tag == "fina":
            q, j = it[1], it[2]
            t = q * QT + j
            em.wait(sync, f"dY{t % 2}", t)
            if (q, j) not in ((0, 0), (0, 1)):
                pj = (q, j - 2) if j >= 2 else (q - 1, j + QT - 2)
                em.wait(sync, "vFF", (pj[0], pj[1], "fout"))
            em.dma(sync, f"dYL{j % 2}", (q, j), yld[j % 2][:],
                   yscr[t * P:(t + 1) * P, :])
        elif tag == "finb":
            q, j = it[1], it[2]
            t = q * QT + j
            em.wait(sync, "vFF", (q, j, "fout"))
            em.dma(sync, f"dOUT{j % 2}", (q, j), out[t * P:(t + 1) * P, :],
                   ftmp[j % 2][:])

    def ln_stats(v, bn, ag, src):
        v.bn_stats(bn[:, 0:6], src[:, 0:512])
        v.bn_stats(bn[:, 6:12], src[:, 512:1024])
        v.drain()
        v.bn_aggr(ag[:], bn[:, 0:12])

    def h_vector(em, v, it):
        tag = it[0]
        t = it[1] if len(it) > 1 else None
        i = t % 2 if isinstance(t, int) else None
        if tag == "a1_v_st":
            i4 = t % 4 if NT % 4 == 0 else t % 2
            em.wait(v, f"dX{i4}", t)
            em.wait(v, f"dXh{i4}", t)
            if t >= 4:
                em.wait(v, "aA1", (t - 4, "rs"))
            ln_stats(v, bnA[i], ag4[i4], xA4[i4])
            em.ms(v, "vA1", (t, "st"))
        elif tag == "a1_v_hb":
            i4 = t % 4 if NT % 4 == 0 else t % 2
            em.wait(v, "aA1", (t, "rs"))
            if t >= 4:
                em.wait(v, "pA1", (t - 4, "trhb"))
            v.tensor_scalar(hb4[i4][:], xA4[i4][:], ag4[i4][:, 0:1], rs4[i4][:],
                            ALU.subtract, ALU.mult)
            em.ms(v, "vA1", (t, "hb"))
        elif tag == "a1_v_hT":
            i4 = t % 4 if NT % 4 == 0 else t % 2
            em.wait(v, "pA1", (t, "trhb"))
            if t >= 4:
                em.wait(v, "pA1", (t - 4, "mm_q1"))
            pT = pbT if t % 2 == 0 else pbT2
            v.tensor_copy(hT4[i4][:].rearrange("p (k c) -> p k c", k=8),
                          pT[:].rearrange("p (k c) -> p k c", k=8))
            em.ms(v, "vA1", (t, "hTev"))
        elif tag == "a1_v_vb":
            em.wait(v, "pA1", (t, "mm_v"))
            if t >= 2:
                em.wait(v, "pA1", (t - 2, "mm_E"))
            v.tensor_copy(vbq[i][:].rearrange("p (h c) -> p h c", c=65)[:, :, 0:64],
                          pb[1][:].rearrange("p (h c) -> p h c", c=64))
            em.ms(v, "vA1", (t, "vbev"))
        elif tag == "a1_v_Eacc":
            em.wait(v, "pA1", (t, "mm_E"))
            v.drain()
            ii = v.tensor_tensor(E_sb[:], E_sb[:], pb[2 + i][:, 0:260], ALU.add)
            em.op_ms(ii, "vA1", (t, "Eacc"))
        elif tag == "a1_v_qn2":
            em.wait(v, "aA1", (t, "expq1"))
            if t >= 2:
                em.wait(v, f"dQS{t % 2}", t - 2)
            v.reduce_sum(qs[i][:].rearrange("p (h o) -> p h o", o=1),
                         qe[i][:].rearrange("p (h c) -> p h c", h=HEADS),
                         axis=AX.X)
            v.drain()
            v.reciprocal(rq[i][:], qs[i][:])
            v.drain()
            for h in range(HEADS):
                v.tensor_scalar(qn[i][:, h * DH:(h + 1) * DH],
                                qe[i][:, h * DH:(h + 1) * DH],
                                rq[i][:, h:h + 1], SCALE, ALU.mult, ALU.mult)
            em.ms(v, "vA1", (t, "qn2"))
        elif tag == "mid":
            v.drain()
            for h in range(HEADS):
                pbase, fo = (h % 2) * 64, (h // 2) * 65
                v.reciprocal(sInv[pbase:pbase + 64, h // 2:h // 2 + 1],
                             E_sb[pbase:pbase + 64, fo + 64:fo + 65])
            v.drain()
            for h in range(HEADS):
                pbase, fo = (h % 2) * 64, (h // 2) * 65
                v.tensor_scalar(ctx[pbase:pbase + 64, (h // 2) * DH:(h // 2 + 1) * DH],
                                E_sb[pbase:pbase + 64, fo:fo + 64],
                                sInv[pbase:pbase + 64, h // 2:h // 2 + 1],
                                None, ALU.mult)
            em.ms(v, "vMID", "ctx")
        elif tag == "a2_v_qn":
            em.wait(v, "aA2", (t, "expq"))
            v.reduce_sum(qs[i][:].rearrange("p (h o) -> p h o", o=1),
                         qe[i][:].rearrange("p (h c) -> p h c", h=HEADS),
                         axis=AX.X)
            v.drain()
            v.reciprocal(rq[i][:], qs[i][:])
            v.drain()
            if t >= 2:
                em.wait(v, "pA2", (t - 2, "trqn"))
            for h in range(HEADS):
                v.tensor_scalar(qn[i][:, h * DH:(h + 1) * DH],
                                qe[i][:, h * DH:(h + 1) * DH],
                                rq[i][:, h:h + 1], SCALE, ALU.mult, ALU.mult)
            em.ms(v, "vA2", (t, "qn"))
        elif tag == "a2_v_qT":
            em.wait(v, "pA2", (t, "trqn"))
            if t >= 2:
                em.wait(v, "pA2", (t - 2, "o2"))
            pT = pbT if t % 2 == 0 else pbT2
            v.tensor_copy(qTt[i][:], pT[:, 0:512])
            em.ms(v, "vA2", (t, "qTev"))
        elif tag == "a2_v_o2ev":
            em.wait(v, "pA2", (t, "o2"))
            if t >= 2:
                em.wait(v, "pA2", (t - 2, "wo"))
            v.tensor_copy(o2b[i][:], pb[it[2]][:])
            em.ms(v, "vA2", (t, "o2ev"))
        elif tag == "a2_v_att":
            em.wait(v, "pA2", (t, "wo0"))
            if t == 0:
                em.wait(v, "vRP", "boeb")
            if t >= 2:
                em.wait(v, "vA2", (t - 2, "ub"))
                em.wait(v, f"dY{i}", t - 2)
            v.tensor_tensor(att[i][:, 0:512], pb[it[2]][:], boRep[:, 0:512], ALU.add)
            em.wait(v, "pA2", (t, "wo"))
            ii = v.tensor_tensor(att[i][:, 512:1024], pb[it[3]][:],
                                 boRep[:, 512:1024], ALU.add)
            em.op_ms(ii, "vA2", (t, "att"))
            v.drain()
            if t >= 2:
                em.wait(v, "aA2", (t - 2, "rs1"))
            ln_stats(v, bn1[i], ag1[i], att[i])
            em.ms(v, "vA2", (t, "st1"))
        elif tag == "a2_v_y":
            em.wait(v, "aA2", (t, "rs1"))
            v.tensor_scalar(att[i][:], att[i][:], ag1[i][:, 0:1], rs1[i][:],
                            ALU.subtract, ALU.mult)
            v.drain()
            em.wait(v, f"dX2{i}", t)
            em.wait(v, f"dX2h{i}", t)
            v.tensor_tensor(att[i][:], att[i][:], xA[i][:], ALU.add)
            em.ms(v, "vA2", (t, "y"))
            if t >= 2:
                em.wait(v, "aA2", (t - 2, "rs2"))
            ln_stats(v, bn2[i], ag2[i], att[i])
            em.ms(v, "vA2", (t, "st2"))
        elif tag == "a2_v_ub":
            em.wait(v, "aA2", (t, "rs2"))
            if t >= 2:
                em.wait(v, "pA2", (t - 2, "trub"))
            v.tensor_scalar(ub[i][:], att[i][:], ag2[i][:, 0:1], rs2[i][:],
                            ALU.subtract, ALU.mult)
            em.ms(v, "vA2", (t, "ub"))
        elif tag == "a2_v_uT":
            em.wait(v, "pA2", (t, "trub"))
            pT = pbT if t % 2 == 0 else pbT2
            v.tensor_copy(uT[:].rearrange("p (k n) -> p k n", k=8)[:, :, t * P:(t + 1) * P],
                          pT[:].rearrange("p (k n) -> p k n", k=8))
            em.ms(v, "vA2", (t, "uTev"))
        elif tag == "a2_v_scev":
            em.wait(v, "pA2", (t, "sc"))
            if t >= 2:
                em.wait(v, "aA2", (t - 2, "es"))
            v.tensor_tensor(scb[i][:], pb[it[2]][:, 0:NE], ebRep[:], ALU.add)
            v.drain()
            v.reduce_max(ngm[i][:], scb[i][:], axis=AX.X, negate=True)
            em.ms(v, "vA2", (t, "scev"))
        elif tag == "a2_v_g":
            em.wait(v, "aA2", (t, "es"))
            v.reduce_sum(ssm[i][:], esb[i][:], axis=AX.X)
            v.drain()
            v.reciprocal(rsm[i][:], ssm[i][:])
            v.reduce_max(mx1[i][:], esb[i][:], axis=AX.X)
            v.drain()
            v.tensor_scalar(mk1[i][:], esb[i][:], mx1[i][:], None, ALU.is_ge)
            v.drain()
            v.select(sm2[i][:], mk1[i][:], neg3e4[:], esb[i][:], add_drain=True)
            v.drain()
            v.reduce_max(m2v[i][:], sm2[i][:], axis=AX.X)
            v.drain()
            v.tensor_scalar(mk2[i][:], esb[i][:], m2v[i][:], None, ALU.is_ge)
            v.drain()
            v.select(gsl[i][:], mk2[i][:], esb[i][:], zero16[:], add_drain=True)
            v.drain()
            v.tensor_scalar(gAll[:, t * NE:(t + 1) * NE], gsl[i][:],
                            rsm[i][:], None, ALU.mult)
            v.drain()
            if t >= 1:
                em.wait(v, "pA2", (t - 1, "trg"))
            v.tensor_copy(gb[:], gAll[:, t * NE:(t + 1) * NE])
            em.ms(v, "vA2", (t, "g"))
        elif tag == "a2_v_gT":
            em.wait(v, "pA2", (t, "trg"))
            pT = pbT if t % 2 == 0 else pbT2
            v.tensor_copy(gTx[0:NE, t * P:(t + 1) * P], pT[0:NE, 0:P])
            em.ms(v, "vA2", (t, "gTev"))
        elif tag == "ffnv":
            q, s, oh, j, bk = it[1], it[2], it[3], it[4], it[5]
            tt = q * QT + j
            em.wait(v, "pF", (q, s, oh, j))
            if q == 0 and s == 0:
                em.wait(v, "pA1", (NT - 1, "mm_E"))
            if s >= 1:
                em.wait(v, "vF", (q, s - 1, oh, j))
            elif q >= 1:
                em.wait(v, "vFF", (q - 1, j, "fout"))
            rsl = racc[:, j * D + oh * HID:j * D + (oh + 1) * HID]
            if s == 0:
                ii = v.tensor_copy(rsl, pb[bk][:])
            else:
                e = s - 1
                gcol = (gAll[:, tt * NE + e:tt * NE + e + 1]
                        if e < NE else halfc[:])
                ii = v.scalar_tensor_tensor(rsl, pb[bk][:], gcol, rsl,
                                            ALU.mult, ALU.add)
            em.op_ms(ii, "vF", (q, s, oh, j))
        elif tag == "fina":
            q, j = it[1], it[2]
            jp = j % 2
            em.wait(v, "pF", (q, NSTEP - 1, 1, j))
            if q == 0 and j < 2:
                em.wait(v, "pA1", (NT - 1, "mm_E"))
            if (q, j) not in ((0, 0), (0, 1)):
                pj = (q, j - 2) if j >= 2 else (q - 1, j + QT - 2)
                em.wait(v, "aFF", (pj[0], pj[1], "rsF"))
            v.drain()
            ln_stats(v, bnF[jp], agF[jp], racc[:, j * D:(j + 1) * D])
            em.ms(v, "vFF", (q, j, "stF"))
        elif tag == "finb":
            q, j = it[1], it[2]
            jp = j % 2
            em.wait(v, "aFF", (q, j, "rsF"))
            em.wait(v, f"dYL{jp}", (q, j))
            if (q, j) not in ((0, 0), (0, 1)):
                pj = (q, j - 2) if j >= 2 else (q - 1, j + QT - 2)
                em.wait(v, f"dOUT{jp}", pj)
            v.tensor_scalar(ftmp[jp][:], racc[:, j * D:(j + 1) * D],
                            agF[jp][:, 0:1], rsF[jp][:], ALU.subtract, ALU.mult)
            v.drain()
            v.tensor_tensor(ftmp[jp][:], ftmp[jp][:], yld[jp][:], ALU.add)
            em.ms(v, "vFF", (q, j, "fout"))
        elif tag == "setup_v":
            v.memset(onesf[:], 1.0)
            v.memset(neg3e4[:], -30000.0)
            v.memset(zero16[:], 0.0)
            v.memset(halfc[:], 0.5)
            v.memset(gTx[:], 0.5)
            v.memset(vbq[0][:], 1.0)
            v.memset(vbq[1][:], 1.0)
            v.memset(E_sb[:], 0.0)
            v.memset(eyeblk[:], 0.0)
            v.memset(epsc[:], EPS)
            em.wait(v, "dSU", "eb")
            v.drain()
            for kx in range(4):
                v.tensor_copy(eyeblk[:, kx * HID + kx * P:kx * HID + (kx + 1) * P],
                              eye[:])
            em.ms(v, "vST", "memset")
            em.wait(v, "pRP", "bo0")
            v.tensor_copy(boRep[:, 0:512], pb[5][:])
            em.ms(v, "vRP", "bo0ev")
            em.wait(v, "pRP", "bo1")
            v.tensor_copy(boRep[:, 512:1024], pb[4][:])
            em.wait(v, "pRP", "eb")
            v.tensor_copy(ebRep[:], pb[5][:, 0:NE])
            em.ms(v, "vRP", "boeb")
            em.wait(v, "pRP", "cent")
            v.tensor_copy(ct_bf[:], pbT[:, 0:P])
            em.ms(v, "vRP", "ctev")

    def h_scalar(em, sc, it):
        tag = it[0]
        t = it[1] if len(it) > 1 else None
        i = t % 2 if isinstance(t, int) else None
        if tag == "a1_a_rs":
            i4 = t % 4 if NT % 4 == 0 else t % 2
            em.wait(sc, "vA1", (t, "st"))
            if t >= 4:
                em.wait(sc, "vA1", (t - 4, "hb"))
            sc.activation(lv4[i4][:], ag4[i4][:, 1:2], AF.Ln, bias=epsc[:])
            sc.drain()
            sc.activation(rs4[i4][:], lv4[i4][:], AF.Exp, scale=-0.5)
            em.ms(sc, "aA1", (t, "rs"))
        elif tag == "a1_a_ek":
            em.wait(sc, "pA1", (t, "mm_k"))
            if t >= 2:
                em.wait(sc, "pA1", (t - 2, "mm_E"))
            sc.activation(ek[i][:], pb[0 if i == 0 else 4][:], AF.Exp)
            em.ms(sc, "aA1", (t, "expk"))
            if t + 4 < NT:
                em.dma(sc, f"dXh{(t + 4) % 4}", t + 4,
                       xA4[(t + 4) % 4][:, HID:D],
                       xin[(t + 4) * P:(t + 5) * P, HID:D])
        elif tag == "a1_a_eq":
            em.wait(sc, "pA1", (t, "mm_q1"))
            if t >= 2:
                em.wait(sc, "vA1", (t - 2, "qn2"))
            sc.activation(qe[i][:], pb[5][:], AF.Exp)
            em.ms(sc, "aA1", (t, "expq1"))
            if t >= 1:
                em.wait(sc, "vA1", (t - 1, "qn2"))
                em.dma(sc, f"dQS{(t - 1) % 2}", t - 1,
                       qscr[(t - 1) * P:t * P, :], qn[(t - 1) % 2][:])
        elif tag == "a2_a_rs1":
            em.wait(sc, "vA2", (t, "st1"))
            if t >= 2:
                em.wait(sc, "vA2", (t - 2, "y"))
            sc.activation(lv1[i][:], ag1[i][:, 1:2], AF.Ln, bias=epsc[:])
            sc.drain()
            sc.activation(rs1[i][:], lv1[i][:], AF.Exp, scale=-0.5)
            em.ms(sc, "aA2", (t, "rs1"))
        elif tag == "a2_a_rs2":
            em.wait(sc, "vA2", (t, "st2"))
            if t >= 2:
                em.wait(sc, "vA2", (t - 2, "ub"))
            sc.activation(lv2[i][:], ag2[i][:, 1:2], AF.Ln, bias=epsc[:])
            sc.drain()
            sc.activation(rs2[i][:], lv2[i][:], AF.Exp, scale=-0.5)
            em.ms(sc, "aA2", (t, "rs2"))
        elif tag == "a2_a_es":
            em.wait(sc, "vA2", (t, "scev"))
            if t >= 2:
                em.wait(sc, "vA2", (t - 2, "g"))
            sc.activation(esb[i][:], scb[i][:], AF.Exp, bias=ngm[i][:])
            em.ms(sc, "aA2", (t, "es"))
        elif tag == "finact":
            q, j = it[1], it[2]
            jp = j % 2
            em.wait(sc, "vFF", (q, j, "stF"))
            if (q, j) not in ((0, 0), (0, 1)):
                pj = (q, j - 2) if j >= 2 else (q - 1, j + QT - 2)
                em.wait(sc, "vFF", (pj[0], pj[1], "fout"))
            sc.activation(lvF[jp][:], agF[jp][:, 1:2], AF.Ln, bias=epsc[:])
            sc.drain()
            sc.activation(rsF[jp][:], lvF[jp][:], AF.Exp, scale=-0.5)
            em.ms(sc, "aFF", (q, j, "rsF"))

    def h_tensor(em, pe, it):
        tag = it[0]
        t = it[1] if len(it) > 1 else None
        i = t % 2 if isinstance(t, int) else None
        if tag == "a1_p_trhb":
            em.wait(pe, "vA1", (t, "hb"))
            if t == 0:
                em.wait(pe, "vRP", "ctev")
            if t >= 2:
                em.wait(pe, "vA1", (t - 2, "hTev"))
            i4 = t % 4 if NT % 4 == 0 else t % 2
            pT = pbT if t % 2 == 0 else pbT2
            last = None
            for kk in range(8):
                last = pe.transpose(pT[:, kk * P:(kk + 1) * P],
                                    hb4[i4][:, kk * P:(kk + 1) * P], eye[:])
            em.pe_ms(last, "pA1", (t, "trhb"))
        elif tag == "a1_p_k":
            i4 = t % 4 if NT % 4 == 0 else t % 2
            em.wait(pe, "vA1", (t, "hTev"))
            if t == 0:
                em.wait(pe, "dSW", "cen")
            if t == 1:
                em.wait(pe, "vRP", "boeb")
            if t >= 2:
                em.wait(pe, "aA1", (t - 2, "expk"))
            last = None
            for kk in range(8):
                last = pe.matmul(pb[0 if i == 0 else 4][:],
                                 hT4[i4][:, kk * P:(kk + 1) * P],
                                 wkb[:, kk * HID:(kk + 1) * HID],
                                 start=(kk == 0), stop=(kk == 7))
            em.pe_ms(last, "pA1", (t, "mm_k"))
        elif tag == "a1_p_v":
            i4 = t % 4 if NT % 4 == 0 else t % 2
            if t >= 1:
                em.wait(pe, "vA1", (t - 1, "vbev"))
            last = None
            for kk in range(8):
                last = pe.matmul(pb[1][:],
                                 hT4[i4][:, kk * P:(kk + 1) * P],
                                 wvb[:, kk * HID:(kk + 1) * HID],
                                 start=(kk == 0), stop=(kk == 7))
            em.pe_ms(last, "pA1", (t, "mm_v"))
        elif tag == "a1_p_q2":
            i4 = t % 4 if NT % 4 == 0 else t % 2
            if t == 0:
                em.wait(pe, "vRP", "boeb")
            if t >= 1:
                em.wait(pe, "aA1", (t - 1, "expq1"))
            last = None
            for kk in range(8):
                last = pe.matmul(pb[5][:],
                                 hT4[i4][:, kk * P:(kk + 1) * P],
                                 wqb[:, kk * HID:(kk + 1) * HID],
                                 start=(kk == 0), stop=(kk == 7))
            em.pe_ms(last, "pA1", (t, "mm_q1"))
        elif tag == "a1_p_E":
            em.wait(pe, "aA1", (t, "expk"))
            em.wait(pe, "vA1", (t, "vbev"))
            if t >= 2:
                em.wait(pe, "vA1", (t - 2, "Eacc"))
            last = None
            for h in range(HEADS):
                pbase = (h % 2) * 64
                fo = (h // 2) * 65
                last = pe.matmul(pb[2 + i][pbase:pbase + 64, fo:fo + 65],
                                 ek[i][:, h * DH:(h + 1) * DH],
                                 vbq[i][:, h * 65:(h + 1) * 65],
                                 start=True, stop=True,
                                 tile_position=(0, pbase))
            em.pe_ms(last, "pA1", (t, "mm_E"))
        elif tag == "a2_p_o2":
            bk, pv = it[2]
            em.wait(pe, f"dQT{t % 2}", (t, 3))
            if t == 0:
                em.wait(pe, "vMID", "ctx")
            em.wait(pe, pv[0], pv[1])
            last = None
            for h in range(HEADS):
                pbase = (h % 2) * 64
                last = pe.matmul(pb[bk][pbase:pbase + 64, (h // 2) * P:(h // 2 + 1) * P],
                                 ctx[pbase:pbase + 64, (h // 2) * DH:(h // 2 + 1) * DH],
                                 qTt[i][pbase:pbase + 64, (h // 2) * P:(h // 2 + 1) * P],
                                 start=True, stop=True,
                                 tile_position=(pbase, pbase))
            em.pe_ms(last, "pA2", (t, "o2"))
        elif tag == "a2_p_wo":
            b0, pv0 = it[2]
            b1, pv1 = it[3]
            if t == 0:
                em.wait(pe, "dSW", "bs")
            em.wait(pe, "vA2", (t, "o2ev"))
            em.wait(pe, pv0[0], pv0[1])
            em.wait(pe, pv1[0], pv1[1])
            last = None
            for oh, bk in ((0, b0), (1, b1)):
                for jx in range(4):
                    last = pe.matmul(pb[bk][:], o2b[i][:, jx * P:(jx + 1) * P],
                                     wob[:, jx * D + oh * HID:jx * D + (oh + 1) * HID],
                                     start=(jx == 0), stop=(jx == 3))
                if oh == 0:
                    em.pe_ms(last, "pA2", (t, "wo0"))
            em.pe_ms(last, "pA2", (t, "wo"))
        elif tag == "a2_p_trub":
            em.wait(pe, "vA2", (t, "ub"))
            if t < 2:
                em.wait(pe, "vA1", (NT - 2 + t, "hTev"))
            else:
                em.wait(pe, "vA2", (t - 2, "gTev"))
            pT = pbT if t % 2 == 0 else pbT2
            last = None
            for kk in range(8):
                last = pe.transpose(pT[:, kk * P:(kk + 1) * P],
                                    ub[i][:, kk * P:(kk + 1) * P], eye[:])
            em.pe_ms(last, "pA2", (t, "trub"))
        elif tag == "a2_p_sc":
            bk, pv = it[2]
            em.wait(pe, "vA2", (t, "uTev"))
            em.wait(pe, pv[0], pv[1])
            last = None
            for kk in range(8):
                last = pe.matmul(pb[bk][:, 0:NE],
                                 uT[:, kk * NTOK + t * P:kk * NTOK + (t + 1) * P],
                                 ct_bf[:, kk * NE:(kk + 1) * NE],
                                 start=(kk == 0), stop=(kk == 7))
            em.pe_ms(last, "pA2", (t, "sc"))
        elif tag == "a2_p_trg":
            em.wait(pe, "vA2", (t, "g"))
            em.wait(pe, "vA2", (t, "uTev"))
            pT = pbT if t % 2 == 0 else pbT2
            last = pe.transpose(pT[0:NE, 0:P], gb[:], eye[:])
            em.pe_ms(last, "pA2", (t, "trg"))
        elif tag == "ffnp":
            q, s, oh, j = it[1], it[2], it[3], it[4]
            bk, pv = it[5]
            tt = q * QT + j
            em.wait(pe, "vA2", (tt, "gTev"))
            em.wait(pe, pv[0], pv[1])
            if s == 0:
                last = pe.matmul(pb[bk][:], gTx[:, tt * P:(tt + 1) * P],
                                 brx[:, oh * HID:(oh + 1) * HID],
                                 start=True, stop=False)
                for kx in range(4):
                    kk = 4 * oh + kx
                    last = pe.matmul(
                        pb[bk][:],
                        uT[:, kk * NTOK + tt * P:kk * NTOK + (tt + 1) * P],
                        eyeblk[:, kx * HID:(kx + 1) * HID],
                        start=False, stop=(kx == 3))
            else:
                par = unit_idx(q, s, oh) % 2
                if j == 0:
                    em.wait(pe, f"dWT{par}", (q, s, oh, 7))
                last = None
                for kk in range(8):
                    last = pe.matmul(
                        pb[bk][:],
                        uT[:, kk * NTOK + tt * P:kk * NTOK + (tt + 1) * P],
                        wrt[par][:, kk * HID:(kk + 1) * HID],
                        start=(kk == 0), stop=(kk == 7))
            em.pe_ms(last, "pF", (q, s, oh, j))
        elif tag == "setup_p":
            em.wait(pe, "dSU", "eb")
            em.wait(pe, "vST", "memset")
            i1 = pe.matmul(pb[5][:], onesf[0:1, :], bo_row[:, 0:512],
                           start=True, stop=True)
            em.pe_ms(i1, "pRP", "bo0")
            i2 = pe.matmul(pb[4][:], onesf[0:1, :], bo_row[:, 512:1024],
                           start=True, stop=True)
            em.pe_ms(i2, "pRP", "bo1")
            em.wait(pe, "vRP", "bo0ev")
            i3 = pe.matmul(pb[5][:, 0:NE], onesf[0:1, :], eb_row[:],
                           start=True, stop=True)
            em.pe_ms(i3, "pRP", "eb")
            em.wait(pe, "dSW", "cen")
            last = None
            for kk in range(8):
                last = pe.transpose(pbT[:, kk * NE:(kk + 1) * NE],
                                    cen_st[0:NE, kk * P:(kk + 1) * P],
                                    eye[0:NE, 0:NE])
            em.pe_ms(last, "pRP", "cent")

    def emit_sync(em, sync):
        em.dma(sync, "dSU", "eye", eye[:], eye_in[:])
        em.dma(sync, "dSU", "bo", bo_row[:], bo[:])
        em.dma(sync, "dSU", "eb", eb_row[:], ebi[:])
        xdepth = 4 if NT % 4 == 0 else 2
        for t in range(NT):
            if t >= xdepth:
                em.wait(sync, "vA1", (t - xdepth, "hb"))
            em.dma(sync, f"dX{t % xdepth}", t, xA4[t % xdepth][:, 0:HID],
                   xin[t * P:(t + 1) * P, 0:HID])
        em.wait(sync, "vA1", (NT - 1, "hb"))
        em.dma(sync, "dX20", 0, xA[0][:, 0:HID], xin[0:P, 0:HID])
        em.dma(sync, "dX2h0", 0, xA[0][:, HID:D], xin[0:P, HID:D])
        em.dma(sync, "dX21", 1, xA[1][:, 0:HID], xin[P:2 * P, 0:HID])
        em.dma(sync, "dX2h1", 1, xA[1][:, HID:D], xin[P:2 * P, HID:D])
        for c in range(min(2, NT)):
            em.wait(sync, f"dQS{c % 2}", c)
            for jx in range(4):
                em.dma(sync, f"dQT{c % 2}", (c, jx),
                       qTt[c % 2][:, jx * P:(jx + 1) * P],
                       qscr[c * P:(c + 1) * P, jx * P:(jx + 1) * P],
                       transpose=True)
        for it in flow_tail:
            h_sync(em, sync, it)
        em.wait(sync, "dOUT0", (NQ - 1, QT - 2))
        em.wait(sync, "dOUT1", (NQ - 1, QT - 1))

    def emit_gpsimd(em, gp):
        em.dma(gp, "dSW", "wq", wqb[:].rearrange("p (k c) -> p k c", k=8),
               wq.rearrange("(k p) c -> p k c", p=P))
        em.dma(gp, "dSW", "wk", wkb[:].rearrange("p (k c) -> p k c", k=8),
               wk.rearrange("(k p) c -> p k c", p=P))
        em.dma(gp, "dSW", "wv", wvb[:].rearrange("p (k c) -> p k c", k=8),
               wv.rearrange("(k p) c -> p k c", p=P))
        em.dma(gp, "dSW", "cen", cen_st[:], cen[:])
        em.wait(gp, "vA1", (min(4, NT - 1), "hb"))
        em.dma(gp, "dSW", "wo", wob[:].rearrange("p (k c) -> p k c", k=4),
               wo.rearrange("(k p) c -> p k c", p=P))
        em.dma(gp, "dSW", "br", brx[0:NE, :], brr[:])
        em.dma(gp, "dSW", "bs", brx[NE:NMAT, :], bss[:])
        for m in range(NMAT):
            src = wr[m] if m < NE else ws[m - NE]
            if m >= 1:
                em.wait(gp, "vA1", (min(2 * m + 3, NT - 1), "hb"))
            em.dma(gp, "dWC", m, wbf[m], src)
        if DEBUG:
            em.wait(gp, "dOUT0", (NQ - 1, QT - 2))
            em.wait(gp, "dOUT1", (NQ - 1, QT - 1))
            em.dma(gp, "dDBG", 0, dbg_E[:], E_sb[:])
            em.dma(gp, "dDBG", 1, dbg_ctx[:], ctx[:])
            em.dma(gp, "dDBG", 2, dbg_g[:], gAll[:])
            em.dma(gp, "dDBG", 3, dbg_y[:], yscr[:])
            em.dma(gp, "dDBG", 4, dbg_uT[:], uT[:])
            em.dma(gp, "dDBG", 6, dbg_racc[:], racc[:])
            em.dma(gp, "dDBG", 7, dbg_gTx[:], gTx[:])
            em.wait(gp, "dDBG", 7)

    def emit_vector(em, v):
        h_vector(em, v, ("setup_v",))
        for it in flow_a1:
            h_vector(em, v, it)
        h_vector(em, v, ("mid",))
        for it in flow_tail:
            h_vector(em, v, it)

    def emit_scalar(em, sc):
        for t0 in range(min(4, NT)):
            em.dma(sc, f"dXh{t0 % 4}", t0, xA4[t0 % 4][:, HID:D],
                   xin[t0 * P:(t0 + 1) * P, HID:D])
        for it in flow_a1:
            h_scalar(em, sc, it)
        em.wait(sc, "vA1", (NT - 1, "qn2"))
        em.dma(sc, f"dQS{(NT - 1) % 2}", NT - 1,
               qscr[(NT - 1) * P:NT * P, :], qn[(NT - 1) % 2][:])
        for it in flow_tail:
            h_scalar(em, sc, it)

    def emit_tensor(em, pe):
        h_tensor(em, pe, ("setup_p",))
        for it in flow_a1:
            h_tensor(em, pe, it)
        for it in flow_tail:
            h_tensor(em, pe, it)

    em = Em()
    dummy = _Dummy()
    for f in (emit_sync, emit_gpsimd, emit_tensor, emit_vector, emit_scalar):
        f(em, dummy)
    em.mode = "emit"
    em.cnt = {}
    for name in em.vals:
        em.sem[name] = nc.alloc_semaphore(name)
    with nc.Block() as block:
        @block.sync
        def _(sync):
            emit_sync(em, sync)

        @block.gpsimd
        def _(gp):
            emit_gpsimd(em, gp)

        @block.tensor
        def _(pe):
            emit_tensor(em, pe)

        @block.vector
        def _(v):
            emit_vector(em, v)

        @block.scalar
        def _(sc):
            emit_scalar(em, sc)

    return nc


def kernel(L0_attn_x, L0_attn_Wq, L0_attn_Wk, L0_attn_Wv, L0_attn_Wo,
           L0_attn_bo, L0_ffn_centroids, L0_ffn_expert_biases, L0_ffn_Wr,
           L0_ffn_br, L0_ffn_Ws, L0_ffn_bs):
    x = np.ascontiguousarray(np.asarray(L0_attn_x, dtype=np.float32))
    com = {
        "wq": np.ascontiguousarray(np.asarray(L0_attn_Wq, np.float32)),
        "wk": np.ascontiguousarray(np.asarray(L0_attn_Wk, np.float32)),
        "wv": np.ascontiguousarray(np.asarray(L0_attn_Wv, np.float32)),
        "wo": np.ascontiguousarray(np.asarray(L0_attn_Wo, np.float32)),
        "bo": np.ascontiguousarray(np.asarray(L0_attn_bo, np.float32).reshape(1, D)),
        "cen": np.ascontiguousarray(np.asarray(L0_ffn_centroids, np.float32)),
        "ebi": np.ascontiguousarray(np.asarray(L0_ffn_expert_biases, np.float32).reshape(1, NE)),
        "wr": np.ascontiguousarray(np.asarray(L0_ffn_Wr, np.float32)),
        "brr": np.ascontiguousarray(np.asarray(L0_ffn_br, np.float32)),
        "ws": np.ascontiguousarray(np.asarray(L0_ffn_Ws, np.float32)),
        "bss": np.ascontiguousarray(np.asarray(L0_ffn_bs, np.float32)),
        "eye_in": np.eye(P, dtype=ml_dtypes.bfloat16),
    }
    nc = build_nc()
    in_maps = [dict(com, xin=np.ascontiguousarray(x[b])) for b in range(N_CORES)]
    res = run_bass_kernel_spmd(nc, in_maps, core_ids=list(range(N_CORES)))
    return np.stack([r["out"] for r in res.results], axis=0)


# version 19
# speedup vs baseline: 1.1446x; 1.0236x over previous
import numpy as np
import ml_dtypes
import concourse.bass as bass
import concourse.mybir as mybir
from concourse.bass_utils import run_bass_kernel_spmd

F32 = mybir.dt.float32
BF16 = mybir.dt.bfloat16
U32 = mybir.dt.uint32
AF = mybir.ActivationFunctionType
ALU = mybir.AluOpType
AX = mybir.AxisListType

N_CORES = 8
NTOK = 4096
D = 1024
P = 128
NT = NTOK // P
HEADS = 8
DH = 64
HID = 512
SCALE = DH ** -0.5
NE = 16
NSH = 2
NMAT = NE + NSH
NSTEP = 1 + NMAT
NQ = 4
QT = NT // NQ
EPS = 1e-5
DEBUG = False
PHASES = 'ALL'


class _Dummy:
    def __getattr__(self, k):
        return lambda *a, **kw: self

    def __call__(self, *a, **kw):
        return self


class Em:

    def __init__(self):
        self.vals = {}
        self.cnt = {}
        self.mode = "count"
        self.sem = {}

    def _bump(self, chan, label, k):
        c = self.cnt.get(chan, 0) + k
        self.cnt[chan] = c
        d = self.vals.setdefault(chan, {})
        if self.mode == "count":
            assert label not in d, f"dup label {chan}/{label}"
            d[label] = c
        else:
            assert d.get(label) == c, f"label drift {chan}/{label}: {d.get(label)} != {c}"
        return c

    def ms(self, eng, chan, label):
        self._bump(chan, label, 1)
        if self.mode == "emit":
            eng.drain().then_inc(self.sem[chan], 1)

    def pe_ms(self, instr, chan, label):
        self._bump(chan, label, 1)
        if self.mode == "emit":
            instr.then_inc(self.sem[chan], 1)

    def op_ms(self, instr, chan, label):
        self._bump(chan, label, 1)
        if self.mode == "emit":
            instr.then_inc(self.sem[chan], 1)

    def dma(self, eng, chan, label, out, in_, **kw):
        self._bump(chan, label, 16)
        if self.mode == "emit":
            eng.dma_start(out=out, in_=in_, **kw).then_inc(self.sem[chan], 16)

    def wait(self, eng, chan, label):
        if self.mode == "emit":
            eng.wait_ge(self.sem[chan], self.vals[chan][label])


def build_nc():
    nc = bass.Bass("TRN2", target_bir_lowering=False, debug=False)

    xin = nc.dram_tensor("xin", [NTOK, D], F32, kind="ExternalInput").ap()
    wq = nc.dram_tensor("wq", [D, HID], F32, kind="ExternalInput").ap()
    wk = nc.dram_tensor("wk", [D, HID], F32, kind="ExternalInput").ap()
    wv = nc.dram_tensor("wv", [D, HID], F32, kind="ExternalInput").ap()
    wo = nc.dram_tensor("wo", [HID, D], F32, kind="ExternalInput").ap()
    bo = nc.dram_tensor("bo", [1, D], F32, kind="ExternalInput").ap()
    cen = nc.dram_tensor("cen", [NE, D], F32, kind="ExternalInput").ap()
    ebi = nc.dram_tensor("ebi", [1, NE], F32, kind="ExternalInput").ap()
    wr = nc.dram_tensor("wr", [NE, D, D], F32, kind="ExternalInput").ap()
    brr = nc.dram_tensor("brr", [NE, D], F32, kind="ExternalInput").ap()
    ws = nc.dram_tensor("ws", [NSH, D, D], F32, kind="ExternalInput").ap()
    bss = nc.dram_tensor("bss", [NSH, D], F32, kind="ExternalInput").ap()
    eye_in = nc.dram_tensor("eye_in", [P, P], BF16, kind="ExternalInput").ap()
    out = nc.dram_tensor("out", [NTOK, D], F32, kind="ExternalOutput").ap()
    wbf = nc.dram_tensor("wbf", [NMAT, D, D], BF16, kind="Internal").ap()
    yscr = nc.dram_tensor("yscr", [NTOK, D], F32, kind="Internal").ap()
    if DEBUG:
        dbg_E = nc.dram_tensor("dbg_E", [P, 4 * 65], F32, kind="ExternalOutput").ap()
        dbg_ctx = nc.dram_tensor("dbg_ctx", [P, 4 * DH], F32, kind="ExternalOutput").ap()
        dbg_g = nc.dram_tensor("dbg_g", [P, NT * NE], F32, kind="ExternalOutput").ap()
        dbg_y = nc.dram_tensor("dbg_y", [NTOK, D], F32, kind="ExternalOutput").ap()
        dbg_uT = nc.dram_tensor("dbg_uT", [P, 8 * NTOK], F32, kind="ExternalOutput").ap()
        dbg_racc = nc.dram_tensor("dbg_racc", [P, QT * D], F32, kind="ExternalOutput").ap()
        dbg_gTx = nc.dram_tensor("dbg_gTx", [NMAT, NTOK], F32, kind="ExternalOutput").ap()

    qscr = nc.dram_tensor("qscr", [NTOK, HID], BF16, kind="Internal").ap()

    A = nc.alloc_sbuf_tensor
    eye = A("eye", [P, P], BF16)
    onesf = A("onesf", [4, P], F32)
    boRep = A("boRep", [P, D], F32)
    ebRep = A("ebRep", [P, NE], F32)
    ct_bf = A("ct_bf", [P, P], BF16)
    brx = A("brx", [NMAT, D], BF16)
    gTx = A("gTx", [NMAT, NTOK], BF16)
    uT = A("uT", [P, 8 * NTOK], BF16)
    gAll = A("gAll", [P, NT * NE], F32)
    halfc = A("halfc", [P, 1], F32)
    neg3e4 = A("neg3e4", [P, NE], F32)
    zero16 = A("zero16", [P, NE], F32)
    gb = A("gb", [P, NE], BF16)
    epsc = A("epsc", [P, 1], F32)
    eyeblk = A("eyeblk", [P, 4 * HID], BF16)
    E_sb = A("E_sb", [P, 4 * 65], F32)
    sInv = A("sInv", [P, 4], F32)
    ctx = A("ctx", [P, 4 * DH], BF16)

    def stat4(nm):
        return ([A(f"{nm}bn{i}", [P, 12], F32) for i in range(2)],
                [A(f"{nm}ag{i}", [P, 2], F32) for i in range(2)],
                [A(f"{nm}lv{i}", [P, 1], F32) for i in range(2)],
                [A(f"{nm}rs{i}", [P, 1], F32) for i in range(2)])

    bnA, agA, lvA, rsA = stat4("A")
    bn0, ag0, lv0, rs0 = stat4("B")
    ag4, lv4, rs4 = agA + ag0, lvA + lv0, rsA + rs0
    bn1, ag1, lv1, rs1 = stat4("C")
    bn2, ag2, lv2, rs2 = stat4("D")
    bnF, agF, lvF, rsF = stat4("E")

    qs = [A(f"qs{i}", [P, HEADS], F32) for i in range(2)]
    rq = [A(f"rq{i}", [P, HEADS], F32) for i in range(2)]
    scb = [A(f"scb{i}", [P, NE], F32) for i in range(2)]
    esb = [A(f"esb{i}", [P, NE], F32) for i in range(2)]
    ngm = [A(f"ngm{i}", [P, 1], F32) for i in range(2)]
    rsm = [A(f"rsm{i}", [P, 1], F32) for i in range(2)]
    mx1 = [A(f"mx1{i}", [P, 1], F32) for i in range(2)]
    m2v = [A(f"m2v{i}", [P, 1], F32) for i in range(2)]
    mk1 = [A(f"mk1{i}", [P, NE], U32) for i in range(2)]
    mk2 = [A(f"mk2{i}", [P, NE], U32) for i in range(2)]
    sm2 = [A(f"sm2{i}", [P, NE], F32) for i in range(2)]
    gsl = [A(f"gsl{i}", [P, NE], F32) for i in range(2)]
    ssm = [A(f"ssm{i}", [P, 1], F32) for i in range(2)]

    wqb = A("wqb", [P, 8 * HID], BF16)
    wob = A("wob", [P, 4 * D], BF16)
    xA = [A(f"xA{i}", [P, D], F32) for i in range(2)]
    hb = [A(f"hb{i}", [P, D], BF16) for i in range(2)]
    hT = [A(f"hT{i}", [P, D], BF16) for i in range(2)]
    qe = [A(f"qe{i}", [P, HID], F32) for i in range(2)]
    _qs, _qe = nc.bump_sbuf(2 * HID * 2, "left", align_bytes=128)
    qn = [nc.alloc_sbuf_tensor_at(f"qn{i}", [P, HID], BF16,
                                  offset=_qs + i * HID * 2) for i in range(2)]
    qTt = [A(f"qTt{i}", [P, HID], BF16) for i in range(2)]
    qT4 = qTt + [nc.alloc_sbuf_tensor_at(f"qTal{i}", [P, HID], BF16,
                 offset=_qs + i * HID * 2) for i in range(2)]
    o2b = [A(f"o2b{i}", [P, HID], BF16) for i in range(2)]
    att = [A(f"att{i}", [P, D], F32) for i in range(2)]
    ub = [A(f"ub{i}", [P, D], BF16) for i in range(2)]
    wrt = [A(f"wrt{i}", [P, 8 * HID], BF16, is_transpose_dest=True) for i in range(2)]
    _rs, _re = nc.bump_sbuf(max(QT * D * 4, 32768), "left", align_bytes=128)
    racc = nc.alloc_sbuf_tensor_at("racc", [P, QT * D], F32, offset=_rs)
    wkb = nc.alloc_sbuf_tensor_at("wkb", [P, 8 * HID], BF16, offset=_rs)
    wvb = nc.alloc_sbuf_tensor_at("wvb", [P, 8 * HID], BF16, offset=_rs + 8 * HID * 2)
    xA4 = xA + [nc.alloc_sbuf_tensor_at(f"xA{i}", [P, D], F32,
                offset=_rs + 16384 + (i - 2) * 4096) for i in (2, 3)]
    hT4 = hT + [nc.alloc_sbuf_tensor_at(f"hT{i}", [P, D], BF16,
                offset=_rs + 24576 + (i - 2) * 2048) for i in (2, 3)]
    hb4 = hb + [nc.alloc_sbuf_tensor_at(f"hb{i}", [P, D], BF16,
                offset=_rs + 28672 + (i - 2) * 2048) for i in (2, 3)]
    _fs, _fe = nc.bump_sbuf(16384, "left", align_bytes=128)
    AT = nc.alloc_sbuf_tensor_at
    yld = [AT(f"yld{i}", [P, D], F32, offset=_fs + i * 4096) for i in range(2)]
    ftmp = [AT(f"ftmp{i}", [P, D], F32, offset=_fs + 8192 + i * 4096) for i in range(2)]
    bo_row = AT("bo_row", [1, D], F32, offset=_fs)
    cen_st = AT("cen_st", [NE, D], BF16, offset=_fs + 4096)
    eb_row = AT("eb_row", [1, NE], F32, offset=_fs + 6144)
    ek = [AT(f"ek{i}", [P, HID], BF16, offset=_fs + 6272 + i * 1024) for i in range(2)]
    vbq = [AT(f"vb{i}", [P, 8 * 65], BF16, offset=_fs + 8352 + i * 1056) for i in range(2)]

    PA = nc.alloc_psum_tensor
    pbT = PA("pbT", [P, 1024], BF16)
    pbT2 = PA("pbT2", [P, 1024], BF16)
    pb = [PA(f"pb{i}", [P, 512], F32) for i in range(6)]

    def a1_items(t):
        return [("a1_v_st", t), ("a1_a_rs", t), ("a1_v_hb", t),
                ("a1_p_trhb", t), ("a1_v_hT", t), ("a1_p_k", t),
                ("a1_a_ek", t), ("a1_p_v", t), ("a1_v_vb", t),
                ("a1_p_E", t), ("a1_v_Eacc", t),
                ("a1_p_q2", t), ("a1_a_eq", t), ("a1_v_qn2", t)]

    def zip_merge(la, lb):
        outl = []
        na, nb = len(la), len(lb)
        ia = ib = 0
        while ia < na or ib < nb:
            if ia < na:
                outl.append(la[ia]); ia += 1
            if ib < nb:
                outl.append(lb[ib]); ib += 1
        return outl

    flow_a1 = []
    HEADI = 7
    if NT % 4 == 0:
        for t in range(0, NT, 4):
            lists = [a1_items(t + d) for d in range(4)]
            n = len(lists[0])
            for r in range(HEADI):
                for lst in lists:
                    flow_a1.append(lst[r])
            for pp in range(2):
                for r in range(HEADI, n):
                    for lst in lists[2 * pp:2 * pp + 2]:
                        flow_a1.append(lst[r])
    else:
        for t in range(0, NT, 2):
            flow_a1 += zip_merge(a1_items(t), a1_items(t + 1))

    def a2_items(t):
        return [("a2_p_o2", t), ("a2_v_o2ev", t),
                ("a2_p_wo", t), ("a2_v_att", t), ("a2_a_rs1", t),
                ("a2_v_y", t), ("a2_a_rs2", t), ("a2_v_ub", t),
                ("a2_p_trub", t), ("a2_v_uT", t), ("a2_p_sc", t),
                ("a2_v_scev", t), ("a2_a_es", t), ("a2_v_g", t),
                ("a2_p_trg", t), ("a2_v_gT", t)]

    units = [(q, s, oh) for q in range(NQ) for s in range(1, NSTEP)
             for oh in range(2)]

    def unit_idx(q, s, oh):
        return (q * (NSTEP - 1) + (s - 1)) * 2 + oh

    def ffn_quarter(q):
        its = []
        for oh in range(2):
            for j in range(QT):
                its.append(("ffnp", q, 0, oh, j))
                its.append(("ffnv", q, 0, oh, j))
        for s in range(1, NSTEP):
            for oh in range(2):
                k = unit_idx(q, s, oh)
                if k + 2 < len(units):
                    its.append(("ffnl",) + units[k + 2])
                for j in range(QT):
                    its.append(("ffnp", q, s, oh, j))
                    its.append(("ffnv", q, s, oh, j))
        return its

    def fin_quarter(q):
        its = []
        for j in range(QT):
            its.append(("fina", q, j))
            its.append(("finact", q, j))
            if j >= 1:
                its.append(("finb", q, j - 1))
        its.append(("finb", q, QT - 1))
        return its

    def prop_merge(la, lb):
        outl = []
        na, nb = len(la), len(lb)
        if na == 0:
            return list(lb)
        ratio = nb / na
        ib = 0
        for ia in range(na):
            outl.append(la[ia])
            want = int(round((ia + 1) * ratio * 0.65))
            while ib < want:
                outl.append(lb[ib]); ib += 1
        outl.extend(lb[ib:])
        return outl

    flow_tail = []
    flow_tail.append(("ffnl",) + units[0])
    flow_tail.append(("ffnl",) + units[1])
    for w in range(NQ + 2):
        la = []
        if w < NQ:
            for t in range(QT * w, QT * w + QT, 2):
                la += zip_merge(a2_items(t), a2_items(t + 1))
        lb = []
        if 2 <= w:
            lb += fin_quarter(w - 2)
        if 1 <= w <= NQ:
            lb += ffn_quarter(w - 1)
        flow_tail += prop_merge(la, lb)

    pool = [
        [0, ("aA1", (NT - 2, "expk")), -1],
        [1, ("vA1", (NT - 1, "vbev")), -1],
        [2, ("vA1", (NT - 2, "Eacc")), -1],
        [3, ("vA1", (NT - 1, "Eacc")), -1],
        [4, ("aA1", (NT - 1, "expk")), -1],
        [5, ("aA1", (NT - 1, "expq1")), -1],
    ]
    _needs = {"a2_p_o2": ("vA2", "o2ev", 1),
              "a2_p_wo": ("vA2", "att", 2), "a2_p_sc": ("vA2", "scev", 1)}
    _evac_tag = {"a2_p_o2": "a2_v_o2ev",
                 "a2_p_wo": "a2_v_att", "a2_p_sc": "a2_v_scev"}
    evac_pos = {}
    for pos, it in enumerate(flow_tail):
        if it[0] in ("a2_v_o2ev", "a2_v_att", "a2_v_scev"):
            evac_pos[(it[0], it[1])] = pos
        elif it[0] == "ffnv":
            evac_pos[("ffnv", it[1], it[2], it[3], it[4])] = pos

    def pool_take(pos, n):
        got = []
        for _ in range(n):
            cands = [e for e in pool if e[2] < pos]
            assert cands, f"no reusable psum bank at flow pos {pos}"
            e = min(cands, key=lambda x: x[2])
            pool.remove(e)
            got.append(e)
        return got

    pend = {}
    flow2 = []
    for pos, it in enumerate(flow_tail):
        tag = it[0]
        if tag in _needs:
            t = it[1]
            chan, lbl, nbk = _needs[tag]
            got = pool_take(pos, nbk)
            ep = evac_pos[(_evac_tag[tag], t)]
            res = []
            for bk, pv, _e in got:
                res.append((bk, pv))
                pool.append([bk, (chan, (t, lbl)), ep])
            pend[(tag, t)] = res
            flow2.append(it + tuple(res))
        elif tag == "a2_v_o2ev":
            flow2.append(it + (pend[("a2_p_o2", it[1])][0][0],))
        elif tag == "a2_v_att":
            g = pend[("a2_p_wo", it[1])]
            flow2.append(it + (g[0][0], g[1][0]))
        elif tag == "a2_v_scev":
            flow2.append(it + (pend[("a2_p_sc", it[1])][0][0],))
        elif tag == "ffnp":
            q, s, oh, j = it[1], it[2], it[3], it[4]
            got = pool_take(pos, 1)[0]
            ep = evac_pos[("ffnv", q, s, oh, j)]
            pool.append([got[0], ("vF", (q, s, oh, j)), ep])
            pend[("f", q, s, oh, j)] = (got[0], got[1])
            flow2.append(it + ((got[0], got[1]),))
        elif tag == "ffnv":
            q, s, oh, j = it[1], it[2], it[3], it[4]
            flow2.append(it + (pend[("f", q, s, oh, j)][0],))
        else:
            flow2.append(it)
    flow_tail = flow2

    def h_sync(em, sync, it):
        tag = it[0]
        if tag == "a2_v_y":
            t = it[1]; i = t % 2
            em.wait(sync, "vA2", (t, "y"))
            em.dma(sync, f"dY{i}", t, yscr[t * P:(t + 1) * P, :], att[i][:])
            if t + 4 < NT:
                em.wait(sync, f"dQS{(t + 4) % 2}", t + 4)
                for jx in range(4):
                    em.dma(sync, f"dQT{(t + 4) % 4}", (t + 4, jx),
                           qT4[(t + 4) % 4][:, jx * P:(jx + 1) * P],
                           qscr[(t + 4) * P:(t + 5) * P, jx * P:(jx + 1) * P],
                           transpose=True)
            if t + 2 < NT:
                em.dma(sync, f"dX2{i}", t + 2, xA[i][:, 0:HID],
                       xin[(t + 2) * P:(t + 3) * P, 0:HID])
                em.dma(sync, f"dX2h{i}", t + 2, xA[i][:, HID:D],
                       xin[(t + 2) * P:(t + 3) * P, HID:D])

        elif tag == "ffnl":
            q, s, oh = it[1], it[2], it[3]
            k = unit_idx(q, s, oh)
            par = k % 2
            if k >= 2:
                pq, ps, po = units[k - 2]
                em.wait(sync, "pF", (pq, ps, po, QT - 1))
            em.wait(sync, "dWC", NMAT - 1)
            for kk in range(8):
                em.dma(sync, f"dWT{par}", (q, s, oh, kk),
                       wrt[par][:, kk * HID:(kk + 1) * HID],
                       wbf[s - 1, oh * HID:(oh + 1) * HID, kk * P:(kk + 1) * P],
                       transpose=True)
        elif tag == "fina":
            q, j = it[1], it[2]
            t = q * QT + j
            em.wait(sync, f"dY{t % 2}", t)
            if (q, j) not in ((0, 0), (0, 1)):
                pj = (q, j - 2) if j >= 2 else (q - 1, j + QT - 2)
                em.wait(sync, "vFF", (pj[0], pj[1], "fout"))
            em.dma(sync, f"dYL{j % 2}", (q, j), yld[j % 2][:],
                   yscr[t * P:(t + 1) * P, :])
        elif tag == "finb":
            q, j = it[1], it[2]
            t = q * QT + j
            em.wait(sync, "vFF", (q, j, "fout"))
            em.dma(sync, f"dOUT{j % 2}", (q, j), out[t * P:(t + 1) * P, :],
                   ftmp[j % 2][:])

    def ln_stats(v, bn, ag, src):
        v.bn_stats(bn[:, 0:6], src[:, 0:512])
        v.bn_stats(bn[:, 6:12], src[:, 512:1024])
        v.drain()
        v.bn_aggr(ag[:], bn[:, 0:12])

    def h_vector(em, v, it):
        tag = it[0]
        t = it[1] if len(it) > 1 else None
        i = t % 2 if isinstance(t, int) else None
        if tag == "a1_v_st":
            i4 = t % 4 if NT % 4 == 0 else t % 2
            em.wait(v, f"dX{i4}", t)
            em.wait(v, f"dXh{i4}", t)
            if t >= 4:
                em.wait(v, "aA1", (t - 4, "rs"))
            ln_stats(v, bnA[i], ag4[i4], xA4[i4])
            em.ms(v, "vA1", (t, "st"))
        elif tag == "a1_v_hb":
            i4 = t % 4 if NT % 4 == 0 else t % 2
            em.wait(v, "aA1", (t, "rs"))
            if t >= 4:
                em.wait(v, "pA1", (t - 4, "trhb"))
            v.tensor_scalar(hb4[i4][:], xA4[i4][:], ag4[i4][:, 0:1], rs4[i4][:],
                            ALU.subtract, ALU.mult)
            em.ms(v, "vA1", (t, "hb"))
        elif tag == "a1_v_hT":
            i4 = t % 4 if NT % 4 == 0 else t % 2
            em.wait(v, "pA1", (t, "trhb"))
            if t >= 4:
                em.wait(v, "pA1", (t - 4, "mm_q1"))
            pT = pbT if t % 2 == 0 else pbT2
            v.tensor_copy(hT4[i4][:].rearrange("p (k c) -> p k c", k=8),
                          pT[:].rearrange("p (k c) -> p k c", k=8))
            em.ms(v, "vA1", (t, "hTev"))
        elif tag == "a1_v_vb":
            em.wait(v, "pA1", (t, "mm_v"))
            if t >= 2:
                em.wait(v, "pA1", (t - 2, "mm_E"))
            v.tensor_copy(vbq[i][:].rearrange("p (h c) -> p h c", c=65)[:, :, 0:64],
                          pb[1][:].rearrange("p (h c) -> p h c", c=64))
            em.ms(v, "vA1", (t, "vbev"))
        elif tag == "a1_v_Eacc":
            em.wait(v, "pA1", (t, "mm_E"))
            v.drain()
            ii = v.tensor_tensor(E_sb[:], E_sb[:], pb[2 + i][:, 0:260], ALU.add)
            em.op_ms(ii, "vA1", (t, "Eacc"))
        elif tag == "a1_v_qn2":
            em.wait(v, "aA1", (t, "expq1"))
            if t >= 2:
                em.wait(v, f"dQS{t % 2}", t - 2)
            v.reduce_sum(qs[i][:].rearrange("p (h o) -> p h o", o=1),
                         qe[i][:].rearrange("p (h c) -> p h c", h=HEADS),
                         axis=AX.X)
            v.drain()
            v.reciprocal(rq[i][:], qs[i][:])
            v.drain()
            for h in range(HEADS):
                v.tensor_scalar(qn[i][:, h * DH:(h + 1) * DH],
                                qe[i][:, h * DH:(h + 1) * DH],
                                rq[i][:, h:h + 1], SCALE, ALU.mult, ALU.mult)
            em.ms(v, "vA1", (t, "qn2"))
        elif tag == "mid":
            v.drain()
            for h in range(HEADS):
                pbase, fo = (h % 2) * 64, (h // 2) * 65
                v.reciprocal(sInv[pbase:pbase + 64, h // 2:h // 2 + 1],
                             E_sb[pbase:pbase + 64, fo + 64:fo + 65])
            v.drain()
            for h in range(HEADS):
                pbase, fo = (h % 2) * 64, (h // 2) * 65
                v.tensor_scalar(ctx[pbase:pbase + 64, (h // 2) * DH:(h // 2 + 1) * DH],
                                E_sb[pbase:pbase + 64, fo:fo + 64],
                                sInv[pbase:pbase + 64, h // 2:h // 2 + 1],
                                None, ALU.mult)
            em.ms(v, "vMID", "ctx")
        elif tag == "a2_v_qn":
            em.wait(v, "aA2", (t, "expq"))
            v.reduce_sum(qs[i][:].rearrange("p (h o) -> p h o", o=1),
                         qe[i][:].rearrange("p (h c) -> p h c", h=HEADS),
                         axis=AX.X)
            v.drain()
            v.reciprocal(rq[i][:], qs[i][:])
            v.drain()
            if t >= 2:
                em.wait(v, "pA2", (t - 2, "trqn"))
            for h in range(HEADS):
                v.tensor_scalar(qn[i][:, h * DH:(h + 1) * DH],
                                qe[i][:, h * DH:(h + 1) * DH],
                                rq[i][:, h:h + 1], SCALE, ALU.mult, ALU.mult)
            em.ms(v, "vA2", (t, "qn"))
        elif tag == "a2_v_qT":
            em.wait(v, "pA2", (t, "trqn"))
            if t >= 2:
                em.wait(v, "pA2", (t - 2, "o2"))
            pT = pbT if t % 2 == 0 else pbT2
            v.tensor_copy(qTt[i][:], pT[:, 0:512])
            em.ms(v, "vA2", (t, "qTev"))
        elif tag == "a2_v_o2ev":
            em.wait(v, "pA2", (t, "o2"))
            if t >= 2:
                em.wait(v, "pA2", (t - 2, "wo"))
            v.tensor_copy(o2b[i][:], pb[it[2]][:])
            em.ms(v, "vA2", (t, "o2ev"))
        elif tag == "a2_v_att":
            em.wait(v, "pA2", (t, "wo0"))
            if t == 0:
                em.wait(v, "vRP", "boeb")
            if t >= 2:
                em.wait(v, "vA2", (t - 2, "ub"))
                em.wait(v, f"dY{i}", t - 2)
            v.tensor_tensor(att[i][:, 0:512], pb[it[2]][:], boRep[:, 0:512], ALU.add)
            em.wait(v, "pA2", (t, "wo"))
            ii = v.tensor_tensor(att[i][:, 512:1024], pb[it[3]][:],
                                 boRep[:, 512:1024], ALU.add)
            em.op_ms(ii, "vA2", (t, "att"))
            v.drain()
            if t >= 2:
                em.wait(v, "aA2", (t - 2, "rs1"))
            ln_stats(v, bn1[i], ag1[i], att[i])
            em.ms(v, "vA2", (t, "st1"))
        elif tag == "a2_v_y":
            em.wait(v, "aA2", (t, "rs1"))
            v.tensor_scalar(att[i][:], att[i][:], ag1[i][:, 0:1], rs1[i][:],
                            ALU.subtract, ALU.mult)
            v.drain()
            em.wait(v, f"dX2{i}", t)
            em.wait(v, f"dX2h{i}", t)
            v.tensor_tensor(att[i][:], att[i][:], xA[i][:], ALU.add)
            em.ms(v, "vA2", (t, "y"))
            if t >= 2:
                em.wait(v, "aA2", (t - 2, "rs2"))
            ln_stats(v, bn2[i], ag2[i], att[i])
            em.ms(v, "vA2", (t, "st2"))
        elif tag == "a2_v_ub":
            em.wait(v, "aA2", (t, "rs2"))
            if t >= 2:
                em.wait(v, "pA2", (t - 2, "trub"))
            v.tensor_scalar(ub[i][:], att[i][:], ag2[i][:, 0:1], rs2[i][:],
                            ALU.subtract, ALU.mult)
            em.ms(v, "vA2", (t, "ub"))
        elif tag == "a2_v_uT":
            em.wait(v, "pA2", (t, "trub"))
            pT = pbT if t % 2 == 0 else pbT2
            v.tensor_copy(uT[:].rearrange("p (k n) -> p k n", k=8)[:, :, t * P:(t + 1) * P],
                          pT[:].rearrange("p (k n) -> p k n", k=8))
            em.ms(v, "vA2", (t, "uTev"))
        elif tag == "a2_v_scev":
            em.wait(v, "pA2", (t, "sc"))
            if t >= 2:
                em.wait(v, "aA2", (t - 2, "es"))
            v.tensor_tensor(scb[i][:], pb[it[2]][:, 0:NE], ebRep[:], ALU.add)
            v.drain()
            v.reduce_max(ngm[i][:], scb[i][:], axis=AX.X, negate=True)
            em.ms(v, "vA2", (t, "scev"))
        elif tag == "a2_v_g":
            em.wait(v, "aA2", (t, "es"))
            v.reduce_sum(ssm[i][:], esb[i][:], axis=AX.X)
            v.drain()
            v.reciprocal(rsm[i][:], ssm[i][:])
            v.reduce_max(mx1[i][:], esb[i][:], axis=AX.X)
            v.drain()
            v.tensor_scalar(mk1[i][:], esb[i][:], mx1[i][:], None, ALU.is_ge)
            v.drain()
            v.select(sm2[i][:], mk1[i][:], neg3e4[:], esb[i][:], add_drain=True)
            v.drain()
            v.reduce_max(m2v[i][:], sm2[i][:], axis=AX.X)
            v.drain()
            v.tensor_scalar(mk2[i][:], esb[i][:], m2v[i][:], None, ALU.is_ge)
            v.drain()
            v.select(gsl[i][:], mk2[i][:], esb[i][:], zero16[:], add_drain=True)
            v.drain()
            v.tensor_scalar(gAll[:, t * NE:(t + 1) * NE], gsl[i][:],
                            rsm[i][:], None, ALU.mult)
            v.drain()
            if t >= 1:
                em.wait(v, "pA2", (t - 1, "trg"))
            v.tensor_copy(gb[:], gAll[:, t * NE:(t + 1) * NE])
            em.ms(v, "vA2", (t, "g"))
        elif tag == "a2_v_gT":
            em.wait(v, "pA2", (t, "trg"))
            pT = pbT if t % 2 == 0 else pbT2
            v.tensor_copy(gTx[0:NE, t * P:(t + 1) * P], pT[0:NE, 0:P])
            em.ms(v, "vA2", (t, "gTev"))
        elif tag == "ffnv":
            q, s, oh, j, bk = it[1], it[2], it[3], it[4], it[5]
            tt = q * QT + j
            em.wait(v, "pF", (q, s, oh, j))
            if q == 0 and s == 0:
                em.wait(v, "pA1", (NT - 1, "mm_E"))
            if s >= 1:
                em.wait(v, "vF", (q, s - 1, oh, j))
            elif q >= 1:
                em.wait(v, "vFF", (q - 1, j, "fout"))
            rsl = racc[:, j * D + oh * HID:j * D + (oh + 1) * HID]
            if s == 0:
                ii = v.tensor_copy(rsl, pb[bk][:])
            else:
                e = s - 1
                gcol = (gAll[:, tt * NE + e:tt * NE + e + 1]
                        if e < NE else halfc[:])
                ii = v.scalar_tensor_tensor(rsl, pb[bk][:], gcol, rsl,
                                            ALU.mult, ALU.add)
            em.op_ms(ii, "vF", (q, s, oh, j))
        elif tag == "fina":
            q, j = it[1], it[2]
            jp = j % 2
            em.wait(v, "pF", (q, NSTEP - 1, 1, j))
            if q == 0 and j < 2:
                em.wait(v, "pA1", (NT - 1, "mm_E"))
            if (q, j) not in ((0, 0), (0, 1)):
                pj = (q, j - 2) if j >= 2 else (q - 1, j + QT - 2)
                em.wait(v, "aFF", (pj[0], pj[1], "rsF"))
            v.drain()
            ln_stats(v, bnF[jp], agF[jp], racc[:, j * D:(j + 1) * D])
            em.ms(v, "vFF", (q, j, "stF"))
        elif tag == "finb":
            q, j = it[1], it[2]
            jp = j % 2
            em.wait(v, "aFF", (q, j, "rsF"))
            em.wait(v, f"dYL{jp}", (q, j))
            if (q, j) not in ((0, 0), (0, 1)):
                pj = (q, j - 2) if j >= 2 else (q - 1, j + QT - 2)
                em.wait(v, f"dOUT{jp}", pj)
            v.tensor_scalar(ftmp[jp][:], racc[:, j * D:(j + 1) * D],
                            agF[jp][:, 0:1], rsF[jp][:], ALU.subtract, ALU.mult)
            v.drain()
            v.tensor_tensor(ftmp[jp][:], ftmp[jp][:], yld[jp][:], ALU.add)
            em.ms(v, "vFF", (q, j, "fout"))
        elif tag == "setup_v":
            v.memset(onesf[:], 1.0)
            v.memset(neg3e4[:], -30000.0)
            v.memset(zero16[:], 0.0)
            v.memset(halfc[:], 0.5)
            v.memset(E_sb[:], 0.0)
            v.memset(gTx[:], 0.5)
            v.memset(vbq[0][:], 1.0)
            v.memset(vbq[1][:], 1.0)
            v.memset(eyeblk[:], 0.0)
            v.memset(epsc[:], EPS)
            em.wait(v, "dSU", "eb")
            v.drain()
            for kx in range(4):
                v.tensor_copy(eyeblk[:, kx * HID + kx * P:kx * HID + (kx + 1) * P],
                              eye[:])
            em.ms(v, "vST", "memset")
            em.wait(v, "pRP", "bo0")
            v.tensor_copy(boRep[:, 0:512], pb[5][:])
            em.ms(v, "vRP", "bo0ev")
            em.wait(v, "pRP", "bo1")
            v.tensor_copy(boRep[:, 512:1024], pb[4][:])
            em.wait(v, "pRP", "eb")
            v.tensor_copy(ebRep[:], pb[5][:, 0:NE])
            em.ms(v, "vRP", "boeb")
            em.wait(v, "pRP", "cent")
            v.tensor_copy(ct_bf[:], pbT[:, 0:P])
            em.ms(v, "vRP", "ctev")

    def h_scalar(em, sc, it):
        tag = it[0]
        t = it[1] if len(it) > 1 else None
        i = t % 2 if isinstance(t, int) else None
        if tag == "a1_a_rs":
            i4 = t % 4 if NT % 4 == 0 else t % 2
            em.wait(sc, "vA1", (t, "st"))
            if t >= 4:
                em.wait(sc, "vA1", (t - 4, "hb"))
            sc.activation(lv4[i4][:], ag4[i4][:, 1:2], AF.Ln, bias=epsc[:])
            sc.drain()
            sc.activation(rs4[i4][:], lv4[i4][:], AF.Exp, scale=-0.5)
            em.ms(sc, "aA1", (t, "rs"))
        elif tag == "a1_a_ek":
            em.wait(sc, "pA1", (t, "mm_k"))
            if t >= 2:
                em.wait(sc, "pA1", (t - 2, "mm_E"))
            sc.activation(ek[i][:], pb[0 if i == 0 else 4][:], AF.Exp)
            em.ms(sc, "aA1", (t, "expk"))
            if t + 4 < NT:
                em.dma(sc, f"dXh{(t + 4) % 4}", t + 4,
                       xA4[(t + 4) % 4][:, HID:D],
                       xin[(t + 4) * P:(t + 5) * P, HID:D])
        elif tag == "a1_a_eq":
            em.wait(sc, "pA1", (t, "mm_q1"))
            if t >= 2:
                em.wait(sc, "vA1", (t - 2, "qn2"))
            sc.activation(qe[i][:], pb[5][:], AF.Exp)
            em.ms(sc, "aA1", (t, "expq1"))
            if t >= 1:
                em.wait(sc, "vA1", (t - 1, "qn2"))
                em.dma(sc, f"dQS{(t - 1) % 2}", t - 1,
                       qscr[(t - 1) * P:t * P, :], qn[(t - 1) % 2][:])
        elif tag == "a2_a_rs1":
            em.wait(sc, "vA2", (t, "st1"))
            if t >= 2:
                em.wait(sc, "vA2", (t - 2, "y"))
            sc.activation(lv1[i][:], ag1[i][:, 1:2], AF.Ln, bias=epsc[:])
            sc.drain()
            sc.activation(rs1[i][:], lv1[i][:], AF.Exp, scale=-0.5)
            em.ms(sc, "aA2", (t, "rs1"))
        elif tag == "a2_a_rs2":
            em.wait(sc, "vA2", (t, "st2"))
            if t >= 2:
                em.wait(sc, "vA2", (t - 2, "ub"))
            sc.activation(lv2[i][:], ag2[i][:, 1:2], AF.Ln, bias=epsc[:])
            sc.drain()
            sc.activation(rs2[i][:], lv2[i][:], AF.Exp, scale=-0.5)
            em.ms(sc, "aA2", (t, "rs2"))
        elif tag == "a2_a_es":
            em.wait(sc, "vA2", (t, "scev"))
            if t >= 2:
                em.wait(sc, "vA2", (t - 2, "g"))
            sc.activation(esb[i][:], scb[i][:], AF.Exp, bias=ngm[i][:])
            em.ms(sc, "aA2", (t, "es"))
        elif tag == "finact":
            q, j = it[1], it[2]
            jp = j % 2
            em.wait(sc, "vFF", (q, j, "stF"))
            if (q, j) not in ((0, 0), (0, 1)):
                pj = (q, j - 2) if j >= 2 else (q - 1, j + QT - 2)
                em.wait(sc, "vFF", (pj[0], pj[1], "fout"))
            sc.activation(lvF[jp][:], agF[jp][:, 1:2], AF.Ln, bias=epsc[:])
            sc.drain()
            sc.activation(rsF[jp][:], lvF[jp][:], AF.Exp, scale=-0.5)
            em.ms(sc, "aFF", (q, j, "rsF"))

    def h_tensor(em, pe, it):
        tag = it[0]
        t = it[1] if len(it) > 1 else None
        i = t % 2 if isinstance(t, int) else None
        if tag == "a1_p_trhb":
            em.wait(pe, "vA1", (t, "hb"))
            if t == 0:
                em.wait(pe, "vRP", "ctev")
            if t >= 2:
                em.wait(pe, "vA1", (t - 2, "hTev"))
            i4 = t % 4 if NT % 4 == 0 else t % 2
            pT = pbT if t % 2 == 0 else pbT2
            last = None
            for kk in range(8):
                last = pe.transpose(pT[:, kk * P:(kk + 1) * P],
                                    hb4[i4][:, kk * P:(kk + 1) * P], eye[:])
            em.pe_ms(last, "pA1", (t, "trhb"))
        elif tag == "a1_p_k":
            i4 = t % 4 if NT % 4 == 0 else t % 2
            em.wait(pe, "vA1", (t, "hTev"))
            if t == 0:
                em.wait(pe, "dSW", "cen")
            if t == 1:
                em.wait(pe, "vRP", "boeb")
            if t >= 2:
                em.wait(pe, "aA1", (t - 2, "expk"))
            last = None
            for kk in range(8):
                last = pe.matmul(pb[0 if i == 0 else 4][:],
                                 hT4[i4][:, kk * P:(kk + 1) * P],
                                 wkb[:, kk * HID:(kk + 1) * HID],
                                 start=(kk == 0), stop=(kk == 7))
            em.pe_ms(last, "pA1", (t, "mm_k"))
        elif tag == "a1_p_v":
            i4 = t % 4 if NT % 4 == 0 else t % 2
            if t >= 1:
                em.wait(pe, "vA1", (t - 1, "vbev"))
            last = None
            for kk in range(8):
                last = pe.matmul(pb[1][:],
                                 hT4[i4][:, kk * P:(kk + 1) * P],
                                 wvb[:, kk * HID:(kk + 1) * HID],
                                 start=(kk == 0), stop=(kk == 7))
            em.pe_ms(last, "pA1", (t, "mm_v"))
        elif tag == "a1_p_q2":
            i4 = t % 4 if NT % 4 == 0 else t % 2
            if t == 0:
                em.wait(pe, "vRP", "boeb")
            if t >= 1:
                em.wait(pe, "aA1", (t - 1, "expq1"))
            last = None
            for kk in range(8):
                last = pe.matmul(pb[5][:],
                                 hT4[i4][:, kk * P:(kk + 1) * P],
                                 wqb[:, kk * HID:(kk + 1) * HID],
                                 start=(kk == 0), stop=(kk == 7))
            em.pe_ms(last, "pA1", (t, "mm_q1"))
        elif tag == "a1_p_E":
            em.wait(pe, "aA1", (t, "expk"))
            em.wait(pe, "vA1", (t, "vbev"))
            if t >= 2:
                em.wait(pe, "vA1", (t - 2, "Eacc"))
            last = None
            for h in range(HEADS):
                pbase = (h % 2) * 64
                fo = (h // 2) * 65
                last = pe.matmul(pb[2 + i][pbase:pbase + 64, fo:fo + 65],
                                 ek[i][:, h * DH:(h + 1) * DH],
                                 vbq[i][:, h * 65:(h + 1) * 65],
                                 start=True, stop=True,
                                 tile_position=(0, pbase))
            em.pe_ms(last, "pA1", (t, "mm_E"))
        elif tag == "a2_p_o2":
            bk, pv = it[2]
            em.wait(pe, f"dQT{t % 4}", (t, 3))
            if t == 0:
                em.wait(pe, "vMID", "ctx")
            em.wait(pe, pv[0], pv[1])
            last = None
            for h in range(HEADS):
                pbase = (h % 2) * 64
                last = pe.matmul(pb[bk][pbase:pbase + 64, (h // 2) * P:(h // 2 + 1) * P],
                                 ctx[pbase:pbase + 64, (h // 2) * DH:(h // 2 + 1) * DH],
                                 qT4[t % 4][pbase:pbase + 64, (h // 2) * P:(h // 2 + 1) * P],
                                 start=True, stop=True,
                                 tile_position=(pbase, pbase))
            em.pe_ms(last, "pA2", (t, "o2"))
        elif tag == "a2_p_wo":
            b0, pv0 = it[2]
            b1, pv1 = it[3]
            if t == 0:
                em.wait(pe, "dSW", "bs")
            em.wait(pe, "vA2", (t, "o2ev"))
            em.wait(pe, pv0[0], pv0[1])
            em.wait(pe, pv1[0], pv1[1])
            last = None
            for oh, bk in ((0, b0), (1, b1)):
                for jx in range(4):
                    last = pe.matmul(pb[bk][:], o2b[i][:, jx * P:(jx + 1) * P],
                                     wob[:, jx * D + oh * HID:jx * D + (oh + 1) * HID],
                                     start=(jx == 0), stop=(jx == 3))
                if oh == 0:
                    em.pe_ms(last, "pA2", (t, "wo0"))
            em.pe_ms(last, "pA2", (t, "wo"))
        elif tag == "a2_p_trub":
            em.wait(pe, "vA2", (t, "ub"))
            if t < 2:
                em.wait(pe, "vA1", (NT - 2 + t, "hTev"))
            else:
                em.wait(pe, "vA2", (t - 2, "gTev"))
            pT = pbT if t % 2 == 0 else pbT2
            last = None
            for kk in range(8):
                last = pe.transpose(pT[:, kk * P:(kk + 1) * P],
                                    ub[i][:, kk * P:(kk + 1) * P], eye[:])
            em.pe_ms(last, "pA2", (t, "trub"))
        elif tag == "a2_p_sc":
            bk, pv = it[2]
            em.wait(pe, "vA2", (t, "uTev"))
            em.wait(pe, pv[0], pv[1])
            last = None
            for kk in range(8):
                last = pe.matmul(pb[bk][:, 0:NE],
                                 uT[:, kk * NTOK + t * P:kk * NTOK + (t + 1) * P],
                                 ct_bf[:, kk * NE:(kk + 1) * NE],
                                 start=(kk == 0), stop=(kk == 7))
            em.pe_ms(last, "pA2", (t, "sc"))
        elif tag == "a2_p_trg":
            em.wait(pe, "vA2", (t, "g"))
            em.wait(pe, "vA2", (t, "uTev"))
            pT = pbT if t % 2 == 0 else pbT2
            last = pe.transpose(pT[0:NE, 0:P], gb[:], eye[:])
            em.pe_ms(last, "pA2", (t, "trg"))
        elif tag == "ffnp":
            q, s, oh, j = it[1], it[2], it[3], it[4]
            bk, pv = it[5]
            tt = q * QT + j
            em.wait(pe, "vA2", (tt, "gTev"))
            em.wait(pe, pv[0], pv[1])
            if s == 0:
                last = pe.matmul(pb[bk][:], gTx[:, tt * P:(tt + 1) * P],
                                 brx[:, oh * HID:(oh + 1) * HID],
                                 start=True, stop=False)
                for kx in range(4):
                    kk = 4 * oh + kx
                    last = pe.matmul(
                        pb[bk][:],
                        uT[:, kk * NTOK + tt * P:kk * NTOK + (tt + 1) * P],
                        eyeblk[:, kx * HID:(kx + 1) * HID],
                        start=False, stop=(kx == 3))
            else:
                par = unit_idx(q, s, oh) % 2
                if j == 0:
                    em.wait(pe, f"dWT{par}", (q, s, oh, 7))
                last = None
                for kk in range(8):
                    last = pe.matmul(
                        pb[bk][:],
                        uT[:, kk * NTOK + tt * P:kk * NTOK + (tt + 1) * P],
                        wrt[par][:, kk * HID:(kk + 1) * HID],
                        start=(kk == 0), stop=(kk == 7))
            em.pe_ms(last, "pF", (q, s, oh, j))
        elif tag == "setup_p":
            em.wait(pe, "dSU", "eb")
            em.wait(pe, "vST", "memset")
            i1 = pe.matmul(pb[5][:], onesf[0:1, :], bo_row[:, 0:512],
                           start=True, stop=True)
            em.pe_ms(i1, "pRP", "bo0")
            i2 = pe.matmul(pb[4][:], onesf[0:1, :], bo_row[:, 512:1024],
                           start=True, stop=True)
            em.pe_ms(i2, "pRP", "bo1")
            em.wait(pe, "vRP", "bo0ev")
            i3 = pe.matmul(pb[5][:, 0:NE], onesf[0:1, :], eb_row[:],
                           start=True, stop=True)
            em.pe_ms(i3, "pRP", "eb")
            em.wait(pe, "dSW", "cen")
            last = None
            for kk in range(8):
                last = pe.transpose(pbT[:, kk * NE:(kk + 1) * NE],
                                    cen_st[0:NE, kk * P:(kk + 1) * P],
                                    eye[0:NE, 0:NE])
            em.pe_ms(last, "pRP", "cent")

    def emit_sync(em, sync):
        em.dma(sync, "dSU", "eye", eye[:], eye_in[:])
        em.dma(sync, "dSU", "bo", bo_row[:], bo[:])
        em.dma(sync, "dSU", "eb", eb_row[:], ebi[:])
        xdepth = 4 if NT % 4 == 0 else 2
        for t in range(NT):
            if t >= xdepth:
                em.wait(sync, "vA1", (t - xdepth, "hb"))
            em.dma(sync, f"dX{t % xdepth}", t, xA4[t % xdepth][:, 0:HID],
                   xin[t * P:(t + 1) * P, 0:HID])
        em.wait(sync, "vA1", (NT - 1, "hb"))
        em.dma(sync, "dX20", 0, xA[0][:, 0:HID], xin[0:P, 0:HID])
        em.dma(sync, "dX2h0", 0, xA[0][:, HID:D], xin[0:P, HID:D])
        em.dma(sync, "dX21", 1, xA[1][:, 0:HID], xin[P:2 * P, 0:HID])
        em.dma(sync, "dX2h1", 1, xA[1][:, HID:D], xin[P:2 * P, HID:D])
        for c in range(min(4, NT)):
            lbl = c if c < 2 else NT - 4 + c
            em.wait(sync, f"dQS{c % 2}", lbl)
            for jx in range(4):
                em.dma(sync, f"dQT{c % 4}", (c, jx),
                       qT4[c % 4][:, jx * P:(jx + 1) * P],
                       qscr[c * P:(c + 1) * P, jx * P:(jx + 1) * P],
                       transpose=True)
        for it in flow_tail:
            h_sync(em, sync, it)
        em.wait(sync, "dOUT0", (NQ - 1, QT - 2))
        em.wait(sync, "dOUT1", (NQ - 1, QT - 1))

    def emit_gpsimd(em, gp):
        em.dma(gp, "dSW", "wq", wqb[:].rearrange("p (k c) -> p k c", k=8),
               wq.rearrange("(k p) c -> p k c", p=P))
        em.dma(gp, "dSW", "wk", wkb[:].rearrange("p (k c) -> p k c", k=8),
               wk.rearrange("(k p) c -> p k c", p=P))
        em.dma(gp, "dSW", "wv", wvb[:].rearrange("p (k c) -> p k c", k=8),
               wv.rearrange("(k p) c -> p k c", p=P))
        em.dma(gp, "dSW", "cen", cen_st[:], cen[:])
        em.wait(gp, "vA1", (min(4, NT - 1), "hb"))
        em.dma(gp, "dSW", "wo", wob[:].rearrange("p (k c) -> p k c", k=4),
               wo.rearrange("(k p) c -> p k c", p=P))
        em.dma(gp, "dSW", "br", brx[0:NE, :], brr[:])
        em.dma(gp, "dSW", "bs", brx[NE:NMAT, :], bss[:])
        for m in range(NMAT):
            src = wr[m] if m < NE else ws[m - NE]
            if m >= 1:
                em.wait(gp, "vA1", (min(2 * m + 3, NT - 1), "hb"))
            em.dma(gp, "dWC", m, wbf[m], src)
        if DEBUG:
            em.wait(gp, "dOUT0", (NQ - 1, QT - 2))
            em.wait(gp, "dOUT1", (NQ - 1, QT - 1))
            em.dma(gp, "dDBG", 0, dbg_E[:], E_sb[:])
            em.dma(gp, "dDBG", 1, dbg_ctx[:], ctx[:])
            em.dma(gp, "dDBG", 2, dbg_g[:], gAll[:])
            em.dma(gp, "dDBG", 3, dbg_y[:], yscr[:])
            em.dma(gp, "dDBG", 4, dbg_uT[:], uT[:])
            em.dma(gp, "dDBG", 6, dbg_racc[:], racc[:])
            em.dma(gp, "dDBG", 7, dbg_gTx[:], gTx[:])
            em.wait(gp, "dDBG", 7)

    def emit_vector(em, v):
        h_vector(em, v, ("setup_v",))
        for it in flow_a1:
            h_vector(em, v, it)
        h_vector(em, v, ("mid",))
        for it in flow_tail:
            h_vector(em, v, it)

    def emit_scalar(em, sc):
        for t0 in range(min(4, NT)):
            em.dma(sc, f"dXh{t0 % 4}", t0, xA4[t0 % 4][:, HID:D],
                   xin[t0 * P:(t0 + 1) * P, HID:D])
        for it in flow_a1:
            h_scalar(em, sc, it)
        em.wait(sc, "vA1", (NT - 1, "qn2"))
        em.dma(sc, f"dQS{(NT - 1) % 2}", NT - 1,
               qscr[(NT - 1) * P:NT * P, :], qn[(NT - 1) % 2][:])
        for it in flow_tail:
            h_scalar(em, sc, it)

    def emit_tensor(em, pe):
        h_tensor(em, pe, ("setup_p",))
        for it in flow_a1:
            h_tensor(em, pe, it)
        for it in flow_tail:
            h_tensor(em, pe, it)

    em = Em()
    dummy = _Dummy()
    for f in (emit_sync, emit_gpsimd, emit_tensor, emit_vector, emit_scalar):
        f(em, dummy)
    em.mode = "emit"
    em.cnt = {}
    for name in em.vals:
        em.sem[name] = nc.alloc_semaphore(name)
    with nc.Block() as block:
        @block.sync
        def _(sync):
            emit_sync(em, sync)

        @block.gpsimd
        def _(gp):
            emit_gpsimd(em, gp)

        @block.tensor
        def _(pe):
            emit_tensor(em, pe)

        @block.vector
        def _(v):
            emit_vector(em, v)

        @block.scalar
        def _(sc):
            emit_scalar(em, sc)

    return nc


def kernel(L0_attn_x, L0_attn_Wq, L0_attn_Wk, L0_attn_Wv, L0_attn_Wo,
           L0_attn_bo, L0_ffn_centroids, L0_ffn_expert_biases, L0_ffn_Wr,
           L0_ffn_br, L0_ffn_Ws, L0_ffn_bs):
    x = np.ascontiguousarray(np.asarray(L0_attn_x, dtype=np.float32))
    com = {
        "wq": np.ascontiguousarray(np.asarray(L0_attn_Wq, np.float32)),
        "wk": np.ascontiguousarray(np.asarray(L0_attn_Wk, np.float32)),
        "wv": np.ascontiguousarray(np.asarray(L0_attn_Wv, np.float32)),
        "wo": np.ascontiguousarray(np.asarray(L0_attn_Wo, np.float32)),
        "bo": np.ascontiguousarray(np.asarray(L0_attn_bo, np.float32).reshape(1, D)),
        "cen": np.ascontiguousarray(np.asarray(L0_ffn_centroids, np.float32)),
        "ebi": np.ascontiguousarray(np.asarray(L0_ffn_expert_biases, np.float32).reshape(1, NE)),
        "wr": np.ascontiguousarray(np.asarray(L0_ffn_Wr, np.float32)),
        "brr": np.ascontiguousarray(np.asarray(L0_ffn_br, np.float32)),
        "ws": np.ascontiguousarray(np.asarray(L0_ffn_Ws, np.float32)),
        "bss": np.ascontiguousarray(np.asarray(L0_ffn_bs, np.float32)),
        "eye_in": np.eye(P, dtype=ml_dtypes.bfloat16),
    }
    nc = build_nc()
    in_maps = [dict(com, xin=np.ascontiguousarray(x[b])) for b in range(N_CORES)]
    res = run_bass_kernel_spmd(nc, in_maps, core_ids=list(range(N_CORES)))
    return np.stack([r["out"] for r in res.results], axis=0)


# version 20
# speedup vs baseline: 1.1723x; 1.0242x over previous
import numpy as np
import ml_dtypes
import concourse.bass as bass
import concourse.mybir as mybir
from concourse.bass_utils import run_bass_kernel_spmd

F32 = mybir.dt.float32
BF16 = mybir.dt.bfloat16
U32 = mybir.dt.uint32
AF = mybir.ActivationFunctionType
ALU = mybir.AluOpType
AX = mybir.AxisListType

N_CORES = 8
NTOK = 4096
D = 1024
P = 128
NT = NTOK // P
HEADS = 8
DH = 64
HID = 512
SCALE = DH ** -0.5
NE = 16
NSH = 2
NMAT = NE + NSH
NSTEP = 1 + NMAT
NQ = 4
QT = NT // NQ
EPS = 1e-5
DEBUG = False
PHASES = 'ALL'


class _Dummy:
    def __getattr__(self, k):
        return lambda *a, **kw: self

    def __call__(self, *a, **kw):
        return self


class Em:

    def __init__(self):
        self.vals = {}
        self.cnt = {}
        self.mode = "count"
        self.sem = {}

    def _bump(self, chan, label, k):
        c = self.cnt.get(chan, 0) + k
        self.cnt[chan] = c
        d = self.vals.setdefault(chan, {})
        if self.mode == "count":
            assert label not in d, f"dup label {chan}/{label}"
            d[label] = c
        else:
            assert d.get(label) == c, f"label drift {chan}/{label}: {d.get(label)} != {c}"
        return c

    def ms(self, eng, chan, label):
        self._bump(chan, label, 1)
        if self.mode == "emit":
            eng.drain().then_inc(self.sem[chan], 1)

    def pe_ms(self, instr, chan, label):
        self._bump(chan, label, 1)
        if self.mode == "emit":
            instr.then_inc(self.sem[chan], 1)

    def op_ms(self, instr, chan, label):
        self._bump(chan, label, 1)
        if self.mode == "emit":
            instr.then_inc(self.sem[chan], 1)

    def dma(self, eng, chan, label, out, in_, **kw):
        self._bump(chan, label, 16)
        if self.mode == "emit":
            eng.dma_start(out=out, in_=in_, **kw).then_inc(self.sem[chan], 16)

    def wait(self, eng, chan, label):
        if self.mode == "emit":
            eng.wait_ge(self.sem[chan], self.vals[chan][label])


def build_nc():
    nc = bass.Bass("TRN2", target_bir_lowering=False, debug=False)

    xin = nc.dram_tensor("xin", [NTOK, D], F32, kind="ExternalInput").ap()
    wq = nc.dram_tensor("wq", [D, HID], F32, kind="ExternalInput").ap()
    wk = nc.dram_tensor("wk", [D, HID], F32, kind="ExternalInput").ap()
    wv = nc.dram_tensor("wv", [D, HID], F32, kind="ExternalInput").ap()
    wo = nc.dram_tensor("wo", [HID, D], F32, kind="ExternalInput").ap()
    bo = nc.dram_tensor("bo", [1, D], F32, kind="ExternalInput").ap()
    cen = nc.dram_tensor("cen", [NE, D], F32, kind="ExternalInput").ap()
    ebi = nc.dram_tensor("ebi", [1, NE], F32, kind="ExternalInput").ap()
    wr = nc.dram_tensor("wr", [NE, D, D], F32, kind="ExternalInput").ap()
    brr = nc.dram_tensor("brr", [NE, D], F32, kind="ExternalInput").ap()
    ws = nc.dram_tensor("ws", [NSH, D, D], F32, kind="ExternalInput").ap()
    bss = nc.dram_tensor("bss", [NSH, D], F32, kind="ExternalInput").ap()
    eye_in = nc.dram_tensor("eye_in", [P, P], BF16, kind="ExternalInput").ap()
    out = nc.dram_tensor("out", [NTOK, D], F32, kind="ExternalOutput").ap()
    wbf = nc.dram_tensor("wbf", [NMAT, D, D], BF16, kind="Internal").ap()
    yscr = nc.dram_tensor("yscr", [NTOK, D], F32, kind="Internal").ap()
    if DEBUG:
        dbg_E = nc.dram_tensor("dbg_E", [P, 4 * 65], F32, kind="ExternalOutput").ap()
        dbg_ctx = nc.dram_tensor("dbg_ctx", [P, 4 * DH], F32, kind="ExternalOutput").ap()
        dbg_g = nc.dram_tensor("dbg_g", [P, NT * NE], F32, kind="ExternalOutput").ap()
        dbg_y = nc.dram_tensor("dbg_y", [NTOK, D], F32, kind="ExternalOutput").ap()
        dbg_uT = nc.dram_tensor("dbg_uT", [P, 8 * NTOK], F32, kind="ExternalOutput").ap()
        dbg_racc = nc.dram_tensor("dbg_racc", [P, QT * D], F32, kind="ExternalOutput").ap()
        dbg_gTx = nc.dram_tensor("dbg_gTx", [NMAT, NTOK], F32, kind="ExternalOutput").ap()

    hscr = nc.dram_tensor("hscr", [NTOK, D], BF16, kind="Internal").ap()

    A = nc.alloc_sbuf_tensor
    eye = A("eye", [P, P], BF16)
    onesf = A("onesf", [4, P], F32)
    boRep = A("boRep", [P, D], F32)
    ebRep = A("ebRep", [P, NE], F32)
    ct_bf = A("ct_bf", [P, P], BF16)
    brx = A("brx", [NMAT, D], BF16)
    gTx = A("gTx", [NMAT, NTOK], BF16)
    uT = A("uT", [P, 8 * NTOK], BF16)
    gAll = A("gAll", [P, NT * NE], F32)
    halfc = A("halfc", [P, 1], F32)
    neg3e4 = A("neg3e4", [P, NE], F32)
    zero16 = A("zero16", [P, NE], F32)
    gb = A("gb", [P, NE], BF16)
    epsc = A("epsc", [P, 1], F32)
    eyeblk = A("eyeblk", [P, 4 * HID], BF16)
    E_sb = A("E_sb", [P, 4 * 65], F32)
    sInv = A("sInv", [P, 4], F32)
    ctx = A("ctx", [P, 4 * DH], BF16)

    def stat4(nm):
        return ([A(f"{nm}bn{i}", [P, 12], F32) for i in range(2)],
                [A(f"{nm}ag{i}", [P, 2], F32) for i in range(2)],
                [A(f"{nm}lv{i}", [P, 1], F32) for i in range(2)],
                [A(f"{nm}rs{i}", [P, 1], F32) for i in range(2)])

    bnA, agA, lvA, rsA = stat4("A")
    bn0, ag0, lv0, rs0 = stat4("B")
    ag4, lv4, rs4 = agA + ag0, lvA + lv0, rsA + rs0
    bn1, ag1, lv1, rs1 = stat4("C")
    bn2, ag2, lv2, rs2 = stat4("D")
    bnF, agF, lvF, rsF = stat4("E")

    qs = [A(f"qs{i}", [P, HEADS], F32) for i in range(2)]
    rq = [A(f"rq{i}", [P, HEADS], F32) for i in range(2)]
    scb = [A(f"scb{i}", [P, NE], F32) for i in range(2)]
    esb = [A(f"esb{i}", [P, NE], F32) for i in range(2)]
    ngm = [A(f"ngm{i}", [P, 1], F32) for i in range(2)]
    rsm = [A(f"rsm{i}", [P, 1], F32) for i in range(2)]
    mx1 = [A(f"mx1{i}", [P, 1], F32) for i in range(2)]
    m2v = [A(f"m2v{i}", [P, 1], F32) for i in range(2)]
    mk1 = [A(f"mk1{i}", [P, NE], U32) for i in range(2)]
    mk2 = [A(f"mk2{i}", [P, NE], U32) for i in range(2)]
    sm2 = [A(f"sm2{i}", [P, NE], F32) for i in range(2)]
    gsl = [A(f"gsl{i}", [P, NE], F32) for i in range(2)]
    ssm = [A(f"ssm{i}", [P, 1], F32) for i in range(2)]

    wqb = A("wqb", [P, 8 * HID], BF16)
    wob = A("wob", [P, 4 * D], BF16)
    xA = [A(f"xA{i}", [P, D], F32) for i in range(2)]
    hb = [A(f"hb{i}", [P, D], BF16) for i in range(2)]
    hT = [A(f"hT{i}", [P, D], BF16) for i in range(2)]
    qe = [A(f"qe{i}", [P, HID], F32) for i in range(2)]
    qn = [A(f"qn{i}", [P, HID], BF16) for i in range(2)]
    qTt = [A(f"qTt{i}", [P, HID], BF16) for i in range(2)]
    o2b = [A(f"o2b{i}", [P, HID], BF16) for i in range(2)]
    att = [A(f"att{i}", [P, D], F32) for i in range(2)]
    ub = [A(f"ub{i}", [P, D], BF16) for i in range(2)]
    wrt = [A(f"wrt{i}", [P, 8 * HID], BF16, is_transpose_dest=True) for i in range(2)]
    _rs, _re = nc.bump_sbuf(max(QT * D * 4, 32768), "left", align_bytes=128)
    racc = nc.alloc_sbuf_tensor_at("racc", [P, QT * D], F32, offset=_rs)
    wkb = nc.alloc_sbuf_tensor_at("wkb", [P, 8 * HID], BF16, offset=_rs)
    wvb = nc.alloc_sbuf_tensor_at("wvb", [P, 8 * HID], BF16, offset=_rs + 8 * HID * 2)
    xA4 = xA + [nc.alloc_sbuf_tensor_at(f"xA{i}", [P, D], F32,
                offset=_rs + 16384 + (i - 2) * 4096) for i in (2, 3)]
    hT4 = hT + [nc.alloc_sbuf_tensor_at(f"hT{i}", [P, D], BF16,
                offset=_rs + 24576 + (i - 2) * 2048) for i in (2, 3)]
    hb4 = hb + [nc.alloc_sbuf_tensor_at(f"hb{i}", [P, D], BF16,
                offset=_rs + 28672 + (i - 2) * 2048) for i in (2, 3)]
    _fs, _fe = nc.bump_sbuf(16384, "left", align_bytes=128)
    AT = nc.alloc_sbuf_tensor_at
    yld = [AT(f"yld{i}", [P, D], F32, offset=_fs + i * 4096) for i in range(2)]
    ftmp = [AT(f"ftmp{i}", [P, D], F32, offset=_fs + 8192 + i * 4096) for i in range(2)]
    bo_row = AT("bo_row", [1, D], F32, offset=_fs)
    cen_st = AT("cen_st", [NE, D], BF16, offset=_fs + 4096)
    eb_row = AT("eb_row", [1, NE], F32, offset=_fs + 6144)
    ek = [AT(f"ek{i}", [P, HID], BF16, offset=_fs + 6272 + i * 1024) for i in range(2)]
    vbq = [AT(f"vb{i}", [P, 8 * 65], BF16, offset=_fs + 8352 + i * 1056) for i in range(2)]

    PA = nc.alloc_psum_tensor
    pbT = PA("pbT", [P, 1024], BF16)
    pbT2 = PA("pbT2", [P, 1024], BF16)
    pb = [PA(f"pb{i}", [P, 512], F32) for i in range(6)]

    def a1_items(t):
        return [("a1_v_st", t), ("a1_a_rs", t), ("a1_v_hb", t),
                ("a1_p_trhb", t), ("a1_v_hT", t), ("a1_p_k", t),
                ("a1_a_ek", t), ("a1_p_v", t), ("a1_v_vb", t),
                ("a1_p_E", t), ("a1_v_Eacc", t)]

    def zip_merge(la, lb):
        outl = []
        na, nb = len(la), len(lb)
        ia = ib = 0
        while ia < na or ib < nb:
            if ia < na:
                outl.append(la[ia]); ia += 1
            if ib < nb:
                outl.append(lb[ib]); ib += 1
        return outl

    flow_a1 = []
    if NT % 4 == 0:
        for t in range(0, NT, 4):
            lists = [a1_items(t + d) for d in range(4)]
            n = len(lists[0])
            for r in range(n):
                for lst in lists:
                    flow_a1.append(lst[r])
    else:
        for t in range(0, NT, 2):
            flow_a1 += zip_merge(a1_items(t), a1_items(t + 1))

    def a2_items(t):
        return [("a2_p_q", t),
                ("a2_a_expq", t), ("a2_v_qn", t), ("a2_p_trqn", t),
                ("a2_v_qT", t), ("a2_p_o2", t), ("a2_v_o2ev", t),
                ("a2_p_wo", t), ("a2_v_att", t), ("a2_a_rs1", t),
                ("a2_v_y", t), ("a2_a_rs2", t), ("a2_v_ub", t),
                ("a2_p_trub", t), ("a2_v_uT", t), ("a2_p_sc", t),
                ("a2_v_scev", t), ("a2_a_es", t), ("a2_v_g", t),
                ("a2_p_trg", t), ("a2_v_gT", t)]

    units = [(q, s, oh) for q in range(NQ) for s in range(1, NSTEP)
             for oh in range(2)]

    def unit_idx(q, s, oh):
        return (q * (NSTEP - 1) + (s - 1)) * 2 + oh

    def ffn_quarter(q):
        its = []
        for oh in range(2):
            for j in range(QT):
                its.append(("ffnp", q, 0, oh, j))
                its.append(("ffnv", q, 0, oh, j))
        for s in range(1, NSTEP):
            for oh in range(2):
                k = unit_idx(q, s, oh)
                if k + 2 < len(units):
                    its.append(("ffnl",) + units[k + 2])
                for j in range(QT):
                    its.append(("ffnp", q, s, oh, j))
                    its.append(("ffnv", q, s, oh, j))
        return its

    def fin_quarter(q):
        its = []
        for j in range(QT):
            its.append(("fina", q, j))
            its.append(("finact", q, j))
            if j >= 1:
                its.append(("finb", q, j - 1))
        its.append(("finb", q, QT - 1))
        return its

    def prop_merge(la, lb):
        outl = []
        na, nb = len(la), len(lb)
        if na == 0:
            return list(lb)
        ratio = nb / na
        ib = 0
        for ia in range(na):
            outl.append(la[ia])
            want = int(round((ia + 1) * ratio * 0.65))
            while ib < want:
                outl.append(lb[ib]); ib += 1
        outl.extend(lb[ib:])
        return outl

    flow_tail = []
    flow_tail.append(("ffnl",) + units[0])
    flow_tail.append(("ffnl",) + units[1])
    for w in range(NQ + 2):
        la = []
        if w < NQ:
            for t in range(QT * w, QT * w + QT, 2):
                la += zip_merge(a2_items(t), a2_items(t + 1))
        lb = []
        if 2 <= w:
            lb += fin_quarter(w - 2)
        if 1 <= w <= NQ:
            lb += ffn_quarter(w - 1)
        flow_tail += prop_merge(la, lb)

    pool = [
        [0, ("aA1", (NT - 2, "expk")), -1],
        [1, ("vA1", (NT - 2, "vbev")), -1],
        [2, ("vA1", (NT - 2, "Eacc")), -1],
        [3, ("vA1", (NT - 1, "Eacc")), -1],
        [4, ("aA1", (NT - 1, "expk")), -1],
        [5, ("vA1", (NT - 1, "vbev")), -1],
    ]
    _needs = {"a2_p_q": ("aA2", "expq", 1), "a2_p_o2": ("vA2", "o2ev", 1),
              "a2_p_wo": ("vA2", "att", 2), "a2_p_sc": ("vA2", "scev", 1)}
    _evac_tag = {"a2_p_q": "a2_a_expq", "a2_p_o2": "a2_v_o2ev",
                 "a2_p_wo": "a2_v_att", "a2_p_sc": "a2_v_scev"}
    evac_pos = {}
    for pos, it in enumerate(flow_tail):
        if it[0] in ("a2_a_expq", "a2_v_o2ev", "a2_v_att", "a2_v_scev"):
            evac_pos[(it[0], it[1])] = pos
        elif it[0] == "ffnv":
            evac_pos[("ffnv", it[1], it[2], it[3], it[4])] = pos

    def pool_take(pos, n):
        got = []
        for _ in range(n):
            cands = [e for e in pool if e[2] < pos]
            assert cands, f"no reusable psum bank at flow pos {pos}"
            e = min(cands, key=lambda x: x[2])
            pool.remove(e)
            got.append(e)
        return got

    pend = {}
    flow2 = []
    for pos, it in enumerate(flow_tail):
        tag = it[0]
        if tag in _needs:
            t = it[1]
            chan, lbl, nbk = _needs[tag]
            got = pool_take(pos, nbk)
            ep = evac_pos[(_evac_tag[tag], t)]
            res = []
            for bk, pv, _e in got:
                res.append((bk, pv))
                pool.append([bk, (chan, (t, lbl)), ep])
            pend[(tag, t)] = res
            flow2.append(it + tuple(res))
        elif tag == "a2_a_expq":
            flow2.append(it + (pend[("a2_p_q", it[1])][0][0],))
        elif tag == "a2_v_o2ev":
            flow2.append(it + (pend[("a2_p_o2", it[1])][0][0],))
        elif tag == "a2_v_att":
            g = pend[("a2_p_wo", it[1])]
            flow2.append(it + (g[0][0], g[1][0]))
        elif tag == "a2_v_scev":
            flow2.append(it + (pend[("a2_p_sc", it[1])][0][0],))
        elif tag == "ffnp":
            q, s, oh, j = it[1], it[2], it[3], it[4]
            got = pool_take(pos, 1)[0]
            ep = evac_pos[("ffnv", q, s, oh, j)]
            pool.append([got[0], ("vF", (q, s, oh, j)), ep])
            pend[("f", q, s, oh, j)] = (got[0], got[1])
            flow2.append(it + ((got[0], got[1]),))
        elif tag == "ffnv":
            q, s, oh, j = it[1], it[2], it[3], it[4]
            flow2.append(it + (pend[("f", q, s, oh, j)][0],))
        else:
            flow2.append(it)
    flow_tail = flow2

    def h_sync(em, sync, it):
        tag = it[0]
        if tag == "a2_v_y":
            t = it[1]; i = t % 2
            em.wait(sync, "vA2", (t, "y"))
            em.dma(sync, f"dY{i}", t, yscr[t * P:(t + 1) * P, :], att[i][:])
            if t + 2 < NT:
                em.dma(sync, f"dX2{i}", t + 2, xA[i][:, 0:HID],
                       xin[(t + 2) * P:(t + 3) * P, 0:HID])
                em.dma(sync, f"dX2h{i}", t + 2, xA[i][:, HID:D],
                       xin[(t + 2) * P:(t + 3) * P, HID:D])
            if t + 2 < NT:
                em.wait(sync, "pA2", (t, "mm_q"))
                em.wait(sync, f"dHS{(t + 2) % 4}", t + 2)
                em.dma(sync, f"dHL{(t + 2) % 2}", t + 2, hT[(t + 2) % 2][:],
                       hscr[(t + 2) * P:(t + 3) * P, :])
        elif tag == "ffnl":
            q, s, oh = it[1], it[2], it[3]
            k = unit_idx(q, s, oh)
            par = k % 2
            if k >= 2:
                pq, ps, po = units[k - 2]
                em.wait(sync, "pF", (pq, ps, po, QT - 1))
            em.wait(sync, "dWC", NMAT - 1)
            for kk in range(8):
                em.dma(sync, f"dWT{par}", (q, s, oh, kk),
                       wrt[par][:, kk * HID:(kk + 1) * HID],
                       wbf[s - 1, oh * HID:(oh + 1) * HID, kk * P:(kk + 1) * P],
                       transpose=True)
        elif tag == "fina":
            q, j = it[1], it[2]
            t = q * QT + j
            em.wait(sync, f"dY{t % 2}", t)
            if (q, j) not in ((0, 0), (0, 1)):
                pj = (q, j - 2) if j >= 2 else (q - 1, j + QT - 2)
                em.wait(sync, "vFF", (pj[0], pj[1], "fout"))
            em.dma(sync, f"dYL{j % 2}", (q, j), yld[j % 2][:],
                   yscr[t * P:(t + 1) * P, :])
        elif tag == "finb":
            q, j = it[1], it[2]
            t = q * QT + j
            em.wait(sync, "vFF", (q, j, "fout"))
            em.dma(sync, f"dOUT{j % 2}", (q, j), out[t * P:(t + 1) * P, :],
                   ftmp[j % 2][:])

    def ln_stats(v, bn, ag, src):
        v.bn_stats(bn[:, 0:6], src[:, 0:512])
        v.bn_stats(bn[:, 6:12], src[:, 512:1024])
        v.drain()
        v.bn_aggr(ag[:], bn[:, 0:12])

    def h_vector(em, v, it):
        tag = it[0]
        t = it[1] if len(it) > 1 else None
        i = t % 2 if isinstance(t, int) else None
        if tag == "a1_v_st":
            i4 = t % 4 if NT % 4 == 0 else t % 2
            em.wait(v, f"dX{i4}", t)
            em.wait(v, f"dXh{i4}", t)
            if t >= 4:
                em.wait(v, "aA1", (t - 4, "rs"))
            ln_stats(v, bnA[i], ag4[i4], xA4[i4])
            em.ms(v, "vA1", (t, "st"))
        elif tag == "a1_v_hb":
            i4 = t % 4 if NT % 4 == 0 else t % 2
            em.wait(v, "aA1", (t, "rs"))
            if t >= 4:
                em.wait(v, "pA1", (t - 4, "trhb"))
            v.tensor_scalar(hb4[i4][:], xA4[i4][:], ag4[i4][:, 0:1], rs4[i4][:],
                            ALU.subtract, ALU.mult)
            em.ms(v, "vA1", (t, "hb"))
        elif tag == "a1_v_hT":
            i4 = t % 4 if NT % 4 == 0 else t % 2
            em.wait(v, "pA1", (t, "trhb"))
            if t >= 4:
                em.wait(v, "pA1", (t - 4, "mm_v"))
                em.wait(v, f"dHS{t % 4}", t - 4)
            pT = pbT if t % 2 == 0 else pbT2
            v.tensor_copy(hT4[i4][:].rearrange("p (k c) -> p k c", k=8),
                          pT[:].rearrange("p (k c) -> p k c", k=8))
            em.ms(v, "vA1", (t, "hTev"))
        elif tag == "a1_v_vb":
            em.wait(v, "pA1", (t, "mm_v"))
            if t >= 2:
                em.wait(v, "pA1", (t - 2, "mm_E"))
            v.tensor_copy(vbq[i][:].rearrange("p (h c) -> p h c", c=65)[:, :, 0:64],
                          pb[1 if i == 0 else 5][:].rearrange("p (h c) -> p h c", c=64))
            em.ms(v, "vA1", (t, "vbev"))
        elif tag == "a1_v_Eacc":
            em.wait(v, "pA1", (t, "mm_E"))
            v.drain()
            ii = v.tensor_tensor(E_sb[:], E_sb[:], pb[2 + i][:, 0:260], ALU.add)
            em.op_ms(ii, "vA1", (t, "Eacc"))
        elif tag == "mid":
            v.drain()
            for h in range(HEADS):
                pbase, fo = (h % 2) * 64, (h // 2) * 65
                v.reciprocal(sInv[pbase:pbase + 64, h // 2:h // 2 + 1],
                             E_sb[pbase:pbase + 64, fo + 64:fo + 65])
            v.drain()
            for h in range(HEADS):
                pbase, fo = (h % 2) * 64, (h // 2) * 65
                v.tensor_scalar(ctx[pbase:pbase + 64, (h // 2) * DH:(h // 2 + 1) * DH],
                                E_sb[pbase:pbase + 64, fo:fo + 64],
                                sInv[pbase:pbase + 64, h // 2:h // 2 + 1],
                                None, ALU.mult)
            em.ms(v, "vMID", "ctx")
        elif tag == "a2_v_qn":
            em.wait(v, "aA2", (t, "expq"))
            v.reduce_sum(qs[i][:].rearrange("p (h o) -> p h o", o=1),
                         qe[i][:].rearrange("p (h c) -> p h c", h=HEADS),
                         axis=AX.X)
            v.drain()
            v.reciprocal(rq[i][:], qs[i][:])
            v.drain()
            if t >= 2:
                em.wait(v, "pA2", (t - 2, "trqn"))
            for h in range(HEADS):
                v.tensor_scalar(qn[i][:, h * DH:(h + 1) * DH],
                                qe[i][:, h * DH:(h + 1) * DH],
                                rq[i][:, h:h + 1], SCALE, ALU.mult, ALU.mult)
            em.ms(v, "vA2", (t, "qn"))
        elif tag == "a2_v_qT":
            em.wait(v, "pA2", (t, "trqn"))
            if t >= 2:
                em.wait(v, "pA2", (t - 2, "o2"))
            pT = pbT if t % 2 == 0 else pbT2
            v.tensor_copy(qTt[i][:], pT[:, 0:512])
            em.ms(v, "vA2", (t, "qTev"))
        elif tag == "a2_v_o2ev":
            em.wait(v, "pA2", (t, "o2"))
            if t >= 2:
                em.wait(v, "pA2", (t - 2, "wo"))
            v.tensor_copy(o2b[i][:], pb[it[2]][:])
            em.ms(v, "vA2", (t, "o2ev"))
        elif tag == "a2_v_att":
            em.wait(v, "pA2", (t, "wo0"))
            if t == 0:
                em.wait(v, "vRP", "boeb")
            if t >= 2:
                em.wait(v, "vA2", (t - 2, "ub"))
                em.wait(v, f"dY{i}", t - 2)
            v.tensor_tensor(att[i][:, 0:512], pb[it[2]][:], boRep[:, 0:512], ALU.add)
            em.wait(v, "pA2", (t, "wo"))
            ii = v.tensor_tensor(att[i][:, 512:1024], pb[it[3]][:],
                                 boRep[:, 512:1024], ALU.add)
            em.op_ms(ii, "vA2", (t, "att"))
            v.drain()
            if t >= 2:
                em.wait(v, "aA2", (t - 2, "rs1"))
            ln_stats(v, bn1[i], ag1[i], att[i])
            em.ms(v, "vA2", (t, "st1"))
        elif tag == "a2_v_y":
            em.wait(v, "aA2", (t, "rs1"))
            v.tensor_scalar(att[i][:], att[i][:], ag1[i][:, 0:1], rs1[i][:],
                            ALU.subtract, ALU.mult)
            v.drain()
            em.wait(v, f"dX2{i}", t)
            em.wait(v, f"dX2h{i}", t)
            v.tensor_tensor(att[i][:], att[i][:], xA[i][:], ALU.add)
            em.ms(v, "vA2", (t, "y"))
            if t >= 2:
                em.wait(v, "aA2", (t - 2, "rs2"))
            ln_stats(v, bn2[i], ag2[i], att[i])
            em.ms(v, "vA2", (t, "st2"))
        elif tag == "a2_v_ub":
            em.wait(v, "aA2", (t, "rs2"))
            if t >= 2:
                em.wait(v, "pA2", (t - 2, "trub"))
            v.tensor_scalar(ub[i][:], att[i][:], ag2[i][:, 0:1], rs2[i][:],
                            ALU.subtract, ALU.mult)
            em.ms(v, "vA2", (t, "ub"))
        elif tag == "a2_v_uT":
            em.wait(v, "pA2", (t, "trub"))
            pT = pbT if t % 2 == 0 else pbT2
            v.tensor_copy(uT[:].rearrange("p (k n) -> p k n", k=8)[:, :, t * P:(t + 1) * P],
                          pT[:].rearrange("p (k n) -> p k n", k=8))
            em.ms(v, "vA2", (t, "uTev"))
        elif tag == "a2_v_scev":
            em.wait(v, "pA2", (t, "sc"))
            if t >= 2:
                em.wait(v, "aA2", (t - 2, "es"))
            v.tensor_tensor(scb[i][:], pb[it[2]][:, 0:NE], ebRep[:], ALU.add)
            v.drain()
            v.reduce_max(ngm[i][:], scb[i][:], axis=AX.X, negate=True)
            em.ms(v, "vA2", (t, "scev"))
        elif tag == "a2_v_g":
            em.wait(v, "aA2", (t, "es"))
            v.reduce_sum(ssm[i][:], esb[i][:], axis=AX.X)
            v.drain()
            v.reciprocal(rsm[i][:], ssm[i][:])
            v.reduce_max(mx1[i][:], esb[i][:], axis=AX.X)
            v.drain()
            v.tensor_scalar(mk1[i][:], esb[i][:], mx1[i][:], None, ALU.is_ge)
            v.drain()
            v.select(sm2[i][:], mk1[i][:], neg3e4[:], esb[i][:], add_drain=True)
            v.drain()
            v.reduce_max(m2v[i][:], sm2[i][:], axis=AX.X)
            v.drain()
            v.tensor_scalar(mk2[i][:], esb[i][:], m2v[i][:], None, ALU.is_ge)
            v.drain()
            v.select(gsl[i][:], mk2[i][:], esb[i][:], zero16[:], add_drain=True)
            v.drain()
            v.tensor_scalar(gAll[:, t * NE:(t + 1) * NE], gsl[i][:],
                            rsm[i][:], None, ALU.mult)
            v.drain()
            if t >= 1:
                em.wait(v, "pA2", (t - 1, "trg"))
            v.tensor_copy(gb[:], gAll[:, t * NE:(t + 1) * NE])
            em.ms(v, "vA2", (t, "g"))
        elif tag == "a2_v_gT":
            em.wait(v, "pA2", (t, "trg"))
            pT = pbT if t % 2 == 0 else pbT2
            v.tensor_copy(gTx[0:NE, t * P:(t + 1) * P], pT[0:NE, 0:P])
            em.ms(v, "vA2", (t, "gTev"))
        elif tag == "ffnv":
            q, s, oh, j, bk = it[1], it[2], it[3], it[4], it[5]
            tt = q * QT + j
            em.wait(v, "pF", (q, s, oh, j))
            if q == 0 and s == 0:
                em.wait(v, "pA1", (NT - 1, "mm_E"))
                for tl in range(max(0, NT - 4), NT):
                    em.wait(v, f"dHS{tl % 4}", tl)
            if s >= 1:
                em.wait(v, "vF", (q, s - 1, oh, j))
            elif q >= 1:
                em.wait(v, "vFF", (q - 1, j, "fout"))
            rsl = racc[:, j * D + oh * HID:j * D + (oh + 1) * HID]
            if s == 0:
                ii = v.tensor_copy(rsl, pb[bk][:])
            else:
                e = s - 1
                gcol = (gAll[:, tt * NE + e:tt * NE + e + 1]
                        if e < NE else halfc[:])
                ii = v.scalar_tensor_tensor(rsl, pb[bk][:], gcol, rsl,
                                            ALU.mult, ALU.add)
            em.op_ms(ii, "vF", (q, s, oh, j))
        elif tag == "fina":
            q, j = it[1], it[2]
            jp = j % 2
            em.wait(v, "pF", (q, NSTEP - 1, 1, j))
            if q == 0 and j < 2:
                em.wait(v, "pA1", (NT - 1, "mm_E"))
            if (q, j) not in ((0, 0), (0, 1)):
                pj = (q, j - 2) if j >= 2 else (q - 1, j + QT - 2)
                em.wait(v, "aFF", (pj[0], pj[1], "rsF"))
            v.drain()
            ln_stats(v, bnF[jp], agF[jp], racc[:, j * D:(j + 1) * D])
            em.ms(v, "vFF", (q, j, "stF"))
        elif tag == "finb":
            q, j = it[1], it[2]
            jp = j % 2
            em.wait(v, "aFF", (q, j, "rsF"))
            em.wait(v, f"dYL{jp}", (q, j))
            if (q, j) not in ((0, 0), (0, 1)):
                pj = (q, j - 2) if j >= 2 else (q - 1, j + QT - 2)
                em.wait(v, f"dOUT{jp}", pj)
            v.tensor_scalar(ftmp[jp][:], racc[:, j * D:(j + 1) * D],
                            agF[jp][:, 0:1], rsF[jp][:], ALU.subtract, ALU.mult)
            v.drain()
            v.tensor_tensor(ftmp[jp][:], ftmp[jp][:], yld[jp][:], ALU.add)
            em.ms(v, "vFF", (q, j, "fout"))
        elif tag == "setup_v":
            v.memset(onesf[:], 1.0)
            v.memset(neg3e4[:], -30000.0)
            v.memset(zero16[:], 0.0)
            v.memset(halfc[:], 0.5)
            v.memset(gTx[:], 0.5)
            v.memset(vbq[0][:], 1.0)
            v.memset(vbq[1][:], 1.0)
            v.memset(E_sb[:], 0.0)
            v.memset(eyeblk[:], 0.0)
            v.memset(epsc[:], EPS)
            em.wait(v, "dSU", "eb")
            v.drain()
            for kx in range(4):
                v.tensor_copy(eyeblk[:, kx * HID + kx * P:kx * HID + (kx + 1) * P],
                              eye[:])
            em.ms(v, "vST", "memset")
            em.wait(v, "pRP", "bo0")
            v.tensor_copy(boRep[:, 0:512], pb[5][:])
            em.ms(v, "vRP", "bo0ev")
            em.wait(v, "pRP", "bo1")
            v.tensor_copy(boRep[:, 512:1024], pb[4][:])
            em.wait(v, "pRP", "eb")
            v.tensor_copy(ebRep[:], pb[5][:, 0:NE])
            em.ms(v, "vRP", "boeb")
            em.wait(v, "pRP", "cent")
            v.tensor_copy(ct_bf[:], pbT[:, 0:P])
            em.ms(v, "vRP", "ctev")

    def h_scalar(em, sc, it):
        tag = it[0]
        t = it[1] if len(it) > 1 else None
        i = t % 2 if isinstance(t, int) else None
        if tag == "a1_a_rs":
            i4 = t % 4 if NT % 4 == 0 else t % 2
            em.wait(sc, "vA1", (t, "st"))
            if t >= 4:
                em.wait(sc, "vA1", (t - 4, "hb"))
            sc.activation(lv4[i4][:], ag4[i4][:, 1:2], AF.Ln, bias=epsc[:])
            sc.drain()
            sc.activation(rs4[i4][:], lv4[i4][:], AF.Exp, scale=-0.5)
            em.ms(sc, "aA1", (t, "rs"))
        elif tag == "a1_a_ek":
            em.wait(sc, "pA1", (t, "mm_k"))
            if t >= 2:
                em.wait(sc, "pA1", (t - 2, "mm_E"))
            sc.activation(ek[i][:], pb[0 if i == 0 else 4][:], AF.Exp)
            em.ms(sc, "aA1", (t, "expk"))
        elif tag == "a2_a_expq":
            em.wait(sc, "pA2", (t, "mm_q"))
            if t >= 2:
                em.wait(sc, "vA2", (t - 2, "qn"))
            sc.activation(qe[i][:], pb[it[2]][:], AF.Exp)
            em.ms(sc, "aA2", (t, "expq"))
        elif tag == "a2_a_rs1":
            em.wait(sc, "vA2", (t, "st1"))
            if t >= 2:
                em.wait(sc, "vA2", (t - 2, "y"))
            sc.activation(lv1[i][:], ag1[i][:, 1:2], AF.Ln, bias=epsc[:])
            sc.drain()
            sc.activation(rs1[i][:], lv1[i][:], AF.Exp, scale=-0.5)
            em.ms(sc, "aA2", (t, "rs1"))
        elif tag == "a2_a_rs2":
            em.wait(sc, "vA2", (t, "st2"))
            if t >= 2:
                em.wait(sc, "vA2", (t - 2, "ub"))
            sc.activation(lv2[i][:], ag2[i][:, 1:2], AF.Ln, bias=epsc[:])
            sc.drain()
            sc.activation(rs2[i][:], lv2[i][:], AF.Exp, scale=-0.5)
            em.ms(sc, "aA2", (t, "rs2"))
        elif tag == "a2_a_es":
            em.wait(sc, "vA2", (t, "scev"))
            if t >= 2:
                em.wait(sc, "vA2", (t - 2, "g"))
            sc.activation(esb[i][:], scb[i][:], AF.Exp, bias=ngm[i][:])
            em.ms(sc, "aA2", (t, "es"))
        elif tag == "finact":
            q, j = it[1], it[2]
            jp = j % 2
            em.wait(sc, "vFF", (q, j, "stF"))
            if (q, j) not in ((0, 0), (0, 1)):
                pj = (q, j - 2) if j >= 2 else (q - 1, j + QT - 2)
                em.wait(sc, "vFF", (pj[0], pj[1], "fout"))
            sc.activation(lvF[jp][:], agF[jp][:, 1:2], AF.Ln, bias=epsc[:])
            sc.drain()
            sc.activation(rsF[jp][:], lvF[jp][:], AF.Exp, scale=-0.5)
            em.ms(sc, "aFF", (q, j, "rsF"))

    def h_tensor(em, pe, it):
        tag = it[0]
        t = it[1] if len(it) > 1 else None
        i = t % 2 if isinstance(t, int) else None
        if tag == "a1_p_trhb":
            em.wait(pe, "vA1", (t, "hb"))
            if t == 0:
                em.wait(pe, "vRP", "ctev")
            if t >= 2:
                em.wait(pe, "vA1", (t - 2, "hTev"))
            i4 = t % 4 if NT % 4 == 0 else t % 2
            pT = pbT if t % 2 == 0 else pbT2
            last = None
            for kk in range(8):
                last = pe.transpose(pT[:, kk * P:(kk + 1) * P],
                                    hb4[i4][:, kk * P:(kk + 1) * P], eye[:])
            em.pe_ms(last, "pA1", (t, "trhb"))
        elif tag == "a1_p_k":
            i4 = t % 4 if NT % 4 == 0 else t % 2
            em.wait(pe, "vA1", (t, "hTev"))
            if t == 0:
                em.wait(pe, "dSW", "cen")
            if t == 1:
                em.wait(pe, "vRP", "boeb")
            if t >= 2:
                em.wait(pe, "aA1", (t - 2, "expk"))
            last = None
            for kk in range(8):
                last = pe.matmul(pb[0 if i == 0 else 4][:],
                                 hT4[i4][:, kk * P:(kk + 1) * P],
                                 wkb[:, kk * HID:(kk + 1) * HID],
                                 start=(kk == 0), stop=(kk == 7))
            em.pe_ms(last, "pA1", (t, "mm_k"))
        elif tag == "a1_p_v":
            i4 = t % 4 if NT % 4 == 0 else t % 2
            if t == 1:
                em.wait(pe, "vRP", "boeb")
            if t >= 2:
                em.wait(pe, "vA1", (t - 2, "vbev"))
            last = None
            for kk in range(8):
                last = pe.matmul(pb[1 if i == 0 else 5][:],
                                 hT4[i4][:, kk * P:(kk + 1) * P],
                                 wvb[:, kk * HID:(kk + 1) * HID],
                                 start=(kk == 0), stop=(kk == 7))
            em.pe_ms(last, "pA1", (t, "mm_v"))
        elif tag == "a1_p_E":
            em.wait(pe, "aA1", (t, "expk"))
            em.wait(pe, "vA1", (t, "vbev"))
            if t >= 2:
                em.wait(pe, "vA1", (t - 2, "Eacc"))
            last = None
            for h in range(HEADS):
                pbase = (h % 2) * 64
                fo = (h // 2) * 65
                last = pe.matmul(pb[2 + i][pbase:pbase + 64, fo:fo + 65],
                                 ek[i][:, h * DH:(h + 1) * DH],
                                 vbq[i][:, h * 65:(h + 1) * 65],
                                 start=True, stop=True,
                                 tile_position=(0, pbase))
            em.pe_ms(last, "pA1", (t, "mm_E"))
        elif tag == "a2_p_q":
            bk, pv = it[2]
            em.wait(pe, f"dHL{t % 2}", t)
            em.wait(pe, pv[0], pv[1])
            last = None
            for kk in range(8):
                last = pe.matmul(pb[bk][:], hT[t % 2][:, kk * P:(kk + 1) * P],
                                 wqb[:, kk * HID:(kk + 1) * HID],
                                 start=(kk == 0), stop=(kk == 7))
            em.pe_ms(last, "pA2", (t, "mm_q"))
        elif tag == "a2_p_trqn":
            em.wait(pe, "vA2", (t, "qn"))
            if t < 2:
                em.wait(pe, "vA1", (NT - 2 + t, "hTev"))
            else:
                em.wait(pe, "vA2", (t - 2, "gTev"))
            pT = pbT if t % 2 == 0 else pbT2
            last = None
            for jx in range(4):
                last = pe.transpose(pT[:, jx * P:(jx + 1) * P],
                                    qn[i][:, jx * P:(jx + 1) * P], eye[:])
            em.pe_ms(last, "pA2", (t, "trqn"))
        elif tag == "a2_p_o2":
            bk, pv = it[2]
            em.wait(pe, "vA2", (t, "qTev"))
            if t == 0:
                em.wait(pe, "vMID", "ctx")
            em.wait(pe, pv[0], pv[1])
            last = None
            for h in range(HEADS):
                pbase = (h % 2) * 64
                last = pe.matmul(pb[bk][pbase:pbase + 64, (h // 2) * P:(h // 2 + 1) * P],
                                 ctx[pbase:pbase + 64, (h // 2) * DH:(h // 2 + 1) * DH],
                                 qTt[i][pbase:pbase + 64, (h // 2) * P:(h // 2 + 1) * P],
                                 start=True, stop=True,
                                 tile_position=(pbase, pbase))
            em.pe_ms(last, "pA2", (t, "o2"))
        elif tag == "a2_p_wo":
            b0, pv0 = it[2]
            b1, pv1 = it[3]
            if t == 0:
                em.wait(pe, "dSW", "bs")
            em.wait(pe, "vA2", (t, "o2ev"))
            em.wait(pe, pv0[0], pv0[1])
            em.wait(pe, pv1[0], pv1[1])
            last = None
            for oh, bk in ((0, b0), (1, b1)):
                for jx in range(4):
                    last = pe.matmul(pb[bk][:], o2b[i][:, jx * P:(jx + 1) * P],
                                     wob[:, jx * D + oh * HID:jx * D + (oh + 1) * HID],
                                     start=(jx == 0), stop=(jx == 3))
                if oh == 0:
                    em.pe_ms(last, "pA2", (t, "wo0"))
            em.pe_ms(last, "pA2", (t, "wo"))
        elif tag == "a2_p_trub":
            em.wait(pe, "vA2", (t, "ub"))
            em.wait(pe, "vA2", (t, "qTev"))
            pT = pbT if t % 2 == 0 else pbT2
            last = None
            for kk in range(8):
                last = pe.transpose(pT[:, kk * P:(kk + 1) * P],
                                    ub[i][:, kk * P:(kk + 1) * P], eye[:])
            em.pe_ms(last, "pA2", (t, "trub"))
        elif tag == "a2_p_sc":
            bk, pv = it[2]
            em.wait(pe, "vA2", (t, "uTev"))
            em.wait(pe, pv[0], pv[1])
            last = None
            for kk in range(8):
                last = pe.matmul(pb[bk][:, 0:NE],
                                 uT[:, kk * NTOK + t * P:kk * NTOK + (t + 1) * P],
                                 ct_bf[:, kk * NE:(kk + 1) * NE],
                                 start=(kk == 0), stop=(kk == 7))
            em.pe_ms(last, "pA2", (t, "sc"))
        elif tag == "a2_p_trg":
            em.wait(pe, "vA2", (t, "g"))
            em.wait(pe, "vA2", (t, "uTev"))
            pT = pbT if t % 2 == 0 else pbT2
            last = pe.transpose(pT[0:NE, 0:P], gb[:], eye[:])
            em.pe_ms(last, "pA2", (t, "trg"))
        elif tag == "ffnp":
            q, s, oh, j = it[1], it[2], it[3], it[4]
            bk, pv = it[5]
            tt = q * QT + j
            em.wait(pe, "vA2", (tt, "gTev"))
            em.wait(pe, pv[0], pv[1])
            if s == 0:
                last = pe.matmul(pb[bk][:], gTx[:, tt * P:(tt + 1) * P],
                                 brx[:, oh * HID:(oh + 1) * HID],
                                 start=True, stop=False)
                for kx in range(4):
                    kk = 4 * oh + kx
                    last = pe.matmul(
                        pb[bk][:],
                        uT[:, kk * NTOK + tt * P:kk * NTOK + (tt + 1) * P],
                        eyeblk[:, kx * HID:(kx + 1) * HID],
                        start=False, stop=(kx == 3))
            else:
                par = unit_idx(q, s, oh) % 2
                if j == 0:
                    em.wait(pe, f"dWT{par}", (q, s, oh, 7))
                last = None
                for kk in range(8):
                    last = pe.matmul(
                        pb[bk][:],
                        uT[:, kk * NTOK + tt * P:kk * NTOK + (tt + 1) * P],
                        wrt[par][:, kk * HID:(kk + 1) * HID],
                        start=(kk == 0), stop=(kk == 7))
            em.pe_ms(last, "pF", (q, s, oh, j))
        elif tag == "setup_p":
            em.wait(pe, "dSU", "eb")
            em.wait(pe, "vST", "memset")
            i1 = pe.matmul(pb[5][:], onesf[0:1, :], bo_row[:, 0:512],
                           start=True, stop=True)
            em.pe_ms(i1, "pRP", "bo0")
            i2 = pe.matmul(pb[4][:], onesf[0:1, :], bo_row[:, 512:1024],
                           start=True, stop=True)
            em.pe_ms(i2, "pRP", "bo1")
            em.wait(pe, "vRP", "bo0ev")
            i3 = pe.matmul(pb[5][:, 0:NE], onesf[0:1, :], eb_row[:],
                           start=True, stop=True)
            em.pe_ms(i3, "pRP", "eb")
            em.wait(pe, "dSW", "cen")
            last = None
            for kk in range(8):
                last = pe.transpose(pbT[:, kk * NE:(kk + 1) * NE],
                                    cen_st[0:NE, kk * P:(kk + 1) * P],
                                    eye[0:NE, 0:NE])
            em.pe_ms(last, "pRP", "cent")

    def emit_sync(em, sync):
        em.dma(sync, "dSU", "eye", eye[:], eye_in[:])
        em.dma(sync, "dSU", "bo", bo_row[:], bo[:])
        em.dma(sync, "dSU", "eb", eb_row[:], ebi[:])
        xdepth = 4 if NT % 4 == 0 else 2
        for t in range(NT):
            if t >= xdepth:
                em.wait(sync, "vA1", (t - xdepth, "hb"))
            em.dma(sync, f"dX{t % xdepth}", t, xA4[t % xdepth][:, 0:HID],
                   xin[t * P:(t + 1) * P, 0:HID])
            em.dma(sync, f"dXh{t % xdepth}", t, xA4[t % xdepth][:, HID:D],
                   xin[t * P:(t + 1) * P, HID:D])
            if t >= 4:
                em.wait(sync, "vA1", (t - 4, "hTev"))
                em.dma(sync, f"dHS{(t - 4) % 4}", t - 4,
                       hscr[(t - 4) * P:(t - 3) * P, :], hT4[(t - 4) % 4][:])
        for t in range(max(0, NT - 4), NT):
            em.wait(sync, "vA1", (t, "hTev"))
            em.dma(sync, f"dHS{t % 4}", t,
                   hscr[t * P:(t + 1) * P, :], hT4[t % 4][:])
        em.wait(sync, "vA1", (NT - 1, "hb"))
        em.dma(sync, "dX20", 0, xA[0][:, 0:HID], xin[0:P, 0:HID])
        em.dma(sync, "dX2h0", 0, xA[0][:, HID:D], xin[0:P, HID:D])
        em.dma(sync, "dX21", 1, xA[1][:, 0:HID], xin[P:2 * P, 0:HID])
        em.dma(sync, "dX2h1", 1, xA[1][:, HID:D], xin[P:2 * P, HID:D])
        for c in range(min(2, NT)):
            lastu = ((NT - 1 - c) // 4) * 4 + c
            em.wait(sync, "pA1", (lastu, "mm_v"))
            em.wait(sync, f"dHS{c}", lastu)
            em.dma(sync, f"dHL{c}", c, hT[c][:], hscr[c * P:(c + 1) * P, :])
        for it in flow_tail:
            h_sync(em, sync, it)
        em.wait(sync, "dOUT0", (NQ - 1, QT - 2))
        em.wait(sync, "dOUT1", (NQ - 1, QT - 1))

    def emit_gpsimd(em, gp):
        em.dma(gp, "dSW", "wq", wqb[:].rearrange("p (k c) -> p k c", k=8),
               wq.rearrange("(k p) c -> p k c", p=P))
        em.dma(gp, "dSW", "wk", wkb[:].rearrange("p (k c) -> p k c", k=8),
               wk.rearrange("(k p) c -> p k c", p=P))
        em.dma(gp, "dSW", "wv", wvb[:].rearrange("p (k c) -> p k c", k=8),
               wv.rearrange("(k p) c -> p k c", p=P))
        em.dma(gp, "dSW", "cen", cen_st[:], cen[:])
        em.wait(gp, "vA1", (min(4, NT - 1), "hb"))
        em.dma(gp, "dSW", "wo", wob[:].rearrange("p (k c) -> p k c", k=4),
               wo.rearrange("(k p) c -> p k c", p=P))
        em.dma(gp, "dSW", "br", brx[0:NE, :], brr[:])
        em.dma(gp, "dSW", "bs", brx[NE:NMAT, :], bss[:])
        for m in range(NMAT):
            src = wr[m] if m < NE else ws[m - NE]
            if m >= 1:
                em.wait(gp, "vA1", (min(2 * m + 3, NT - 1), "hb"))
            em.dma(gp, "dWC", m, wbf[m], src)
        if DEBUG:
            em.wait(gp, "dOUT0", (NQ - 1, QT - 2))
            em.wait(gp, "dOUT1", (NQ - 1, QT - 1))
            em.dma(gp, "dDBG", 0, dbg_E[:], E_sb[:])
            em.dma(gp, "dDBG", 1, dbg_ctx[:], ctx[:])
            em.dma(gp, "dDBG", 2, dbg_g[:], gAll[:])
            em.dma(gp, "dDBG", 3, dbg_y[:], yscr[:])
            em.dma(gp, "dDBG", 4, dbg_uT[:], uT[:])
            em.dma(gp, "dDBG", 6, dbg_racc[:], racc[:])
            em.dma(gp, "dDBG", 7, dbg_gTx[:], gTx[:])
            em.wait(gp, "dDBG", 7)

    def emit_vector(em, v):
        h_vector(em, v, ("setup_v",))
        for it in flow_a1:
            h_vector(em, v, it)
        h_vector(em, v, ("mid",))
        for it in flow_tail:
            h_vector(em, v, it)

    def emit_scalar(em, sc):
        for it in flow_a1:
            h_scalar(em, sc, it)
        for it in flow_tail:
            h_scalar(em, sc, it)

    def emit_tensor(em, pe):
        h_tensor(em, pe, ("setup_p",))
        for it in flow_a1:
            h_tensor(em, pe, it)
        for it in flow_tail:
            h_tensor(em, pe, it)

    em = Em()
    dummy = _Dummy()
    for f in (emit_sync, emit_gpsimd, emit_tensor, emit_vector, emit_scalar):
        f(em, dummy)
    em.mode = "emit"
    em.cnt = {}
    for name in em.vals:
        em.sem[name] = nc.alloc_semaphore(name)
    with nc.Block() as block:
        @block.sync
        def _(sync):
            emit_sync(em, sync)

        @block.gpsimd
        def _(gp):
            emit_gpsimd(em, gp)

        @block.tensor
        def _(pe):
            emit_tensor(em, pe)

        @block.vector
        def _(v):
            emit_vector(em, v)

        @block.scalar
        def _(sc):
            emit_scalar(em, sc)

    return nc


def kernel(L0_attn_x, L0_attn_Wq, L0_attn_Wk, L0_attn_Wv, L0_attn_Wo,
           L0_attn_bo, L0_ffn_centroids, L0_ffn_expert_biases, L0_ffn_Wr,
           L0_ffn_br, L0_ffn_Ws, L0_ffn_bs):
    x = np.ascontiguousarray(np.asarray(L0_attn_x, dtype=np.float32))
    com = {
        "wq": np.ascontiguousarray(np.asarray(L0_attn_Wq, np.float32)),
        "wk": np.ascontiguousarray(np.asarray(L0_attn_Wk, np.float32)),
        "wv": np.ascontiguousarray(np.asarray(L0_attn_Wv, np.float32)),
        "wo": np.ascontiguousarray(np.asarray(L0_attn_Wo, np.float32)),
        "bo": np.ascontiguousarray(np.asarray(L0_attn_bo, np.float32).reshape(1, D)),
        "cen": np.ascontiguousarray(np.asarray(L0_ffn_centroids, np.float32)),
        "ebi": np.ascontiguousarray(np.asarray(L0_ffn_expert_biases, np.float32).reshape(1, NE)),
        "wr": np.ascontiguousarray(np.asarray(L0_ffn_Wr, np.float32)),
        "brr": np.ascontiguousarray(np.asarray(L0_ffn_br, np.float32)),
        "ws": np.ascontiguousarray(np.asarray(L0_ffn_Ws, np.float32)),
        "bss": np.ascontiguousarray(np.asarray(L0_ffn_bs, np.float32)),
        "eye_in": np.eye(P, dtype=ml_dtypes.bfloat16),
    }
    nc = build_nc()
    in_maps = [dict(com, xin=np.ascontiguousarray(x[b])) for b in range(N_CORES)]
    res = run_bass_kernel_spmd(nc, in_maps, core_ids=list(range(N_CORES)))
    return np.stack([r["out"] for r in res.results], axis=0)


# version 21
# speedup vs baseline: 1.1989x; 1.0226x over previous
import numpy as np
import ml_dtypes
import concourse.bass as bass
import concourse.mybir as mybir
from concourse.bass_utils import run_bass_kernel_spmd

F32 = mybir.dt.float32
BF16 = mybir.dt.bfloat16
U32 = mybir.dt.uint32
AF = mybir.ActivationFunctionType
ALU = mybir.AluOpType
AX = mybir.AxisListType

N_CORES = 8
NTOK = 4096
D = 1024
P = 128
NT = NTOK // P
HEADS = 8
DH = 64
HID = 512
SCALE = DH ** -0.5
NE = 16
NSH = 2
NMAT = NE + NSH
NSTEP = 1 + NMAT
NQ = 4
QT = NT // NQ
EPS = 1e-5
DEBUG = False
PHASES = 'ALL'


class _Dummy:
    def __getattr__(self, k):
        return lambda *a, **kw: self

    def __call__(self, *a, **kw):
        return self


class Em:

    def __init__(self):
        self.vals = {}
        self.cnt = {}
        self.mode = "count"
        self.sem = {}

    def _bump(self, chan, label, k):
        c = self.cnt.get(chan, 0) + k
        self.cnt[chan] = c
        d = self.vals.setdefault(chan, {})
        if self.mode == "count":
            assert label not in d, f"dup label {chan}/{label}"
            d[label] = c
        else:
            assert d.get(label) == c, f"label drift {chan}/{label}: {d.get(label)} != {c}"
        return c

    def ms(self, eng, chan, label):
        self._bump(chan, label, 1)
        if self.mode == "emit":
            eng.drain().then_inc(self.sem[chan], 1)

    def pe_ms(self, instr, chan, label):
        self._bump(chan, label, 1)
        if self.mode == "emit":
            instr.then_inc(self.sem[chan], 1)

    def op_ms(self, instr, chan, label):
        self._bump(chan, label, 1)
        if self.mode == "emit":
            instr.then_inc(self.sem[chan], 1)

    def dma(self, eng, chan, label, out, in_, **kw):
        self._bump(chan, label, 16)
        if self.mode == "emit":
            eng.dma_start(out=out, in_=in_, **kw).then_inc(self.sem[chan], 16)

    def wait(self, eng, chan, label):
        if self.mode == "emit":
            eng.wait_ge(self.sem[chan], self.vals[chan][label])


def build_nc():
    nc = bass.Bass("TRN2", target_bir_lowering=False, debug=False)

    xin = nc.dram_tensor("xin", [NTOK, D], F32, kind="ExternalInput").ap()
    wq = nc.dram_tensor("wq", [D, HID], F32, kind="ExternalInput").ap()
    wk = nc.dram_tensor("wk", [D, HID], F32, kind="ExternalInput").ap()
    wv = nc.dram_tensor("wv", [D, HID], F32, kind="ExternalInput").ap()
    wo = nc.dram_tensor("wo", [HID, D], F32, kind="ExternalInput").ap()
    bo = nc.dram_tensor("bo", [1, D], F32, kind="ExternalInput").ap()
    cen = nc.dram_tensor("cen", [NE, D], F32, kind="ExternalInput").ap()
    ebi = nc.dram_tensor("ebi", [1, NE], F32, kind="ExternalInput").ap()
    wr = nc.dram_tensor("wr", [NE, D, D], F32, kind="ExternalInput").ap()
    brr = nc.dram_tensor("brr", [NE, D], F32, kind="ExternalInput").ap()
    ws = nc.dram_tensor("ws", [NSH, D, D], F32, kind="ExternalInput").ap()
    bss = nc.dram_tensor("bss", [NSH, D], F32, kind="ExternalInput").ap()
    eye_in = nc.dram_tensor("eye_in", [P, P], BF16, kind="ExternalInput").ap()
    out = nc.dram_tensor("out", [NTOK, D], F32, kind="ExternalOutput").ap()
    wbf = nc.dram_tensor("wbf", [NMAT, D, D], BF16, kind="Internal").ap()
    yscr = nc.dram_tensor("yscr", [NTOK, D], F32, kind="Internal").ap()
    if DEBUG:
        dbg_E = nc.dram_tensor("dbg_E", [P, 4 * 65], F32, kind="ExternalOutput").ap()
        dbg_ctx = nc.dram_tensor("dbg_ctx", [P, 4 * DH], F32, kind="ExternalOutput").ap()
        dbg_g = nc.dram_tensor("dbg_g", [P, NT * NE], F32, kind="ExternalOutput").ap()
        dbg_y = nc.dram_tensor("dbg_y", [NTOK, D], F32, kind="ExternalOutput").ap()
        dbg_uT = nc.dram_tensor("dbg_uT", [P, 8 * NTOK], F32, kind="ExternalOutput").ap()
        dbg_racc = nc.dram_tensor("dbg_racc", [P, QT * D], F32, kind="ExternalOutput").ap()
        dbg_gTx = nc.dram_tensor("dbg_gTx", [NMAT, NTOK], F32, kind="ExternalOutput").ap()

    hscr = nc.dram_tensor("hscr", [NTOK, D], BF16, kind="Internal").ap()

    A = nc.alloc_sbuf_tensor
    eye = A("eye", [P, P], BF16)
    onesf = A("onesf", [4, P], F32)
    boRep = A("boRep", [P, D], F32)
    ebRep = A("ebRep", [P, NE], F32)
    ct_bf = A("ct_bf", [P, P], BF16)
    brx = A("brx", [NMAT, D], BF16)
    gTx = A("gTx", [NMAT, NTOK], BF16)
    uT = A("uT", [P, 8 * NTOK], BF16)
    gAll = A("gAll", [P, NT * NE], F32)
    halfc = A("halfc", [P, 1], F32)
    neg3e4 = A("neg3e4", [P, NE], F32)
    zero16 = A("zero16", [P, NE], F32)
    gb = A("gb", [P, NE], BF16)
    epsc = A("epsc", [P, 1], F32)
    eyeblk = A("eyeblk", [P, 4 * HID], BF16)
    E_sb = A("E_sb", [P, 4 * 65], F32)
    sInv = A("sInv", [P, 4], F32)
    ctx = A("ctx", [P, 4 * DH], BF16)

    def stat4(nm):
        return ([A(f"{nm}bn{i}", [P, 12], F32) for i in range(2)],
                [A(f"{nm}ag{i}", [P, 2], F32) for i in range(2)],
                [A(f"{nm}lv{i}", [P, 1], F32) for i in range(2)],
                [A(f"{nm}rs{i}", [P, 1], F32) for i in range(2)])

    bnA, agA, lvA, rsA = stat4("A")
    bn0, ag0, lv0, rs0 = stat4("B")
    ag4, lv4, rs4 = agA + ag0, lvA + lv0, rsA + rs0
    bn1, ag1, lv1, rs1 = stat4("C")
    bn2, ag2, lv2, rs2 = stat4("D")
    bnF, agF, lvF, rsF = stat4("E")

    qs = [A(f"qs{i}", [P, HEADS], F32) for i in range(2)]
    rq = [A(f"rq{i}", [P, HEADS], F32) for i in range(2)]
    scb = [A(f"scb{i}", [P, NE], F32) for i in range(2)]
    esb = [A(f"esb{i}", [P, NE], F32) for i in range(2)]
    ngm = [A(f"ngm{i}", [P, 1], F32) for i in range(2)]
    rsm = [A(f"rsm{i}", [P, 1], F32) for i in range(2)]
    mx1 = [A(f"mx1{i}", [P, 1], F32) for i in range(2)]
    m2v = [A(f"m2v{i}", [P, 1], F32) for i in range(2)]
    mk1 = [A(f"mk1{i}", [P, NE], U32) for i in range(2)]
    mk2 = [A(f"mk2{i}", [P, NE], U32) for i in range(2)]
    sm2 = [A(f"sm2{i}", [P, NE], F32) for i in range(2)]
    gsl = [A(f"gsl{i}", [P, NE], F32) for i in range(2)]
    ssm = [A(f"ssm{i}", [P, 1], F32) for i in range(2)]

    wqb = A("wqb", [P, 8 * HID], BF16)
    wob = A("wob", [P, 4 * D], BF16)
    xA = [A(f"xA{i}", [P, D], F32) for i in range(2)]
    hb = [A(f"hb{i}", [P, D], BF16) for i in range(2)]
    hT = [A(f"hT{i}", [P, D], BF16) for i in range(2)]
    qe = [A(f"qe{i}", [P, HID], F32) for i in range(2)]
    qn = [A(f"qn{i}", [P, HID], BF16) for i in range(2)]
    qTt = [A(f"qTt{i}", [P, HID], BF16) for i in range(2)]
    o2b = [A(f"o2b{i}", [P, HID], BF16) for i in range(2)]
    att = [A(f"att{i}", [P, D], F32) for i in range(2)]
    ub = [A(f"ub{i}", [P, D], BF16) for i in range(2)]
    wrt = [A(f"wrt{i}", [P, 8 * HID], BF16, is_transpose_dest=True) for i in range(2)]
    _rs, _re = nc.bump_sbuf(max(QT * D * 4, 32768), "left", align_bytes=128)
    racc = nc.alloc_sbuf_tensor_at("racc", [P, QT * D], F32, offset=_rs)
    wkb = nc.alloc_sbuf_tensor_at("wkb", [P, 8 * HID], BF16, offset=_rs)
    wvb = nc.alloc_sbuf_tensor_at("wvb", [P, 8 * HID], BF16, offset=_rs + 8 * HID * 2)
    xA4 = xA + [nc.alloc_sbuf_tensor_at(f"xA{i}", [P, D], F32,
                offset=_rs + 16384 + (i - 2) * 4096) for i in (2, 3)]
    hT4 = hT + [nc.alloc_sbuf_tensor_at(f"hT{i}", [P, D], BF16,
                offset=_rs + 24576 + (i - 2) * 2048) for i in (2, 3)]
    hb4 = hb + [nc.alloc_sbuf_tensor_at(f"hb{i}", [P, D], BF16,
                offset=_rs + 28672 + (i - 2) * 2048) for i in (2, 3)]
    _fs, _fe = nc.bump_sbuf(16384, "left", align_bytes=128)
    AT = nc.alloc_sbuf_tensor_at
    yld = [AT(f"yld{i}", [P, D], F32, offset=_fs + i * 4096) for i in range(2)]
    ftmp = [AT(f"ftmp{i}", [P, D], F32, offset=_fs + 8192 + i * 4096) for i in range(2)]
    bo_row = AT("bo_row", [1, D], F32, offset=_fs)
    cen_st = AT("cen_st", [NE, D], BF16, offset=_fs + 4096)
    eb_row = AT("eb_row", [1, NE], F32, offset=_fs + 6144)
    ek = [AT(f"ek{i}", [P, HID], BF16, offset=_fs + 6272 + i * 1024) for i in range(2)]
    vbq = [AT(f"vb{i}", [P, 8 * 65], BF16, offset=_fs + 8352 + i * 1056) for i in range(2)]

    PA = nc.alloc_psum_tensor
    pbT = PA("pbT", [P, 1024], BF16)
    pbT2 = PA("pbT2", [P, 1024], BF16)
    pb = [PA(f"pb{i}", [P, 512], F32) for i in range(6)]

    def a1_items(t):
        return [("a1_v_st", t), ("a1_a_rs", t), ("a1_v_hb", t),
                ("a1_p_trhb", t), ("a1_v_hT", t), ("a1_p_k", t),
                ("a1_a_ek", t), ("a1_p_v", t), ("a1_v_vb", t),
                ("a1_p_E", t), ("a1_v_Eacc", t)]

    def zip_merge(la, lb):
        outl = []
        na, nb = len(la), len(lb)
        ia = ib = 0
        while ia < na or ib < nb:
            if ia < na:
                outl.append(la[ia]); ia += 1
            if ib < nb:
                outl.append(lb[ib]); ib += 1
        return outl

    flow_a1 = []
    if NT % 4 == 0:
        for t in range(0, NT, 4):
            lists = [a1_items(t + d) for d in range(4)]
            n = len(lists[0])
            for r in range(n):
                for lst in lists:
                    flow_a1.append(lst[r])
    else:
        for t in range(0, NT, 2):
            flow_a1 += zip_merge(a1_items(t), a1_items(t + 1))

    def a2_items(t):
        return [("a2_p_q", t),
                ("a2_a_expq", t), ("a2_v_qn", t), ("a2_p_trqn", t),
                ("a2_v_qT", t), ("a2_p_o2", t), ("a2_v_o2ev", t),
                ("a2_p_wo", t), ("a2_v_att", t), ("a2_a_rs1", t),
                ("a2_v_y", t), ("a2_a_rs2", t), ("a2_v_ub", t),
                ("a2_p_trub", t), ("a2_v_uT", t), ("a2_p_sc", t),
                ("a2_v_scev", t), ("a2_a_es", t), ("a2_v_g", t),
                ("a2_p_trg", t), ("a2_v_gT", t)]

    units = [(q, s, oh) for q in range(NQ) for s in range(1, NSTEP)
             for oh in range(2)]

    def unit_idx(q, s, oh):
        return (q * (NSTEP - 1) + (s - 1)) * 2 + oh

    def ffn_quarter(q):
        its = []
        for oh in range(2):
            for j in range(QT):
                its.append(("ffnp", q, 0, oh, j))
                its.append(("ffnv", q, 0, oh, j))
        for s in range(1, NSTEP):
            for oh in range(2):
                k = unit_idx(q, s, oh)
                if k + 2 < len(units):
                    its.append(("ffnl",) + units[k + 2])
                for j in range(QT):
                    its.append(("ffnp", q, s, oh, j))
                    its.append(("ffnv", q, s, oh, j))
        return its

    def fin_quarter(q):
        its = []
        for j in range(QT):
            its.append(("fina", q, j))
            its.append(("finact", q, j))
            if j >= 1:
                its.append(("finb", q, j - 1))
        its.append(("finb", q, QT - 1))
        return its

    def prop_merge(la, lb):
        outl = []
        na, nb = len(la), len(lb)
        if na == 0:
            return list(lb)
        ratio = nb / na
        ib = 0
        for ia in range(na):
            outl.append(la[ia])
            want = int(round((ia + 1) * ratio * 0.75))
            while ib < want:
                outl.append(lb[ib]); ib += 1
        outl.extend(lb[ib:])
        return outl

    flow_tail = []
    flow_tail.append(("ffnl",) + units[0])
    flow_tail.append(("ffnl",) + units[1])
    for w in range(NQ + 2):
        la = []
        if w < NQ:
            for t in range(QT * w, QT * w + QT, 2):
                la += zip_merge(a2_items(t), a2_items(t + 1))
        lb = []
        if 2 <= w:
            lb += fin_quarter(w - 2)
        if 1 <= w <= NQ:
            lb += ffn_quarter(w - 1)
        flow_tail += prop_merge(la, lb)

    pool = [
        [0, ("aA1", (NT - 2, "expk")), -1],
        [1, ("vA1", (NT - 2, "vbev")), -1],
        [2, ("vA1", (NT - 2, "Eacc")), -1],
        [3, ("vA1", (NT - 1, "Eacc")), -1],
        [4, ("aA1", (NT - 1, "expk")), -1],
        [5, ("vA1", (NT - 1, "vbev")), -1],
    ]
    _needs = {"a2_p_q": ("aA2", "expq", 1), "a2_p_o2": ("vA2", "o2ev", 1),
              "a2_p_wo": ("vA2", "att", 2), "a2_p_sc": ("vA2", "scev", 1)}
    _evac_tag = {"a2_p_q": "a2_a_expq", "a2_p_o2": "a2_v_o2ev",
                 "a2_p_wo": "a2_v_att", "a2_p_sc": "a2_v_scev"}
    evac_pos = {}
    for pos, it in enumerate(flow_tail):
        if it[0] in ("a2_a_expq", "a2_v_o2ev", "a2_v_att", "a2_v_scev"):
            evac_pos[(it[0], it[1])] = pos
        elif it[0] == "ffnv":
            evac_pos[("ffnv", it[1], it[2], it[3], it[4])] = pos

    def pool_take(pos, n):
        got = []
        for _ in range(n):
            cands = [e for e in pool if e[2] < pos]
            assert cands, f"no reusable psum bank at flow pos {pos}"
            e = min(cands, key=lambda x: x[2])
            pool.remove(e)
            got.append(e)
        return got

    pend = {}
    flow2 = []
    for pos, it in enumerate(flow_tail):
        tag = it[0]
        if tag in _needs:
            t = it[1]
            chan, lbl, nbk = _needs[tag]
            got = pool_take(pos, nbk)
            ep = evac_pos[(_evac_tag[tag], t)]
            res = []
            for bk, pv, _e in got:
                res.append((bk, pv))
                pool.append([bk, (chan, (t, lbl)), ep])
            pend[(tag, t)] = res
            flow2.append(it + tuple(res))
        elif tag == "a2_a_expq":
            flow2.append(it + (pend[("a2_p_q", it[1])][0][0],))
        elif tag == "a2_v_o2ev":
            flow2.append(it + (pend[("a2_p_o2", it[1])][0][0],))
        elif tag == "a2_v_att":
            g = pend[("a2_p_wo", it[1])]
            flow2.append(it + (g[0][0], g[1][0]))
        elif tag == "a2_v_scev":
            flow2.append(it + (pend[("a2_p_sc", it[1])][0][0],))
        elif tag == "ffnp":
            q, s, oh, j = it[1], it[2], it[3], it[4]
            got = pool_take(pos, 1)[0]
            ep = evac_pos[("ffnv", q, s, oh, j)]
            pool.append([got[0], ("vF", (q, s, oh, j)), ep])
            pend[("f", q, s, oh, j)] = (got[0], got[1])
            flow2.append(it + ((got[0], got[1]),))
        elif tag == "ffnv":
            q, s, oh, j = it[1], it[2], it[3], it[4]
            flow2.append(it + (pend[("f", q, s, oh, j)][0],))
        else:
            flow2.append(it)
    flow_tail = flow2

    def h_sync(em, sync, it):
        tag = it[0]
        if tag == "a2_v_y":
            t = it[1]; i = t % 2
            em.wait(sync, "vA2", (t, "y"))
            em.dma(sync, f"dY{i}", t, yscr[t * P:(t + 1) * P, :], att[i][:])
            if t + 2 < NT:
                em.dma(sync, f"dX2{i}", t + 2, xA[i][:, 0:HID],
                       xin[(t + 2) * P:(t + 3) * P, 0:HID])
                em.dma(sync, f"dX2h{i}", t + 2, xA[i][:, HID:D],
                       xin[(t + 2) * P:(t + 3) * P, HID:D])
            if t + 2 < NT:
                em.wait(sync, "pA2", (t, "mm_q"))
                em.wait(sync, f"dHS{(t + 2) % 4}", t + 2)
                em.dma(sync, f"dHL{(t + 2) % 2}", t + 2, hT[(t + 2) % 2][:],
                       hscr[(t + 2) * P:(t + 3) * P, :])
        elif tag == "ffnl":
            q, s, oh = it[1], it[2], it[3]
            k = unit_idx(q, s, oh)
            par = k % 2
            if k >= 2:
                pq, ps, po = units[k - 2]
                em.wait(sync, "pF", (pq, ps, po, QT - 1))
            em.wait(sync, "dWC", NMAT - 1)
            for kk in range(8):
                em.dma(sync, f"dWT{par}", (q, s, oh, kk),
                       wrt[par][:, kk * HID:(kk + 1) * HID],
                       wbf[s - 1, oh * HID:(oh + 1) * HID, kk * P:(kk + 1) * P],
                       transpose=True)
        elif tag == "fina":
            q, j = it[1], it[2]
            t = q * QT + j
            em.wait(sync, f"dY{t % 2}", t)
            if (q, j) not in ((0, 0), (0, 1)):
                pj = (q, j - 2) if j >= 2 else (q - 1, j + QT - 2)
                em.wait(sync, "vFF", (pj[0], pj[1], "fout"))
            em.dma(sync, f"dYL{j % 2}", (q, j), yld[j % 2][:],
                   yscr[t * P:(t + 1) * P, :])
        elif tag == "finb":
            q, j = it[1], it[2]
            t = q * QT + j
            em.wait(sync, "vFF", (q, j, "fout"))
            em.dma(sync, f"dOUT{j % 2}", (q, j), out[t * P:(t + 1) * P, :],
                   ftmp[j % 2][:])

    def ln_stats(v, bn, ag, src):
        v.bn_stats(bn[:, 0:6], src[:, 0:512])
        v.bn_stats(bn[:, 6:12], src[:, 512:1024])
        v.drain()
        v.bn_aggr(ag[:], bn[:, 0:12])

    def h_vector(em, v, it):
        tag = it[0]
        t = it[1] if len(it) > 1 else None
        i = t % 2 if isinstance(t, int) else None
        if tag == "a1_v_st":
            i4 = t % 4 if NT % 4 == 0 else t % 2
            em.wait(v, f"dX{i4}", t)
            em.wait(v, f"dXh{i4}", t)
            if t >= 4:
                em.wait(v, "aA1", (t - 4, "rs"))
            ln_stats(v, bnA[i], ag4[i4], xA4[i4])
            em.ms(v, "vA1", (t, "st"))
        elif tag == "a1_v_hb":
            i4 = t % 4 if NT % 4 == 0 else t % 2
            em.wait(v, "aA1", (t, "rs"))
            if t >= 4:
                em.wait(v, "pA1", (t - 4, "trhb"))
            v.tensor_scalar(hb4[i4][:], xA4[i4][:], ag4[i4][:, 0:1], rs4[i4][:],
                            ALU.subtract, ALU.mult)
            em.ms(v, "vA1", (t, "hb"))
        elif tag == "a1_v_hT":
            i4 = t % 4 if NT % 4 == 0 else t % 2
            em.wait(v, "pA1", (t, "trhb"))
            if t >= 4:
                em.wait(v, "pA1", (t - 4, "mm_v"))
                em.wait(v, f"dHS{t % 4}", t - 4)
            pT = pbT if t % 2 == 0 else pbT2
            v.tensor_copy(hT4[i4][:].rearrange("p (k c) -> p k c", k=8),
                          pT[:].rearrange("p (k c) -> p k c", k=8))
            em.ms(v, "vA1", (t, "hTev"))
        elif tag == "a1_v_vb":
            em.wait(v, "pA1", (t, "mm_v"))
            if t >= 2:
                em.wait(v, "pA1", (t - 2, "mm_E"))
            v.tensor_copy(vbq[i][:].rearrange("p (h c) -> p h c", c=65)[:, :, 0:64],
                          pb[1 if i == 0 else 5][:].rearrange("p (h c) -> p h c", c=64))
            em.ms(v, "vA1", (t, "vbev"))
        elif tag == "a1_v_Eacc":
            em.wait(v, "pA1", (t, "mm_E"))
            v.drain()
            ii = v.tensor_tensor(E_sb[:], E_sb[:], pb[2 + i][:, 0:260], ALU.add)
            em.op_ms(ii, "vA1", (t, "Eacc"))
        elif tag == "mid":
            v.drain()
            for h in range(HEADS):
                pbase, fo = (h % 2) * 64, (h // 2) * 65
                v.reciprocal(sInv[pbase:pbase + 64, h // 2:h // 2 + 1],
                             E_sb[pbase:pbase + 64, fo + 64:fo + 65])
            v.drain()
            for h in range(HEADS):
                pbase, fo = (h % 2) * 64, (h // 2) * 65
                v.tensor_scalar(ctx[pbase:pbase + 64, (h // 2) * DH:(h // 2 + 1) * DH],
                                E_sb[pbase:pbase + 64, fo:fo + 64],
                                sInv[pbase:pbase + 64, h // 2:h // 2 + 1],
                                None, ALU.mult)
            em.ms(v, "vMID", "ctx")
        elif tag == "a2_v_qn":
            em.wait(v, "aA2", (t, "expq"))
            v.reduce_sum(qs[i][:].rearrange("p (h o) -> p h o", o=1),
                         qe[i][:].rearrange("p (h c) -> p h c", h=HEADS),
                         axis=AX.X)
            v.drain()
            v.reciprocal(rq[i][:], qs[i][:])
            v.drain()
            if t >= 2:
                em.wait(v, "pA2", (t - 2, "trqn"))
            for h in range(HEADS):
                v.tensor_scalar(qn[i][:, h * DH:(h + 1) * DH],
                                qe[i][:, h * DH:(h + 1) * DH],
                                rq[i][:, h:h + 1], SCALE, ALU.mult, ALU.mult)
            em.ms(v, "vA2", (t, "qn"))
        elif tag == "a2_v_qT":
            em.wait(v, "pA2", (t, "trqn"))
            if t >= 2:
                em.wait(v, "pA2", (t - 2, "o2"))
            pT = pbT if t % 2 == 0 else pbT2
            v.tensor_copy(qTt[i][:], pT[:, 0:512])
            em.ms(v, "vA2", (t, "qTev"))
        elif tag == "a2_v_o2ev":
            em.wait(v, "pA2", (t, "o2"))
            if t >= 2:
                em.wait(v, "pA2", (t - 2, "wo"))
            v.tensor_copy(o2b[i][:], pb[it[2]][:])
            em.ms(v, "vA2", (t, "o2ev"))
        elif tag == "a2_v_att":
            em.wait(v, "pA2", (t, "wo0"))
            if t == 0:
                em.wait(v, "vRP", "boeb")
            if t >= 2:
                em.wait(v, "vA2", (t - 2, "ub"))
                em.wait(v, f"dY{i}", t - 2)
            v.tensor_tensor(att[i][:, 0:512], pb[it[2]][:], boRep[:, 0:512], ALU.add)
            em.wait(v, "pA2", (t, "wo"))
            ii = v.tensor_tensor(att[i][:, 512:1024], pb[it[3]][:],
                                 boRep[:, 512:1024], ALU.add)
            em.op_ms(ii, "vA2", (t, "att"))
            v.drain()
            if t >= 2:
                em.wait(v, "aA2", (t - 2, "rs1"))
            ln_stats(v, bn1[i], ag1[i], att[i])
            em.ms(v, "vA2", (t, "st1"))
        elif tag == "a2_v_y":
            em.wait(v, "aA2", (t, "rs1"))
            v.tensor_scalar(att[i][:], att[i][:], ag1[i][:, 0:1], rs1[i][:],
                            ALU.subtract, ALU.mult)
            v.drain()
            em.wait(v, f"dX2{i}", t)
            em.wait(v, f"dX2h{i}", t)
            v.tensor_tensor(att[i][:], att[i][:], xA[i][:], ALU.add)
            em.ms(v, "vA2", (t, "y"))
            if t >= 2:
                em.wait(v, "aA2", (t - 2, "rs2"))
            ln_stats(v, bn2[i], ag2[i], att[i])
            em.ms(v, "vA2", (t, "st2"))
        elif tag == "a2_v_ub":
            em.wait(v, "aA2", (t, "rs2"))
            if t >= 2:
                em.wait(v, "pA2", (t - 2, "trub"))
            v.tensor_scalar(ub[i][:], att[i][:], ag2[i][:, 0:1], rs2[i][:],
                            ALU.subtract, ALU.mult)
            em.ms(v, "vA2", (t, "ub"))
        elif tag == "a2_v_uT":
            em.wait(v, "pA2", (t, "trub"))
            pT = pbT if t % 2 == 0 else pbT2
            v.tensor_copy(uT[:].rearrange("p (k n) -> p k n", k=8)[:, :, t * P:(t + 1) * P],
                          pT[:].rearrange("p (k n) -> p k n", k=8))
            em.ms(v, "vA2", (t, "uTev"))
        elif tag == "a2_v_scev":
            em.wait(v, "pA2", (t, "sc"))
            if t >= 2:
                em.wait(v, "aA2", (t - 2, "es"))
            v.tensor_tensor(scb[i][:], pb[it[2]][:, 0:NE], ebRep[:], ALU.add)
            v.drain()
            v.reduce_max(ngm[i][:], scb[i][:], axis=AX.X, negate=True)
            em.ms(v, "vA2", (t, "scev"))
        elif tag == "a2_v_g":
            em.wait(v, "aA2", (t, "es"))
            v.reduce_sum(ssm[i][:], esb[i][:], axis=AX.X)
            v.drain()
            v.reciprocal(rsm[i][:], ssm[i][:])
            v.reduce_max(mx1[i][:], esb[i][:], axis=AX.X)
            v.drain()
            v.tensor_scalar(mk1[i][:], esb[i][:], mx1[i][:], None, ALU.is_ge)
            v.drain()
            v.select(sm2[i][:], mk1[i][:], neg3e4[:], esb[i][:], add_drain=True)
            v.drain()
            v.reduce_max(m2v[i][:], sm2[i][:], axis=AX.X)
            v.drain()
            v.tensor_scalar(mk2[i][:], esb[i][:], m2v[i][:], None, ALU.is_ge)
            v.drain()
            v.select(gsl[i][:], mk2[i][:], esb[i][:], zero16[:], add_drain=True)
            v.drain()
            v.tensor_scalar(gAll[:, t * NE:(t + 1) * NE], gsl[i][:],
                            rsm[i][:], None, ALU.mult)
            v.drain()
            if t >= 1:
                em.wait(v, "pA2", (t - 1, "trg"))
            v.tensor_copy(gb[:], gAll[:, t * NE:(t + 1) * NE])
            em.ms(v, "vA2", (t, "g"))
        elif tag == "a2_v_gT":
            em.wait(v, "pA2", (t, "trg"))
            pT = pbT if t % 2 == 0 else pbT2
            v.tensor_copy(gTx[0:NE, t * P:(t + 1) * P], pT[0:NE, 0:P])
            em.ms(v, "vA2", (t, "gTev"))
        elif tag == "ffnv":
            q, s, oh, j, bk = it[1], it[2], it[3], it[4], it[5]
            tt = q * QT + j
            em.wait(v, "pF", (q, s, oh, j))
            if q == 0 and s == 0:
                em.wait(v, "pA1", (NT - 1, "mm_E"))
                for tl in range(max(0, NT - 4), NT):
                    em.wait(v, f"dHS{tl % 4}", tl)
            if s >= 1:
                em.wait(v, "vF", (q, s - 1, oh, j))
            elif q >= 1:
                em.wait(v, "vFF", (q - 1, j, "fout"))
            rsl = racc[:, j * D + oh * HID:j * D + (oh + 1) * HID]
            if s == 0:
                ii = v.tensor_copy(rsl, pb[bk][:])
            else:
                e = s - 1
                gcol = (gAll[:, tt * NE + e:tt * NE + e + 1]
                        if e < NE else halfc[:])
                ii = v.scalar_tensor_tensor(rsl, pb[bk][:], gcol, rsl,
                                            ALU.mult, ALU.add)
            em.op_ms(ii, "vF", (q, s, oh, j))
        elif tag == "fina":
            q, j = it[1], it[2]
            jp = j % 2
            em.wait(v, "pF", (q, NSTEP - 1, 1, j))
            if q == 0 and j < 2:
                em.wait(v, "pA1", (NT - 1, "mm_E"))
            if (q, j) not in ((0, 0), (0, 1)):
                pj = (q, j - 2) if j >= 2 else (q - 1, j + QT - 2)
                em.wait(v, "aFF", (pj[0], pj[1], "rsF"))
            v.drain()
            ln_stats(v, bnF[jp], agF[jp], racc[:, j * D:(j + 1) * D])
            em.ms(v, "vFF", (q, j, "stF"))
        elif tag == "finb":
            q, j = it[1], it[2]
            jp = j % 2
            em.wait(v, "aFF", (q, j, "rsF"))
            em.wait(v, f"dYL{jp}", (q, j))
            if (q, j) not in ((0, 0), (0, 1)):
                pj = (q, j - 2) if j >= 2 else (q - 1, j + QT - 2)
                em.wait(v, f"dOUT{jp}", pj)
            v.tensor_scalar(ftmp[jp][:], racc[:, j * D:(j + 1) * D],
                            agF[jp][:, 0:1], rsF[jp][:], ALU.subtract, ALU.mult)
            v.drain()
            v.tensor_tensor(ftmp[jp][:], ftmp[jp][:], yld[jp][:], ALU.add)
            em.ms(v, "vFF", (q, j, "fout"))
        elif tag == "setup_v":
            v.memset(onesf[:], 1.0)
            v.memset(neg3e4[:], -30000.0)
            v.memset(zero16[:], 0.0)
            v.memset(halfc[:], 0.5)
            v.memset(gTx[:], 0.5)
            v.memset(vbq[0][:], 1.0)
            v.memset(vbq[1][:], 1.0)
            v.memset(E_sb[:], 0.0)
            v.memset(eyeblk[:], 0.0)
            v.memset(epsc[:], EPS)
            em.wait(v, "dSU", "eb")
            v.drain()
            for kx in range(4):
                v.tensor_copy(eyeblk[:, kx * HID + kx * P:kx * HID + (kx + 1) * P],
                              eye[:])
            em.ms(v, "vST", "memset")
            em.wait(v, "pRP", "bo0")
            v.tensor_copy(boRep[:, 0:512], pb[5][:])
            em.ms(v, "vRP", "bo0ev")
            em.wait(v, "pRP", "bo1")
            v.tensor_copy(boRep[:, 512:1024], pb[4][:])
            em.wait(v, "pRP", "eb")
            v.tensor_copy(ebRep[:], pb[5][:, 0:NE])
            em.ms(v, "vRP", "boeb")
            em.wait(v, "pRP", "cent")
            v.tensor_copy(ct_bf[:], pbT[:, 0:P])
            em.ms(v, "vRP", "ctev")

    def h_scalar(em, sc, it):
        tag = it[0]
        t = it[1] if len(it) > 1 else None
        i = t % 2 if isinstance(t, int) else None
        if tag == "a1_a_rs":
            i4 = t % 4 if NT % 4 == 0 else t % 2
            em.wait(sc, "vA1", (t, "st"))
            if t >= 4:
                em.wait(sc, "vA1", (t - 4, "hb"))
            sc.activation(lv4[i4][:], ag4[i4][:, 1:2], AF.Ln, bias=epsc[:])
            sc.drain()
            sc.activation(rs4[i4][:], lv4[i4][:], AF.Exp, scale=-0.5)
            em.ms(sc, "aA1", (t, "rs"))
        elif tag == "a1_a_ek":
            em.wait(sc, "pA1", (t, "mm_k"))
            if t >= 2:
                em.wait(sc, "pA1", (t - 2, "mm_E"))
            sc.activation(ek[i][:], pb[0 if i == 0 else 4][:], AF.Exp)
            em.ms(sc, "aA1", (t, "expk"))
        elif tag == "a2_a_expq":
            em.wait(sc, "pA2", (t, "mm_q"))
            if t >= 2:
                em.wait(sc, "vA2", (t - 2, "qn"))
            sc.activation(qe[i][:], pb[it[2]][:], AF.Exp)
            em.ms(sc, "aA2", (t, "expq"))
        elif tag == "a2_a_rs1":
            em.wait(sc, "vA2", (t, "st1"))
            if t >= 2:
                em.wait(sc, "vA2", (t - 2, "y"))
            sc.activation(lv1[i][:], ag1[i][:, 1:2], AF.Ln, bias=epsc[:])
            sc.drain()
            sc.activation(rs1[i][:], lv1[i][:], AF.Exp, scale=-0.5)
            em.ms(sc, "aA2", (t, "rs1"))
        elif tag == "a2_a_rs2":
            em.wait(sc, "vA2", (t, "st2"))
            if t >= 2:
                em.wait(sc, "vA2", (t - 2, "ub"))
            sc.activation(lv2[i][:], ag2[i][:, 1:2], AF.Ln, bias=epsc[:])
            sc.drain()
            sc.activation(rs2[i][:], lv2[i][:], AF.Exp, scale=-0.5)
            em.ms(sc, "aA2", (t, "rs2"))
        elif tag == "a2_a_es":
            em.wait(sc, "vA2", (t, "scev"))
            if t >= 2:
                em.wait(sc, "vA2", (t - 2, "g"))
            sc.activation(esb[i][:], scb[i][:], AF.Exp, bias=ngm[i][:])
            em.ms(sc, "aA2", (t, "es"))
        elif tag == "finact":
            q, j = it[1], it[2]
            jp = j % 2
            em.wait(sc, "vFF", (q, j, "stF"))
            if (q, j) not in ((0, 0), (0, 1)):
                pj = (q, j - 2) if j >= 2 else (q - 1, j + QT - 2)
                em.wait(sc, "vFF", (pj[0], pj[1], "fout"))
            sc.activation(lvF[jp][:], agF[jp][:, 1:2], AF.Ln, bias=epsc[:])
            sc.drain()
            sc.activation(rsF[jp][:], lvF[jp][:], AF.Exp, scale=-0.5)
            em.ms(sc, "aFF", (q, j, "rsF"))

    def h_tensor(em, pe, it):
        tag = it[0]
        t = it[1] if len(it) > 1 else None
        i = t % 2 if isinstance(t, int) else None
        if tag == "a1_p_trhb":
            em.wait(pe, "vA1", (t, "hb"))
            if t == 0:
                em.wait(pe, "vRP", "ctev")
            if t >= 2:
                em.wait(pe, "vA1", (t - 2, "hTev"))
            i4 = t % 4 if NT % 4 == 0 else t % 2
            pT = pbT if t % 2 == 0 else pbT2
            last = None
            for kk in range(8):
                last = pe.transpose(pT[:, kk * P:(kk + 1) * P],
                                    hb4[i4][:, kk * P:(kk + 1) * P], eye[:])
            em.pe_ms(last, "pA1", (t, "trhb"))
        elif tag == "a1_p_k":
            i4 = t % 4 if NT % 4 == 0 else t % 2
            em.wait(pe, "vA1", (t, "hTev"))
            if t == 0:
                em.wait(pe, "dSW", "cen")
            if t == 1:
                em.wait(pe, "vRP", "boeb")
            if t >= 2:
                em.wait(pe, "aA1", (t - 2, "expk"))
            last = None
            for kk in range(8):
                last = pe.matmul(pb[0 if i == 0 else 4][:],
                                 hT4[i4][:, kk * P:(kk + 1) * P],
                                 wkb[:, kk * HID:(kk + 1) * HID],
                                 start=(kk == 0), stop=(kk == 7))
            em.pe_ms(last, "pA1", (t, "mm_k"))
        elif tag == "a1_p_v":
            i4 = t % 4 if NT % 4 == 0 else t % 2
            if t == 1:
                em.wait(pe, "vRP", "boeb")
            if t >= 2:
                em.wait(pe, "vA1", (t - 2, "vbev"))
            last = None
            for kk in range(8):
                last = pe.matmul(pb[1 if i == 0 else 5][:],
                                 hT4[i4][:, kk * P:(kk + 1) * P],
                                 wvb[:, kk * HID:(kk + 1) * HID],
                                 start=(kk == 0), stop=(kk == 7))
            em.pe_ms(last, "pA1", (t, "mm_v"))
        elif tag == "a1_p_E":
            em.wait(pe, "aA1", (t, "expk"))
            em.wait(pe, "vA1", (t, "vbev"))
            if t >= 2:
                em.wait(pe, "vA1", (t - 2, "Eacc"))
            last = None
            for h in range(HEADS):
                pbase = (h % 2) * 64
                fo = (h // 2) * 65
                last = pe.matmul(pb[2 + i][pbase:pbase + 64, fo:fo + 65],
                                 ek[i][:, h * DH:(h + 1) * DH],
                                 vbq[i][:, h * 65:(h + 1) * 65],
                                 start=True, stop=True,
                                 tile_position=(0, pbase))
            em.pe_ms(last, "pA1", (t, "mm_E"))
        elif tag == "a2_p_q":
            bk, pv = it[2]
            em.wait(pe, f"dHL{t % 2}", t)
            em.wait(pe, pv[0], pv[1])
            last = None
            for kk in range(8):
                last = pe.matmul(pb[bk][:], hT[t % 2][:, kk * P:(kk + 1) * P],
                                 wqb[:, kk * HID:(kk + 1) * HID],
                                 start=(kk == 0), stop=(kk == 7))
            em.pe_ms(last, "pA2", (t, "mm_q"))
        elif tag == "a2_p_trqn":
            em.wait(pe, "vA2", (t, "qn"))
            if t < 2:
                em.wait(pe, "vA1", (NT - 2 + t, "hTev"))
            else:
                em.wait(pe, "vA2", (t - 2, "gTev"))
            pT = pbT if t % 2 == 0 else pbT2
            last = None
            for jx in range(4):
                last = pe.transpose(pT[:, jx * P:(jx + 1) * P],
                                    qn[i][:, jx * P:(jx + 1) * P], eye[:])
            em.pe_ms(last, "pA2", (t, "trqn"))
        elif tag == "a2_p_o2":
            bk, pv = it[2]
            em.wait(pe, "vA2", (t, "qTev"))
            if t == 0:
                em.wait(pe, "vMID", "ctx")
            em.wait(pe, pv[0], pv[1])
            last = None
            for h in range(HEADS):
                pbase = (h % 2) * 64
                last = pe.matmul(pb[bk][pbase:pbase + 64, (h // 2) * P:(h // 2 + 1) * P],
                                 ctx[pbase:pbase + 64, (h // 2) * DH:(h // 2 + 1) * DH],
                                 qTt[i][pbase:pbase + 64, (h // 2) * P:(h // 2 + 1) * P],
                                 start=True, stop=True,
                                 tile_position=(pbase, pbase))
            em.pe_ms(last, "pA2", (t, "o2"))
        elif tag == "a2_p_wo":
            b0, pv0 = it[2]
            b1, pv1 = it[3]
            if t == 0:
                em.wait(pe, "dSW", "bs")
            em.wait(pe, "vA2", (t, "o2ev"))
            em.wait(pe, pv0[0], pv0[1])
            em.wait(pe, pv1[0], pv1[1])
            last = None
            for oh, bk in ((0, b0), (1, b1)):
                for jx in range(4):
                    last = pe.matmul(pb[bk][:], o2b[i][:, jx * P:(jx + 1) * P],
                                     wob[:, jx * D + oh * HID:jx * D + (oh + 1) * HID],
                                     start=(jx == 0), stop=(jx == 3))
                if oh == 0:
                    em.pe_ms(last, "pA2", (t, "wo0"))
            em.pe_ms(last, "pA2", (t, "wo"))
        elif tag == "a2_p_trub":
            em.wait(pe, "vA2", (t, "ub"))
            em.wait(pe, "vA2", (t, "qTev"))
            pT = pbT if t % 2 == 0 else pbT2
            last = None
            for kk in range(8):
                last = pe.transpose(pT[:, kk * P:(kk + 1) * P],
                                    ub[i][:, kk * P:(kk + 1) * P], eye[:])
            em.pe_ms(last, "pA2", (t, "trub"))
        elif tag == "a2_p_sc":
            bk, pv = it[2]
            em.wait(pe, "vA2", (t, "uTev"))
            em.wait(pe, pv[0], pv[1])
            last = None
            for kk in range(8):
                last = pe.matmul(pb[bk][:, 0:NE],
                                 uT[:, kk * NTOK + t * P:kk * NTOK + (t + 1) * P],
                                 ct_bf[:, kk * NE:(kk + 1) * NE],
                                 start=(kk == 0), stop=(kk == 7))
            em.pe_ms(last, "pA2", (t, "sc"))
        elif tag == "a2_p_trg":
            em.wait(pe, "vA2", (t, "g"))
            em.wait(pe, "vA2", (t, "uTev"))
            pT = pbT if t % 2 == 0 else pbT2
            last = pe.transpose(pT[0:NE, 0:P], gb[:], eye[:])
            em.pe_ms(last, "pA2", (t, "trg"))
        elif tag == "ffnp":
            q, s, oh, j = it[1], it[2], it[3], it[4]
            bk, pv = it[5]
            tt = q * QT + j
            em.wait(pe, "vA2", (tt, "gTev"))
            em.wait(pe, pv[0], pv[1])
            if s == 0:
                last = pe.matmul(pb[bk][:], gTx[:, tt * P:(tt + 1) * P],
                                 brx[:, oh * HID:(oh + 1) * HID],
                                 start=True, stop=False)
                for kx in range(4):
                    kk = 4 * oh + kx
                    last = pe.matmul(
                        pb[bk][:],
                        uT[:, kk * NTOK + tt * P:kk * NTOK + (tt + 1) * P],
                        eyeblk[:, kx * HID:(kx + 1) * HID],
                        start=False, stop=(kx == 3))
            else:
                par = unit_idx(q, s, oh) % 2
                if j == 0:
                    em.wait(pe, f"dWT{par}", (q, s, oh, 7))
                last = None
                for kk in range(8):
                    last = pe.matmul(
                        pb[bk][:],
                        uT[:, kk * NTOK + tt * P:kk * NTOK + (tt + 1) * P],
                        wrt[par][:, kk * HID:(kk + 1) * HID],
                        start=(kk == 0), stop=(kk == 7))
            em.pe_ms(last, "pF", (q, s, oh, j))
        elif tag == "setup_p":
            em.wait(pe, "dSU", "eb")
            em.wait(pe, "vST", "memset")
            i1 = pe.matmul(pb[5][:], onesf[0:1, :], bo_row[:, 0:512],
                           start=True, stop=True)
            em.pe_ms(i1, "pRP", "bo0")
            i2 = pe.matmul(pb[4][:], onesf[0:1, :], bo_row[:, 512:1024],
                           start=True, stop=True)
            em.pe_ms(i2, "pRP", "bo1")
            em.wait(pe, "vRP", "bo0ev")
            i3 = pe.matmul(pb[5][:, 0:NE], onesf[0:1, :], eb_row[:],
                           start=True, stop=True)
            em.pe_ms(i3, "pRP", "eb")
            em.wait(pe, "dSW", "cen")
            last = None
            for kk in range(8):
                last = pe.transpose(pbT[:, kk * NE:(kk + 1) * NE],
                                    cen_st[0:NE, kk * P:(kk + 1) * P],
                                    eye[0:NE, 0:NE])
            em.pe_ms(last, "pRP", "cent")

    def emit_sync(em, sync):
        em.dma(sync, "dSU", "eye", eye[:], eye_in[:])
        em.dma(sync, "dSU", "bo", bo_row[:], bo[:])
        em.dma(sync, "dSU", "eb", eb_row[:], ebi[:])
        xdepth = 4 if NT % 4 == 0 else 2
        for t in range(NT):
            if t >= xdepth:
                em.wait(sync, "vA1", (t - xdepth, "hb"))
            em.dma(sync, f"dX{t % xdepth}", t, xA4[t % xdepth][:, 0:HID],
                   xin[t * P:(t + 1) * P, 0:HID])
            em.dma(sync, f"dXh{t % xdepth}", t, xA4[t % xdepth][:, HID:D],
                   xin[t * P:(t + 1) * P, HID:D])
            if t >= 4:
                em.wait(sync, "vA1", (t - 4, "hTev"))
                em.dma(sync, f"dHS{(t - 4) % 4}", t - 4,
                       hscr[(t - 4) * P:(t - 3) * P, :], hT4[(t - 4) % 4][:])
        for t in range(max(0, NT - 4), NT):
            em.wait(sync, "vA1", (t, "hTev"))
            em.dma(sync, f"dHS{t % 4}", t,
                   hscr[t * P:(t + 1) * P, :], hT4[t % 4][:])
        em.wait(sync, "vA1", (NT - 1, "hb"))
        em.dma(sync, "dX20", 0, xA[0][:, 0:HID], xin[0:P, 0:HID])
        em.dma(sync, "dX2h0", 0, xA[0][:, HID:D], xin[0:P, HID:D])
        em.dma(sync, "dX21", 1, xA[1][:, 0:HID], xin[P:2 * P, 0:HID])
        em.dma(sync, "dX2h1", 1, xA[1][:, HID:D], xin[P:2 * P, HID:D])
        for c in range(min(2, NT)):
            lastu = ((NT - 1 - c) // 4) * 4 + c
            em.wait(sync, "pA1", (lastu, "mm_v"))
            em.wait(sync, f"dHS{c}", lastu)
            em.dma(sync, f"dHL{c}", c, hT[c][:], hscr[c * P:(c + 1) * P, :])
        for it in flow_tail:
            h_sync(em, sync, it)
        em.wait(sync, "dOUT0", (NQ - 1, QT - 2))
        em.wait(sync, "dOUT1", (NQ - 1, QT - 1))

    def emit_gpsimd(em, gp):
        em.dma(gp, "dSW", "wq", wqb[:].rearrange("p (k c) -> p k c", k=8),
               wq.rearrange("(k p) c -> p k c", p=P))
        em.dma(gp, "dSW", "wk", wkb[:].rearrange("p (k c) -> p k c", k=8),
               wk.rearrange("(k p) c -> p k c", p=P))
        em.dma(gp, "dSW", "wv", wvb[:].rearrange("p (k c) -> p k c", k=8),
               wv.rearrange("(k p) c -> p k c", p=P))
        em.dma(gp, "dSW", "cen", cen_st[:], cen[:])
        em.wait(gp, "vA1", (min(4, NT - 1), "hb"))
        em.dma(gp, "dSW", "wo", wob[:].rearrange("p (k c) -> p k c", k=4),
               wo.rearrange("(k p) c -> p k c", p=P))
        em.dma(gp, "dSW", "br", brx[0:NE, :], brr[:])
        em.dma(gp, "dSW", "bs", brx[NE:NMAT, :], bss[:])
        for m in range(NMAT):
            src = wr[m] if m < NE else ws[m - NE]
            if m >= 1:
                em.wait(gp, "vA1", (min(2 * m + 3, NT - 1), "hb"))
            em.dma(gp, "dWC", m, wbf[m], src)
        if DEBUG:
            em.wait(gp, "dOUT0", (NQ - 1, QT - 2))
            em.wait(gp, "dOUT1", (NQ - 1, QT - 1))
            em.dma(gp, "dDBG", 0, dbg_E[:], E_sb[:])
            em.dma(gp, "dDBG", 1, dbg_ctx[:], ctx[:])
            em.dma(gp, "dDBG", 2, dbg_g[:], gAll[:])
            em.dma(gp, "dDBG", 3, dbg_y[:], yscr[:])
            em.dma(gp, "dDBG", 4, dbg_uT[:], uT[:])
            em.dma(gp, "dDBG", 6, dbg_racc[:], racc[:])
            em.dma(gp, "dDBG", 7, dbg_gTx[:], gTx[:])
            em.wait(gp, "dDBG", 7)

    def emit_vector(em, v):
        h_vector(em, v, ("setup_v",))
        for it in flow_a1:
            h_vector(em, v, it)
        h_vector(em, v, ("mid",))
        for it in flow_tail:
            h_vector(em, v, it)

    def emit_scalar(em, sc):
        for it in flow_a1:
            h_scalar(em, sc, it)
        for it in flow_tail:
            h_scalar(em, sc, it)

    def emit_tensor(em, pe):
        h_tensor(em, pe, ("setup_p",))
        for it in flow_a1:
            h_tensor(em, pe, it)
        for it in flow_tail:
            h_tensor(em, pe, it)

    em = Em()
    dummy = _Dummy()
    for f in (emit_sync, emit_gpsimd, emit_tensor, emit_vector, emit_scalar):
        f(em, dummy)
    em.mode = "emit"
    em.cnt = {}
    for name in em.vals:
        em.sem[name] = nc.alloc_semaphore(name)
    with nc.Block() as block:
        @block.sync
        def _(sync):
            emit_sync(em, sync)

        @block.gpsimd
        def _(gp):
            emit_gpsimd(em, gp)

        @block.tensor
        def _(pe):
            emit_tensor(em, pe)

        @block.vector
        def _(v):
            emit_vector(em, v)

        @block.scalar
        def _(sc):
            emit_scalar(em, sc)

    return nc


def kernel(L0_attn_x, L0_attn_Wq, L0_attn_Wk, L0_attn_Wv, L0_attn_Wo,
           L0_attn_bo, L0_ffn_centroids, L0_ffn_expert_biases, L0_ffn_Wr,
           L0_ffn_br, L0_ffn_Ws, L0_ffn_bs):
    x = np.ascontiguousarray(np.asarray(L0_attn_x, dtype=np.float32))
    com = {
        "wq": np.ascontiguousarray(np.asarray(L0_attn_Wq, np.float32)),
        "wk": np.ascontiguousarray(np.asarray(L0_attn_Wk, np.float32)),
        "wv": np.ascontiguousarray(np.asarray(L0_attn_Wv, np.float32)),
        "wo": np.ascontiguousarray(np.asarray(L0_attn_Wo, np.float32)),
        "bo": np.ascontiguousarray(np.asarray(L0_attn_bo, np.float32).reshape(1, D)),
        "cen": np.ascontiguousarray(np.asarray(L0_ffn_centroids, np.float32)),
        "ebi": np.ascontiguousarray(np.asarray(L0_ffn_expert_biases, np.float32).reshape(1, NE)),
        "wr": np.ascontiguousarray(np.asarray(L0_ffn_Wr, np.float32)),
        "brr": np.ascontiguousarray(np.asarray(L0_ffn_br, np.float32)),
        "ws": np.ascontiguousarray(np.asarray(L0_ffn_Ws, np.float32)),
        "bss": np.ascontiguousarray(np.asarray(L0_ffn_bs, np.float32)),
        "eye_in": np.eye(P, dtype=ml_dtypes.bfloat16),
    }
    nc = build_nc()
    in_maps = [dict(com, xin=np.ascontiguousarray(x[b])) for b in range(N_CORES)]
    res = run_bass_kernel_spmd(nc, in_maps, core_ids=list(range(N_CORES)))
    return np.stack([r["out"] for r in res.results], axis=0)
